# version 10
# baseline (speedup 1.0000x reference)
import contextlib
import os
import numpy as np
import concourse.bass as bass
import concourse.mybir as mybir
from concourse.bass_utils import run_bass_kernel_spmd

F32 = mybir.dt.float32
BF16 = mybir.dt.bfloat16
AX = mybir.AxisListType
OP = mybir.AluOpType
ACT = mybir.ActivationFunctionType

T = 2048
D = 1024
NCH = 16
H = 16
HD = 64
NEXP = 16
HEAVY_I = 4096
NEG = -1e30

PHASES = ("l0a", "l0f", "l1a", "l1f")


def _split2(v):
    hi = np.float32(v)
    lo = np.float32(np.float64(v) - np.float64(hi))
    return float(hi), float(lo)


ALIBI_SLOPES = [_split2(2.0 ** (-8.0 * np.float64(h) / 15.0)) for h in range(H)]


def host_consts():
    c = {"c_ident": np.eye(128, dtype=np.float32)}
    s_ = np.arange(128)[:, None]
    t_ = np.arange(128)[None, :]
    c["c_UT"] = (s_ <= t_).astype(np.float32)
    c["c_ones"] = np.ones((128, 128), np.float32)
    s = np.arange(128)[:, None]
    t = np.arange(128)[None, :]
    rel0 = 128.0 + s - t
    rel1 = s - t
    val0 = s < t
    val1 = s >= t
    R = np.zeros((128, 2, 128), np.float64)
    Mneg = np.zeros((128, 2, 128), np.float64)
    R[:, 0, :] = np.where(val0, -rel0, 0.0)
    R[:, 1, :] = np.where(val1, -rel1, 0.0)
    Mneg[:, 0, :] = np.where(val0, 0.0, NEG)
    Mneg[:, 1, :] = np.where(val1, 0.0, NEG)
    c["c_R"] = R.astype(np.float32)
    c["c_Mneg"] = Mneg.astype(np.float32)
    return c


class Sem:
    def __init__(self, nc, ctx, name, unit=1):
        self.h = ctx.enter_context(nc.semaphore(name))
        self.unit = unit
        self.n = 0

    def sig(self, inst):
        self.n += self.unit
        inst.then_inc(self.h, self.unit)
        return (self, self.n)

    def tok(self):
        return (self, self.n)

    def throttle(self, eng, max_out=1):
        eng.wait_ge(self.h, self.n)


def wt(eng, *toks):
    best = {}
    for tk in toks:
        if tk is None:
            continue
        s, v = tk
        if v > 0 and v > best.get(id(s), (s, 0))[1]:
            best[id(s)] = (s, v)
    for s, v in best.values():
        eng.wait_ge(s.h, v)


def tr_slot(k):
    return (k % 2) * 512 + ((k // 2) % 2) * 128


def build_program(phases=PHASES):
    nc = bass.Bass("TRN2", target_bir_lowering=False)

    def din(name, shape):
        return nc.dram_tensor(name, list(shape), F32, kind="ExternalInput").ap()

    x_hbm = din("x", (T, D))
    xT_hbm = din("xTin", (D, T))
    l0w_hbm = {w: din(f"l0_w{w}T", (8, 128, D)) for w in "qkvo"}
    gate_hbm = din("l0_gateT", (D, NEXP))
    l1wqkv_hbm = din("l1_wqkv", (8, 128, 3 * D))
    l1wout_hbm = din("l1_wout", (8, 128, D))
    ut_hbm = din("c_UT", (128, 128))
    ones_hbm = din("c_ones", (128, 128))
    rrep_hbm = din("l1_rrep", (128, D))
    lgam_hbm = din("l1_lgam", (128, 8))
    hgam_hbm = din("l1_hgam", (128, 8))
    lw1_hbm = din("l1_lw1", (8, 128, 512))
    lw2_hbm = din("l1_lw2", (4, 128, D))
    lb1_hbm = din("l1_lb1", (1, 512))
    lb2_hbm = din("l1_lb2", (1, D))
    hw1_hbm = din("l1_hw1", (D, HEAVY_I))
    hw2_hbm = din("l1_hw2", (HEAVY_I, D))
    hb1_hbm = din("l1_hb1", (1, HEAVY_I))
    hb2_hbm = din("l1_hb2", (1, D))
    moe_w1_hbm = din("l0_moe_w1", (NEXP, D, 512))
    moe_w3_hbm = din("l0_moe_w3", (NEXP, D, 512))
    moe_w2_hbm = din("l0_moe_w2", (NEXP, 512, D))
    ident_h = din("c_ident", (128, 128))
    R_h = din("c_R", (128, 2, 128))
    Mneg_h = din("c_Mneg", (128, 2, 128))
    y_hbm = nc.dram_tensor("y", [T, D], F32, kind="ExternalOutput").ap()
    DBG = os.environ.get("KDBG")
    if DBG:
        dbg_hbm = {nm: nc.dram_tensor(f"dbg_{nm}", sh, dt, kind="ExternalOutput").ap()
                   for nm, sh, dt in [("qn", [128, D], F32),
                                      ("PT", [128, 8 * 2 * 128], BF16),
                                      ("kT", [128, 8 * 128], BF16),
                                      ("qT", [128, 16 * 128], BF16),
                                      ("nrm", [128, 8 * H], F32),
                                      ("vbf", [128, H * 65], BF16)]}

    ctx = contextlib.ExitStack()
    with ctx:
        E = ctx.enter_context
        X = E(nc.sbuf_tensor("X", [128, NCH, D], F32))
        XNT = E(nc.sbuf_tensor("XNT", [128, 8, T], BF16))
        ident = E(nc.sbuf_tensor("ident", [128, 128], F32))
        identb = E(nc.sbuf_tensor("identb", [128, 128], BF16))
        cst_sm = E(nc.sbuf_tensor("cst_sm", [128, 2], F32))
        lnm = E(nc.sbuf_tensor("lnm", [128, 8], F32))

        svals = E(nc.sbuf_tensor("svals", [128, 32], F32))
        rms_sv_tok = [None] * NCH
        dX = Sem(nc, ctx, "dX", 16)
        dW = Sem(nc, ctx, "dW", 16)
        dxin = [Sem(nc, ctx, f"dxin{c}", 16) for c in range(NCH)]
        dC = Sem(nc, ctx, "dC", 16)
        dSl = [Sem(nc, ctx, f"dSl{i}", 16) for i in range(2)]
        dG = Sem(nc, ctx, "dG", 16)
        sT = Sem(nc, ctx, "sT")
        sV = Sem(nc, ctx, "sV")
        sS = Sem(nc, ctx, "sS")
        sG = Sem(nc, ctx, "sG")
        block = E(nc.Block())

        def load_set6(s):
            s.add_instruction(mybir.InstLoadActFuncSet(
                name=nc.get_next_instruction_name(), act_func_set_id=6,
                ins=[], outs=[]))

        def emit(section, fn):
            out = []

            def wrap(eng):
                out.append(fn(eng))
            section(wrap)
            return out[0]

        def emit_transposes(srcs, ptile, dsts, pre_toks=(), evac="vector",
                            qtz_split=None):
            n = len(srcs)
            ttoks = [None] * n
            sem = sS if evac == "scalar" else sV
            base = sem.n

            def cum(k):
                if qtz_split is None:
                    return base + k + 1
                return base + (2 * (k + 1) if k < 8 else 16 + (k - 8 + 1))
            etoks = [(sem, cum(k)) for k in range(n)]

            def tpart(t):
                for k in range(n):
                    if k == 0:
                        wt(t, *pre_toks)
                    if k >= 2:
                        wt(t, etoks[k - 2])
                    i = t.transpose(ptile[:, tr_slot(k):tr_slot(k) + 128],
                                    srcs[k], ident[:, :])
                    ttoks[k] = sT.sig(i)
            block.tensor(tpart)

            def epart(e):
                for k in range(n):
                    wt(e, ttoks[k])
                    src = ptile[:, tr_slot(k):tr_slot(k) + 128]
                    if qtz_split is not None and k < 8:
                        sem.sig(e.tensor_copy(qtz_split[0:64, 2 * k, :],
                                              src[0:64, :]))
                        sem.sig(e.tensor_copy(qtz_split[64:128, 2 * k + 1, :],
                                              src[64:128, :]))
                    elif evac == "scalar":
                        sem.sig(e.copy(dsts[k], src))
                    else:
                        sem.sig(e.tensor_copy(dsts[k], src))
                assert sem.n == cum(n - 1), (sem.n, base, n)
            (block.scalar if evac == "scalar" else block.vector)(epart)
            return etoks

        xin_tok = [None] * NCH

        def _(sy):
            for c in range(NCH):
                i = sy.dma_start(out=X[:, c, :], in_=x_hbm[c * 128:(c + 1) * 128, :])
                xin_tok[c] = dxin[c].sig(i)
        block.sync(_)

        qn = E(nc.sbuf_tensor("qn", [128, D], F32))
        kn = E(nc.sbuf_tensor("kn", [128, D], F32))
        nrm = E(nc.sbuf_tensor("nrm", [128, 8, H], F32))
        pA = E(nc.psum_tensor("pA", [128, 1024], F32))
        pB = E(nc.psum_tensor("pB", [128, 1024], F32))
        pC = E(nc.psum_tensor("pC", [128, 1024], F32))
        pD = E(nc.psum_tensor("pD", [128, 1024], F32))
        l0actx = contextlib.ExitStack()
        EA = l0actx.enter_context
        l0w = {w: EA(nc.sbuf_tensor(f"l0w{w}", [128, 8, D], BF16)) for w in "qkvo"}
        biasT = EA(nc.sbuf_tensor("biasT", [128, H, 2, 128], F32))
        rctx = contextlib.ExitStack()
        Rt = rctx.enter_context(nc.sbuf_tensor("Rt", [128, 2, 128], F32))
        Mnegt = rctx.enter_context(nc.sbuf_tensor("Mnegt", [128, 2, 128], F32))


        def _(sy):
            dC.sig(sy.dma_start(out=ident[:, :], in_=ident_h))
            dC.sig(sy.dma_start(out=Rt[:, :, :], in_=R_h))
            return dC.sig(sy.dma_start(out=Mnegt[:, :, :], in_=Mneg_h))
        cst_tok = emit(block.sync, _)
        cst_tok = (dC, 48)

        wcast_tok = {}

        def _(g):
            for w in "qkvo":
                for q in range(8):
                    dG.sig(g.dma_start(out=l0w[w][:, q, :], in_=l0w_hbm[w][q]))
                    dG.throttle(g)
                wcast_tok[w] = dG.tok()
            return None
        emit(block.gpsimd, _)

        def _(v):
            wt(v, cst_tok)
            for h in range(H):
                hi, _lo = ALIBI_SLOPES[h]
                i = v.scalar_tensor_tensor(biasT[:, h, :, :], Rt[:, :, :], hi,
                                           Mnegt[:, :, :], op0=OP.mult, op1=OP.add)
            wt(v, sV.sig(i))
            for h in range(H):
                _hi, lo = ALIBI_SLOPES[h]
                i = v.scalar_tensor_tensor(biasT[:, h, :, :], Rt[:, :, :], lo,
                                           biasT[:, h, :, :], op0=OP.mult,
                                           op1=OP.add)
            return sV.sig(i)
        bias_tok = emit(block.vector, _)
        rctx.close()

        def _(v):
            v.memset(cst_sm[:, 1:2], float(np.float32(np.log(np.float64(32.0)))))
            wt(v, cst_tok)
            v.tensor_copy(identb[:, :], ident[:, :])
            return sV.sig(v.memset(cst_sm[:, 0:1], 1e-5))
        eps_tok = emit(block.vector, _)

        xT = EA(nc.sbuf_tensor("xT", [128, 8, 128], BF16))
        vbf = [EA(nc.sbuf_tensor(f"vbf{i}", [128, H, 65], BF16)) for i in range(2)]
        kT = [EA(nc.sbuf_tensor(f"kT{i}", [128, 8, 128], BF16)) for i in range(2)]
        qT = [EA(nc.sbuf_tensor(f"qT{i}", [128, 16, 128], BF16)) for i in range(2)]
        PT = EA(nc.sbuf_tensor("PT", [128, 8, 2, 128], BF16))

        def _(v):
            wt(v, bias_tok)
            v.memset(qT[0][:, :, :], 0.0)
            v.memset(qT[1][:, :, :], 0.0)
            v.memset(vbf[0][:, :, 64:65], 1.0)
            return sV.sig(v.memset(vbf[1][:, :, 64:65], 1.0))
        ones_tok = emit(block.vector, _)

        tok_tqkv = [None] * NCH
        tok_qkev = [None] * NCH
        tok_tnrm = [None] * NCH
        tok_tvev = [None] * NCH
        tok_tsm = [[None, None] for _ in range(NCH)]
        tok_exp = [[None, None] for _ in range(NCH)]
        tok_pvT = [[None, None] for _ in range(NCH)]
        tok_pve = [None] * NCH
        tok_aoev = [None] * NCH
        tok_x1ev = [None] * NCH
        tok_tres = [None] * NCH
        tok_tln = [None] * NCH

        def l0a_xload(c):
            def _(g, c=c):
                wt(g, tok_tqkv[c - 1] if c else bias_tok)
                for k in range(8):
                    dG.sig(g.dma_start(out=xT[:, k, :],
                                       in_=xT_hbm[k * 128:(k + 1) * 128,
                                                  c * 128:(c + 1) * 128]))
                    dG.throttle(g)
                return dG.tok()
            return emit(block.gpsimd, _)

        def l0a_qkv(c, xld):
            def _(t, c=c, xld=xld):
                wt(t, wcast_tok["q"], wcast_tok["k"], wcast_tok["v"], xld)
                if c >= 1:
                    for tk in tok_qkev[c - 1]:
                        wt(t, tk)
                if c >= 2:
                    wt(t, tok_tsm[c - 2][0], tok_tsm[c - 2][1])
                for k in range(8):
                    stt, sp = (k == 0), (k == 7)
                    t.matmul(pA[:, 0:512], xT[:, k, :], l0w["q"][:, k, 0:512], start=stt, stop=sp)
                    t.matmul(pA[:, 512:1024], xT[:, k, :], l0w["q"][:, k, 512:1024], start=stt, stop=sp)
                for k in range(8):
                    stt, sp = (k == 0), (k == 7)
                    t.matmul(pB[:, 0:512], xT[:, k, :], l0w["k"][:, k, 0:512], start=stt, stop=sp)
                    t.matmul(pB[:, 512:1024], xT[:, k, :], l0w["k"][:, k, 512:1024], start=stt, stop=sp)
                if c >= 2:
                    wt(t, tok_pve[c - 2])
                for k in range(8):
                    stt, sp = (k == 0), (k == 7)
                    t.matmul(pC[:, 0:512], xT[:, k, :], l0w["v"][:, k, 0:512], start=stt, stop=sp)
                    i = t.matmul(pC[:, 512:1024], xT[:, k, :], l0w["v"][:, k, 512:1024], start=stt, stop=sp)
                return sT.sig(i)
            tok_tqkv[c] = emit(block.tensor, _)

        def l0a_norm(c):
            cur = c % 2

            def _(s, c=c):
                wt(s, tok_tqkv[c], ones_tok)
                if c >= 2:
                    wt(s, tok_aoev[c - 2][-1], tok_exp[c - 2][1])
                s.copy(vbf[cur][:, :, 0:64],
                       pC[:, :].rearrange("p (h d) -> p h d", h=H))
                s.activation(qn[:, :], pA[:, :], ACT.Square)
                return sS.sig(s.activation(kn[:, :], pB[:, :], ACT.Square))
            tok_tvev[c] = tsq = emit(block.scalar, _)

            def _(v, tsq=tsq):
                wt(v, tsq)
                v.tensor_reduce(nrm[:, 0, :], qn[:, :].rearrange("p (h d) -> p h d", h=H),
                                axis=AX.X, op=OP.add)
                i = v.tensor_reduce(nrm[:, 1, :], kn[:, :].rearrange("p (h d) -> p h d", h=H),
                                    axis=AX.X, op=OP.add)
                return sV.sig(i)
            tred = emit(block.vector, _)

            def _(s, tred=tred):
                wt(s, tred)
                s.activation(nrm[:, 2, :], nrm[:, 0, :], ACT.Ln)
                i = s.activation(nrm[:, 3, :], nrm[:, 1, :], ACT.Ln)
                wt(s, sS.sig(i))
                s.activation(nrm[:, 4, :], nrm[:, 2, :], ACT.Exp, scale=-0.5)
                return sS.sig(s.activation(nrm[:, 5, :], nrm[:, 3, :], ACT.Exp,
                                           scale=-0.5))
            tsqrt = emit(block.scalar, _)

            def _(v, c=c, tsqrt=tsqrt):
                wt(v, tsqrt)
                if c >= 2:
                    wt(v, tok_exp[c - 2][1])
                v.tensor_scalar(nrm[:, 4, :], nrm[:, 4, :], 1e6, None, op0=OP.min)
                i = v.tensor_scalar(nrm[:, 5, :], nrm[:, 5, :], 1e6, None, op0=OP.min)
                wt(v, sV.sig(i))
                v.tensor_tensor(qn[:, :].rearrange("p (h d) -> p h d", h=H),
                                pA[:, :].rearrange("p (h d) -> p h d", h=H),
                                nrm[:, 4, :].unsqueeze(2).broadcast_to([128, H, HD]),
                                op=OP.mult)
                i = v.tensor_tensor(kn[:, :].rearrange("p (h d) -> p h d", h=H),
                                    pB[:, :].rearrange("p (h d) -> p h d", h=H),
                                    nrm[:, 5, :].unsqueeze(2).broadcast_to([128, H, HD]),
                                    op=OP.mult)
                return sV.sig(i)
            tok_tnrm[c] = emit(block.vector, _)

        def l0a_qktr(c):
            cur = c % 2
            qksrc = ([qn[:, k * 128:(k + 1) * 128] for k in range(8)] +
                     [kn[:, k * 128:(k + 1) * 128] for k in range(8)])
            qkdst = ([qT[cur][:, 2 * k, :] for k in range(8)] +
                     [kT[cur][:, k, :] for k in range(8)])
            tok_qkev[c] = emit_transposes(qksrc, pA, qkdst,
                                          pre_toks=(tok_tnrm[c],),
                                          qtz_split=qT[cur])

        def l0a_aoT(c):
            ttoks = [None] * 8
            base = sS.n
            etoks = [(sS, base + k + 1) for k in range(8)]

            def _(t, c=c):
                for k in range(8):
                    if k == 0:
                        wt(t, tok_pve[c], eps_tok)
                    if k >= 2:
                        wt(t, etoks[k - 2])
                    i = t.matmul(pD[:, tr_slot(k):tr_slot(k) + 128],
                                 PT[:, k, 1, :], identb[:, :],
                                 start=True, stop=True)
                    ttoks[k] = sT.sig(i)
            block.tensor(_)

            def _(s):
                for k in range(8):
                    wt(s, ttoks[k])
                    sS.sig(s.copy(PT[:, k, 0, :],
                                  pD[:, tr_slot(k):tr_slot(k) + 128]))
                assert sS.n == base + 8
            block.scalar(_)
            tok_aoev[c] = etoks

        def l0a_x1ev(c):
            x1src = [X[:, c, k * 128:(k + 1) * 128] for k in range(8)]
            x1dst = [XNT[:, k, c * 128:(c + 1) * 128] for k in range(8)]
            tok_x1ev[c] = emit_transposes(x1src, pD, x1dst,
                                          pre_toks=(tok_tln[c],
                                                    tok_aoev[c + 1][-1]))

        def l0a_wo_ln(c):
            def _(t, c=c):
                wt(t, wcast_tok["o"], tok_aoev[c][-1])
                if c >= 1 and tok_x1ev[c - 1] is not None:
                    wt(t, tok_x1ev[c - 1][-1])
                for k in range(8):
                    stt, sp = (k == 0), (k == 7)
                    t.matmul(pD[:, 0:512], PT[:, k, 0, :], l0w["o"][:, k, 0:512], start=stt, stop=sp)
                    i = t.matmul(pD[:, 512:1024], PT[:, k, 0, :], l0w["o"][:, k, 512:1024], start=stt, stop=sp)
                return sT.sig(i)
            twoT = emit(block.tensor, _)

            def _(v, c=c, twoT=twoT):
                wt(v, twoT, xin_tok[c])
                return sV.sig(v.tensor_add(X[:, c, :], X[:, c, :], pD[:, :]))
            tok_tres[c] = tres = emit(block.vector, _)

            def _(s, c=c, tres=tres):
                wt(s, tres, eps_tok)
                junk = XNT[:, :, c * 128:(c + 1) * 128]
                xin3 = X[:, c, :].rearrange("p (a t) -> p a t", a=8)
                i = s.activation(junk, xin3, ACT.Copy,
                                 accum_out=lnm[:, 0:1])
                wt(s, sS.sig(i))
                return sS.sig(s.activation(junk, xin3, ACT.Square,
                                           accum_out=lnm[:, 1:2]))
            tst = emit(block.scalar, _)

            def _(v, tst=tst):
                wt(v, tst)
                v.tensor_scalar(lnm[:, 2:3], lnm[:, 0:1], 1.0 / D, None, op0=OP.mult)
                i = v.tensor_scalar(lnm[:, 3:4], lnm[:, 1:2], 1.0 / D, None, op0=OP.mult)
                wt(v, sV.sig(i))
                i = v.tensor_tensor(lnm[:, 4:5], lnm[:, 2:3], lnm[:, 2:3], op=OP.mult)
                wt(v, sV.sig(i))
                return sV.sig(v.tensor_sub(lnm[:, 3:4], lnm[:, 3:4], lnm[:, 4:5]))
            tvar = emit(block.vector, _)

            def _(s, tvar=tvar):
                wt(s, tvar)
                i = s.activation(lnm[:, 4:5], lnm[:, 3:4], ACT.Ln,
                                 bias=cst_sm[:, 0:1])
                wt(s, sS.sig(i))
                return sS.sig(s.activation(lnm[:, 5:6], lnm[:, 4:5], ACT.Exp,
                                           scale=-0.5))
            tsd = emit(block.scalar, _)

            def _(v, c=c, tsd=tsd):
                wt(v, tsd)
                i = v.tensor_scalar(lnm[:, 6:7], lnm[:, 2:3], -1.0, None, op0=OP.mult)
                wt(v, sV.sig(i))
                return sV.sig(v.tensor_scalar(X[:, c, :], X[:, c, :],
                                              lnm[:, 6:7], lnm[:, 5:6],
                                              op0=OP.add, op1=OP.mult))
            tok_tln[c] = emit(block.vector, _)

        def buf(hh):
            b = kn if hh < 4 else qn
            return b[:, (hh % 4) * 256:(hh % 4) * 256 + 256]

        def l0a_scores(qc, g, part, last):
            qcur, qnxt = qc % 2, (qc + 1) % 2

            def ps_of(g, hh):
                if g == 0:
                    return (pB if hh < 4 else pC)
                return (pA if hh < 4 else pB)

            def _(t, qc=qc, g=g, part=part, last=last):
                kc = qc if last else qc + 1
                for tk in tok_qkev[kc]:
                    wt(t, tk)
                wt(t, tok_tnrm[kc], tok_tvev[kc])
                if g == 1 and part == "b":
                    wt(t, tok_tsm[qc][0])
                rng = range(4) if g == 0 or part == "a" else range(4, 8)
                if g == 0:
                    rng = range(8)
                i = None
                for hh in rng:
                    h = g * 8 + hh
                    tl = h // 2
                    ps = ps_of(g, hh)
                    base = (hh % 4) * 256
                    i = t.matmul(ps[:, base + 128:base + 256],
                                 kT[qcur][:, tl, :],
                                 qT[qcur][:, h, :], start=True, stop=True)
                    if not last:
                        i = t.matmul(ps[:, base:base + 128],
                                     kT[qnxt][:, tl, :],
                                     qT[qcur][:, h, :], start=True, stop=True)
                return sT.sig(i)
            return emit(block.tensor, _)

        def l0a_bias(qc, g, sc_toks, last):
            def _(v, qc=qc, g=g, last=last):
                wt(v, bias_tok, *sc_toks)
                if g == 1:
                    wt(v, tok_exp[qc][0])
                elif qc >= 1:
                    wt(v, tok_exp[qc - 1][1])
                i = None
                if last:
                    for hh in range(8):
                        v.memset(buf(hh)[:, 0:128], NEG)
                    for hh in range(8):
                        h = g * 8 + hh
                        ps = (pB if hh < 4 else pC) if g == 0 else (pA if hh < 4 else pB)
                        base = (hh % 4) * 256
                        i = v.tensor_tensor(
                            buf(hh)[:, 128:256],
                            ps[:, base + 128:base + 256],
                            biasT[:, h, 1, :],
                            op=OP.add)
                else:
                    for hh in range(8):
                        h = g * 8 + hh
                        ps = (pB if hh < 4 else pC) if g == 0 else (pA if hh < 4 else pB)
                        base = (hh % 4) * 256
                        i = v.tensor_tensor(
                            buf(hh),
                            ps[:, base:base + 256],
                            biasT[:, h, :, :].rearrange("p r t -> p (r t)"),
                            op=OP.add)
                return sV.sig(i)
            tok_tsm[qc][g] = emit(block.vector, _)

        def l0a_exp(qc, g):
            def _(s, qc=qc, g=g):
                wt(s, tok_tsm[qc][g])
                if g == 1:
                    wt(s, tok_pvT[qc][0])
                s.activation(
                    PT[:, 0:4, :, :].rearrange("p h r t -> p (h r t)"),
                    kn[:, :], ACT.Exp)
                return sS.sig(s.activation(
                    PT[:, 4:8, :, :].rearrange("p h r t -> p (h r t)"),
                    qn[:, :], ACT.Exp))
            tok_exp[qc][g] = emit(block.scalar, _)

        def l0a_pv(qc, g, last):
            qcur, qnxt = qc % 2, (qc + 1) % 2

            def _(t, qc=qc, g=g, last=last):
                wt(t, tok_exp[qc][g])
                if g == 0:
                    if qc >= 2 and tok_x1ev[qc - 2] is not None:
                        wt(t, tok_x1ev[qc - 2][-1])
                    wt(t, tok_tres[qc - 1] if qc >= 1 else None)
                else:
                    wt(t, tok_tsm[qc][0])
                i = None
                for hh in range(8):
                    h = g * 8 + hh
                    ps = pD if g == 0 else pC
                    col = (hh // 4) * 512 + (hh % 4) * 66
                    i = t.matmul(ps[:, col:col + 65], PT[:, hh, 1, :],
                                 vbf[qcur][:, h, :], start=True, stop=last)
                    if not last:
                        i = t.matmul(ps[:, col:col + 65], PT[:, hh, 0, :],
                                     vbf[qnxt][:, h, :], start=False, stop=True)
                return sT.sig(i)
            tok_pvT[qc][g] = emit(block.tensor, _)

        def l0a_pvevac(qc):
            def _(v, qc=qc):
                wt(v, tok_pvT[qc][0], tok_pvT[qc][1])
                i = None
                for g4 in range(4):
                    ps = pD if g4 < 2 else pC
                    col = (g4 % 2) * 512
                    i = v.tensor_copy(nrm[:, 6, g4 * 4:(g4 + 1) * 4],
                                      ps[:, col:col + 264].rearrange(
                                          "p (h e) -> p h e", h=4)[:, :, 64])
                wt(v, sV.sig(i))
                i = v.reciprocal(nrm[:, 7, :], nrm[:, 6, :])
                wt(v, sV.sig(i))
                i = None
                for tile in range(8):
                    ps = pD if tile < 4 else pC
                    col = (tile // 4 % 2) * 0 + ((tile % 4) // 2) * 512 + (tile % 2) * 132
                    i = v.tensor_tensor(
                        PT[:, tile, 1, :].rearrange("p (h d) -> p h d", h=2),
                        ps[:, col:col + 132].rearrange("p (h e) -> p h e", h=2)[:, :, 0:64],
                        nrm[:, 7, tile * 2:tile * 2 + 2].unsqueeze(2).broadcast_to([128, 2, HD]),
                        op=OP.mult)
                return sV.sig(i)
            tok_pve[qc] = emit(block.vector, _)

        def l0a_att_head(qc, last):
            t0 = l0a_scores(qc, 0, None, last)
            l0a_bias(qc, 0, (t0,), last)
            l0a_exp(qc, 0)
            t1a = l0a_scores(qc, 1, "a", last)
            l0a_pv(qc, 0, last)
            t1b = l0a_scores(qc, 1, "b", last)
            l0a_bias(qc, 1, (t1a, t1b), last)
            l0a_exp(qc, 1)
            l0a_pv(qc, 1, last)
            l0a_pvevac(qc)

        if "l0a" in phases:
            for i in range(NCH):
                xld = l0a_xload(i)
                l0a_qkv(i, xld)
                if i >= 2:
                    l0a_aoT(i - 2)
                l0a_norm(i)
                if i >= 3:
                    l0a_x1ev(i - 3)
                if i >= 2:
                    l0a_wo_ln(i - 2)
                l0a_qktr(i)
                if i >= 1:
                    l0a_att_head(i - 1, last=False)
            l0a_aoT(NCH - 2)
            if NCH >= 3:
                l0a_x1ev(NCH - 3)
            l0a_wo_ln(NCH - 2)
            l0a_att_head(NCH - 1, last=True)
            l0a_aoT(NCH - 1)
            l0a_x1ev(NCH - 2)
            l0a_wo_ln(NCH - 1)
            x1src = [X[:, NCH - 1, k * 128:(k + 1) * 128] for k in range(8)]
            x1dst = [XNT[:, k, (NCH - 1) * 128:NCH * 128] for k in range(8)]
            tok_x1ev[NCH - 1] = emit_transposes(
                x1src, pD, x1dst,
                pre_toks=(tok_tln[NCH - 1], tok_tres[NCH - 1]))
        l0actx.close()
        bar_l0a = [(sT, sT.n), (sV, sV.n), (sS, sS.n)]

        def emit_ln_chunk(c, pre_toks, ptile, skip_t=False):
            def _(s):
                wt(s, *pre_toks)
                s.activation(kn[:, :], X[:, c, :], ACT.Copy,
                             accum_out=lnm[:, 0:1])
                return sS.sig(s.activation(qn[:, :], X[:, c, :], ACT.Square,
                                           accum_out=lnm[:, 1:2]))
            tst = emit(block.scalar, _)

            def _(v):
                wt(v, tst)
                v.tensor_scalar(lnm[:, 2:3], lnm[:, 0:1], 1.0 / D, None, op0=OP.mult)
                i = v.tensor_scalar(lnm[:, 3:4], lnm[:, 1:2], 1.0 / D, None, op0=OP.mult)
                wt(v, sV.sig(i))
                i = v.tensor_tensor(lnm[:, 4:5], lnm[:, 2:3], lnm[:, 2:3], op=OP.mult)
                wt(v, sV.sig(i))
                return sV.sig(v.tensor_sub(lnm[:, 3:4], lnm[:, 3:4], lnm[:, 4:5]))
            tvar = emit(block.vector, _)

            def _(s):
                wt(s, tvar)
                i = s.activation(lnm[:, 4:5], lnm[:, 3:4], ACT.Ln,
                                 bias=cst_sm[:, 0:1])
                wt(s, sS.sig(i))
                return sS.sig(s.activation(lnm[:, 5:6], lnm[:, 4:5], ACT.Exp,
                                           scale=-0.5))
            tsd = emit(block.scalar, _)

            def _(v):
                wt(v, tsd)
                i = v.tensor_scalar(lnm[:, 6:7], lnm[:, 2:3], -1.0, None, op0=OP.mult)
                wt(v, sV.sig(i))
                return sV.sig(v.tensor_scalar(X[:, c, :], X[:, c, :],
                                              lnm[:, 6:7], lnm[:, 5:6],
                                              op0=OP.add, op1=OP.mult))
            tln = emit(block.vector, _)

            if skip_t:
                return tln, [tln]
            xsrc = [X[:, c, k * 128:(k + 1) * 128] for k in range(8)]
            xdst = [XNT[:, k, c * 128:(c + 1) * 128] for k in range(8)]
            xev = emit_transposes(xsrc, ptile, xdst, pre_toks=(tln,))
            return tln, xev

        if "l0f" in phases:
            l0fctx = contextlib.ExitStack()
            EF = l0fctx.enter_context
            gateT = EF(nc.sbuf_tensor("gateT", [128, 8, NEXP], BF16))
            gstg = EF(nc.sbuf_tensor("gstg", [128, 8, NEXP], F32))
            scores = EF(nc.sbuf_tensor("scores", [128, NCH, NEXP], F32))
            Lall = EF(nc.sbuf_tensor("Lall", [128, NCH, NEXP], F32))
            Lwk = EF(nc.sbuf_tensor("Lwk", [128, NCH, NEXP], F32))
            rmax = EF(nc.sbuf_tensor("rmax", [128, 4, NCH], F32))
            mw1 = [EF(nc.sbuf_tensor(f"mw1_{i}", [128, 8, 512], BF16)) for i in range(2)]
            mw3 = [EF(nc.sbuf_tensor(f"mw3_{i}", [128, 8, 512], BF16)) for i in range(2)]
            mw2 = [EF(nc.sbuf_tensor(f"mw2_{i}", [128, 4, D], BF16)) for i in range(2)]
            act_bf = EF(nc.sbuf_tensor("act_bf", [128, 4, 512], BF16))
            sg_sb2 = EF(nc.sbuf_tensor("sg_sb2", [128, 4, 512], F32))

            def _(sy):
                wt(sy, *bar_l0a)
                return dW.sig(sy.dma_start(out=gstg[:, :, :],
                                           in_=gate_hbm.rearrange("(k p) e -> p k e", p=128)))
            tgl = emit(block.sync, _)

            def _(v):
                wt(v, tgl, *bar_l0a)
                return sV.sig(v.tensor_copy(gateT[:, :, :], gstg[:, :, :]))
            tgc = emit(block.vector, _)

            rt_toks = []

            def _(t):
                wt(t, tgc)
                for c in range(NCH):
                    for k in range(8):
                        i = t.matmul(pA[:, c * NEXP:(c + 1) * NEXP],
                                     XNT[:, k, c * 128:(c + 1) * 128],
                                     gateT[:, k, :], start=(k == 0), stop=(k == 7))
                    rt_toks.append(sT.sig(i))
                return rt_toks[-1]
            emit(block.tensor, _)

            def _(v):
                wt(v, rt_toks[-1])
                return sV.sig(v.tensor_copy(Lall[:, :, :],
                                            pA[:, 0:NCH * NEXP].rearrange(
                                                "p (c e) -> p c e", e=NEXP)))
            trl = emit(block.vector, _)

            def _(v):
                wt(v, trl)
                i = v.tensor_copy(Lwk[:, :, :], Lall[:, :, :])
                wt(v, sV.sig(i))
                for j in range(4):
                    i = v.tensor_reduce(rmax[:, j, :], Lwk[:, :, :], axis=AX.X,
                                        op=OP.max)
                    wt(v, sV.sig(i))
                    if j < 3:
                        i = v.tensor_tensor(qn[:, 0:NCH * NEXP].rearrange(
                                                "p (c e) -> p c e", e=NEXP),
                                            Lwk[:, :, :],
                                            rmax[:, j, :].unsqueeze(2).broadcast_to(
                                                [128, NCH, NEXP]),
                                            op=OP.is_ge)
                        wt(v, sV.sig(i))
                        i = v.scalar_tensor_tensor(Lwk[:, :, :],
                                                   qn[:, 0:NCH * NEXP].rearrange(
                                                       "p (c e) -> p c e", e=NEXP),
                                                   -1e30, Lwk[:, :, :],
                                                   op0=OP.mult, op1=OP.add)
                        wt(v, sV.sig(i))
                i = v.tensor_tensor(kn[:, 0:NCH * NEXP].rearrange(
                                        "p (c e) -> p c e", e=NEXP),
                                    Lall[:, :, :],
                                    rmax[:, 3, :].unsqueeze(2).broadcast_to(
                                        [128, NCH, NEXP]),
                                    op=OP.is_ge)
                i = v.tensor_scalar(rmax[:, 1, :], rmax[:, 0, :], -1.0, None,
                                    op0=OP.mult)
                return sV.sig(i)
            ttk = emit(block.vector, _)

            def _(v):
                wt(v, ttk)
                i = v.tensor_tensor(Lwk[:, :, :], Lall[:, :, :],
                                    rmax[:, 0, :].unsqueeze(2).broadcast_to(
                                        [128, NCH, NEXP]),
                                    op=OP.subtract)
                return sV.sig(i)
            tsub = emit(block.vector, _)

            def _(s):
                wt(s, tsub)
                return sS.sig(s.activation(Lwk[:, :, :], Lwk[:, :, :], ACT.Exp))
            texp = emit(block.scalar, _)

            def _(v):
                wt(v, texp)
                i = v.tensor_mul(Lwk[:, :, :], Lwk[:, :, :],
                                 kn[:, 0:NCH * NEXP].rearrange(
                                     "p (c e) -> p c e", e=NEXP))
                wt(v, sV.sig(i))
                i = v.tensor_reduce(rmax[:, 2, :], Lwk[:, :, :], axis=AX.X,
                                    op=OP.add)
                wt(v, sV.sig(i))
                i = v.reciprocal(rmax[:, 3, :], rmax[:, 2, :])
                wt(v, sV.sig(i))
                i = v.tensor_tensor(scores[:, :, :], Lwk[:, :, :],
                                    rmax[:, 3, :].unsqueeze(2).broadcast_to(
                                        [128, NCH, NEXP]),
                                    op=OP.mult)
                return sV.sig(i)
            tsc_done = emit(block.vector, _)

            wbuf_free = [None, None]
            cast_tok = [None, None]
            acc_tok = [None] * NCH
            prev_dma = tgl
            act_ev = [None, None]
            dn_ev = [None, None]

            for e in range(NEXP):
                b = e % 2

                def _(g, e=e, b=b):
                    wt(g, wbuf_free[b])
                    if e < 2:
                        wt(g, *bar_l0a)
                    for k in range(8):
                        dG.sig(g.dma_start(out=mw1[b][:, k, :],
                                           in_=moe_w1_hbm[e, k * 128:(k + 1) * 128, :]))
                        dG.throttle(g)
                    for k in range(8):
                        dG.sig(g.dma_start(out=mw3[b][:, k, :],
                                           in_=moe_w3_hbm[e, k * 128:(k + 1) * 128, :]))
                        dG.throttle(g)
                    for k in range(4):
                        dG.sig(g.dma_start(out=mw2[b][:, k, :],
                                           in_=moe_w2_hbm[e, k * 128:(k + 1) * 128, :]))
                        dG.throttle(g)
                    return dG.tok()
                cast_tok[b] = emit(block.gpsimd, _)

                last_dn_tok = None
                TG = 512 if T >= 512 else T
                SUBS = TG // 128
                GPS = (pA, pA, pC, pC)
                UPS = (pB, pB, pD, pD)
                for g in range(T // TG):
                    def _(t, g=g, e=e):
                        wt(t, cast_tok[b], tsc_done)
                        if e == 0 and g == 0:
                            wt(t, trl)
                        wt(t, act_ev[0], act_ev[1])
                        wt(t, dn_ev[0], dn_ev[1])
                        i = None
                        for it in range(4):
                            gp, up = GPS[it], UPS[it]
                            col = (it % 2) * 512
                            for k in range(8):
                                st, sp = (k == 0), (k == 7)
                                t.matmul(gp[:, col:col + TG],
                                         mw1[b][:, k, it * 128:(it + 1) * 128],
                                         XNT[:, k, g * TG:(g + 1) * TG],
                                         start=st, stop=sp)
                                i = t.matmul(up[:, col:col + TG],
                                             mw3[b][:, k, it * 128:(it + 1) * 128],
                                             XNT[:, k, g * TG:(g + 1) * TG],
                                             start=st, stop=sp)
                        return sT.sig(i)
                    tup = emit(block.tensor, _)

                    sg_toks = []

                    def _(s, tup=tup):
                        wt(s, tup)
                        for it in range(4):
                            gp = GPS[it]
                            col = (it % 2) * 512
                            sg_toks.append(sS.sig(s.activation(
                                sg_sb[:, it % 2, 0:TG] if False else sg_sb2[:, it, 0:TG],
                                gp[:, col:col + TG], ACT.Silu)))
                        return None
                    emit(block.scalar, _)

                    def _(v):
                        i = None
                        for it in range(4):
                            up = UPS[it]
                            col = (it % 2) * 512
                            wt(v, sg_toks[it])
                            i = v.tensor_mul(act_bf[:, it, 0:TG],
                                             sg_sb2[:, it, 0:TG],
                                             up[:, col:col + TG])
                        return sV.sig(i)
                    aev = emit(block.vector, _)
                    act_ev[0] = act_ev[1] = aev

                    for sub in range(SUBS):
                        ch = g * SUBS + sub
                        ps = pC if sub % 2 == 0 else pD
                        def _(t, sub=sub, g=g, ps=ps):
                            wt(t, act_ev[0])
                            wt(t, dn_ev[sub % 2])
                            i = None
                            for it in range(4):
                                st, sp = (it == 0), (it == 3)
                                t.matmul(ps[:, 0:512],
                                         act_bf[:, it, sub * 128:(sub + 1) * 128],
                                         mw2[b][:, it, 0:512], start=st, stop=sp)
                                i = t.matmul(ps[:, 512:1024],
                                             act_bf[:, it, sub * 128:(sub + 1) * 128],
                                             mw2[b][:, it, 512:1024], start=st, stop=sp)
                            return sT.sig(i)
                        tdn = emit(block.tensor, _)
                        last_dn_tok = tdn

                        def _(v, ch=ch, e=e, tdn=tdn, ps=ps):
                            wt(v, tdn, acc_tok[ch], tsc_done)
                            return sV.sig(v.scalar_tensor_tensor(
                                X[:, ch, :], ps[:, :], scores[:, ch, e:e + 1],
                                X[:, ch, :], op0=OP.mult, op1=OP.add))
                        acc_tok[ch] = emit(block.vector, _)
                        dn_ev[sub % 2] = acc_tok[ch]
                wbuf_free[b] = last_dn_tok

            for c in range(NCH):
                pre = [acc_tok[c]]
                if c == 0:
                    pre += [last_dn_tok]
                emit_ln_chunk(c, pre, pB if c % 2 else pA)
            l0fctx.close()

        bar_l0f = [(sT, sT.n), (sV, sV.n), (sS, sS.n)]

        if "l1a" in phases:
            S1 = float(np.float32(64.0 ** (-0.25)))
            S2 = float(np.float32(np.float64(64.0 ** (-0.25)) / np.float64(S1)))
            LN1E4 = np.float64(np.log(10000.0)) / 32.0
            E1 = float(np.float32(-LN1E4))
            E2 = float(np.float32(-LN1E4 - np.float64(np.float32(-LN1E4))))
            TWO_PI = 2.0 * np.pi
            C1 = float(np.float32(TWO_PI))
            C2 = float(np.float32(TWO_PI - np.float64(np.float32(TWO_PI))))
            INV2PI = float(np.float32(1.0 / TWO_PI))
            PI_LO = float(np.float32(3.14159250))
            PI2_HI = float(np.float32(np.pi / 2))
            PI2_LO = float(np.float32(np.pi / 2 - np.float64(np.float32(np.pi / 2))))

            l1ctx = contextlib.ExitStack()
            EL = l1ctx.enter_context
            wqkv = EL(nc.sbuf_tensor("wqkv", [128, 8, 3 * D], BF16))
            wout = EL(nc.sbuf_tensor("wout", [128, 8, D], BF16))

            cost = EL(nc.sbuf_tensor("cost", [128, NCH, 32], F32))
            sint = EL(nc.sbuf_tensor("sint", [128, NCH, 32], F32))
            invf = EL(nc.sbuf_tensor("invf", [128, 32], F32))
            pi2t = EL(nc.sbuf_tensor("pi2t", [128, 1], F32))
            UT = EL(nc.sbuf_tensor("UT", [128, 128], F32))
            xTl = EL(nc.sbuf_tensor("xTl", [128, 8, 128], BF16))
            vaug = EL(nc.sbuf_tensor("vaug", [128, H, 65], BF16))
            kTz = EL(nc.sbuf_tensor("kTz", [128, H, 128], BF16))
            qTp = EL(nc.sbuf_tensor("qTp", [128, 8, 128], BF16))
            ktk = EL(nc.sbuf_tensor("ktk", [128, H, 64], BF16))
            PTl = EL(nc.sbuf_tensor("PTl", [128, 8, 128], BF16))
            KVf = EL(nc.sbuf_tensor("KVf", [128, H, 65], F32))
            KVb = EL(nc.sbuf_tensor("KVb", [128, H, 65], BF16))
            rtmp = EL(nc.sbuf_tensor("rtmp", [128, 3, 512], F32))
            rrep = EL(nc.sbuf_tensor("rrep", [128, D], F32))
            trctx = contextlib.ExitStack()
            trig2 = trctx.enter_context(nc.sbuf_tensor("trig2", [128, NCH, 32], F32))
            rni = trctx.enter_context(nc.sbuf_tensor("rni", [128, NCH, 16], mybir.dt.int32))

            l1w_cast = [None]

            def _(g):
                wt(g, *bar_l0f)
                for q in range(8):
                    for j in range(3):
                        dG.sig(g.dma_start(
                            out=wqkv[:, q, j * 1024:(j + 1) * 1024],
                            in_=l1wqkv_hbm[q][:, j * 1024:(j + 1) * 1024]))
                        dG.throttle(g)
                for q in range(8):
                    dG.sig(g.dma_start(out=wout[:, q, :], in_=l1wout_hbm[q]))
                    dG.throttle(g)
                return dG.tok()
            l1w_cast[0] = emit(block.gpsimd, _)

            def _(sy):
                wt(sy, *bar_l0f)
                dC.sig(sy.dma_start(out=rrep[:, :], in_=rrep_hbm))
                return dC.sig(sy.dma_start(out=UT[:, :], in_=ut_hbm))
            ut_tok = emit(block.sync, _)

            def _(g):
                wt(g, *bar_l0f)
                g.iota(cost[:, :, 0:1], pattern=[[128, NCH]], base=0,
                       channel_multiplier=1,
                       allow_small_or_imprecise_dtypes=True)
                return sG.sig(g.iota(invf[0:1, :], pattern=[[1, 32]], base=0,
                                     channel_multiplier=0,
                                     allow_small_or_imprecise_dtypes=True))
            iota_tok = emit(block.gpsimd, _)

            def _(s):
                wt(s, iota_tok)
                i = s.activation(trig2[0:1, 0, 0:32], invf[0:1, :], ACT.Exp, scale=E1)
                wt(s, sS.sig(i))
                return sS.sig(s.activation(invf[0:1, :], invf[0:1, :], ACT.Exp,
                                           scale=E2))
            tinv1 = emit(block.scalar, _)

            def _(v):
                wt(v, tinv1)
                i = v.tensor_mul(invf[0:1, :], invf[0:1, :], trig2[0:1, 0, 0:32])
                wt(v, sV.sig(i))
                i = v.memset(pi2t[:, :], PI2_HI)
                wt(v, sV.sig(i))
                return sV.sig(v.tensor_scalar(pi2t[:, :], pi2t[:, :], PI2_LO,
                                              None, op0=OP.add))
            tinv2 = emit(block.vector, _)

            def _(t):
                wt(t, tinv2, ut_tok)
                return sT.sig(t.matmul(pA[:, 0:32], UT[0:1, :],
                                       invf[0:1, :], start=True, stop=True))
            tbmm = emit(block.tensor, _)

            def _(v):
                wt(v, tbmm)
                return sV.sig(v.tensor_copy(invf[:, :], pA[:, 0:32]))
            tbc = emit(block.vector, _)

            def _(v):
                wt(v, tbc, iota_tok)
                i = v.tensor_tensor(trig2[:, :, :],
                                    cost[:, :, 0:1].broadcast_to([128, NCH, 32]),
                                    invf[:, :].unsqueeze(1).broadcast_to([128, NCH, 32]),
                                    op=OP.mult)
                wt(v, sV.sig(i))
                i = v.tensor_scalar(cost[:, :, :], trig2[:, :, :], INV2PI, 0.5,
                                    op0=OP.mult, op1=OP.add)
                return sV.sig(i)
            tph = emit(block.vector, _)

            def _(v):
                wt(v, tph)
                for h in range(2):
                    i = v.tensor_copy(rni[:, :, :], cost[:, :, h * 16:h * 16 + 16])
                    wt(v, sV.sig(i))
                    i = v.tensor_copy(cost[:, :, h * 16:h * 16 + 16], rni[:, :, :])
                    wt(v, sV.sig(i))
                i = v.scalar_tensor_tensor(trig2[:, :, :], cost[:, :, :], -C1,
                                           trig2[:, :, :], op0=OP.mult, op1=OP.add)
                wt(v, sV.sig(i))
                i = v.scalar_tensor_tensor(trig2[:, :, :], cost[:, :, :], -C2,
                                           trig2[:, :, :], op0=OP.mult, op1=OP.add)
                wt(v, sV.sig(i))
                i = v.tensor_scalar(cost[:, :, :], trig2[:, :, :], PI_LO, None,
                                    op0=OP.is_gt)
                wt(v, sV.sig(i))
                i = v.scalar_tensor_tensor(trig2[:, :, :], cost[:, :, :], -C1,
                                           trig2[:, :, :], op0=OP.mult, op1=OP.add)
                wt(v, sV.sig(i))
                i = v.scalar_tensor_tensor(trig2[:, :, :], cost[:, :, :], -C2,
                                           trig2[:, :, :], op0=OP.mult, op1=OP.add)
                wt(v, sV.sig(i))
                i = v.tensor_scalar(cost[:, :, :], trig2[:, :, :], -PI_LO, None,
                                    op0=OP.is_lt)
                wt(v, sV.sig(i))
                i = v.scalar_tensor_tensor(trig2[:, :, :], cost[:, :, :], C1,
                                           trig2[:, :, :], op0=OP.mult, op1=OP.add)
                wt(v, sV.sig(i))
                i = v.scalar_tensor_tensor(trig2[:, :, :], cost[:, :, :], C2,
                                           trig2[:, :, :], op0=OP.mult, op1=OP.add)
                wt(v, sV.sig(i))
                i = v.tensor_scalar(trig2[:, :, :], trig2[:, :, :], PI_LO, None,
                                    op0=OP.min)
                wt(v, sV.sig(i))
                i = v.tensor_scalar(trig2[:, :, :], trig2[:, :, :], -PI_LO, None,
                                    op0=OP.max)
                return sV.sig(i)
            tred = emit(block.vector, _)

            def _(s):
                wt(s, tred)
                s.activation(sint[:, :, :], trig2[:, :, :], ACT.Sin)
                i = s.activation(cost[:, :, :], trig2[:, :, :], ACT.Abs)
                return sS.sig(i)
            tsin = emit(block.scalar, _)

            def _(s):
                wt(s, tsin)
                return sS.sig(s.activation(cost[:, :, :], cost[:, :, :],
                                           ACT.Sin, bias=pi2t[:, :], scale=-1.0))
            tcos = emit(block.scalar, _)

            def _(v):
                wt(v, tcos)
                i = v.memset(KVf[:, :, :], 0.0)
                wt(v, sV.sig(i))
                i = v.memset(KVb[:, :, :], 0.0)
                wt(v, sV.sig(i))
                i = v.memset(vaug[:, :, 64:65], 1.0)
                wt(v, sV.sig(i))
                return sV.sig(v.memset(kTz[:, :, :], 0.0))
            init_tok = emit(block.vector, _)

            st_l1 = {"pv_evac": None, "kv_cast": init_tok, "wo": None,
                     "att_V": None, "att_S": None, "ptev": None,
                     "kvup": None}

            for c in range(NCH):
                xsrc = [X[:, c, k * 128:(k + 1) * 128] for k in range(8)]
                xdst = [xTl[:, k, :] for k in range(8)]
                xev = emit_transposes(xsrc, pD, xdst,
                                      pre_toks=(st_l1["pv_evac"], init_tok))

                def _(t, c=c):
                    wt(t, l1w_cast[0], st_l1["att_V"], st_l1["att_S"])
                    for k in range(8):
                        wt(t, xev[k])
                        st, sp = (k == 0), (k == 7)
                        for half in range(2):
                            t.matmul(pA[:, half * 512:(half + 1) * 512],
                                     xTl[:, k, :], wqkv[:, k, half * 512:(half + 1) * 512],
                                     start=st, stop=sp)
                            t.matmul(pB[:, half * 512:(half + 1) * 512],
                                     xTl[:, k, :], wqkv[:, k, D + half * 512:D + (half + 1) * 512],
                                     start=st, stop=sp)
                            i = t.matmul(pC[:, half * 512:(half + 1) * 512],
                                         xTl[:, k, :], wqkv[:, k, 2 * D + half * 512:2 * D + (half + 1) * 512],
                                         start=st, stop=sp)
                    return sT.sig(i)
                tqkv = emit(block.tensor, _)

                def _(s, c=c):
                    wt(s, tqkv)
                    return sS.sig(s.copy(vaug[:, :, 0:64],
                                         pC[:, :].rearrange("p (h d) -> p h d", h=H)))
                tvev = emit(block.scalar, _)

                rp_toks = []
                for (srcp, dst) in ((pA, qn), (pB, kn)):
                    def _(v, c=c, srcp=srcp, dst=dst):
                        wt(v, tqkv, rp_toks[-1] if rp_toks else None)
                        s3 = srcp[:, :].rearrange("p (h d) -> p h d", h=H)
                        d3 = dst[:, :].rearrange("p (h d) -> p h d", h=H)
                        cosb = cost[:, c, :].unsqueeze(1).broadcast_to([128, H, 32])
                        sinb = sint[:, c, :].unsqueeze(1).broadcast_to([128, H, 32])
                        c1t = rtmp[:, 0, :].rearrange("p (h x) -> p h x", h=H)
                        s1t = rtmp[:, 1, :].rearrange("p (h x) -> p h x", h=H)
                        t2t = rtmp[:, 2, :].rearrange("p (h x) -> p h x", h=H)
                        v.tensor_tensor(c1t, s3[:, :, 0:32], cosb, op=OP.mult)
                        v.tensor_tensor(s1t, s3[:, :, 0:32], sinb, op=OP.mult)
                        v.tensor_tensor(d3[:, :, 32:64], s3[:, :, 32:64], cosb, op=OP.mult)
                        i = v.tensor_tensor(t2t, s3[:, :, 32:64], sinb, op=OP.mult)
                        wt(v, sV.sig(i))
                        v.tensor_add(d3[:, :, 32:64], d3[:, :, 32:64], s1t)
                        i = v.tensor_sub(d3[:, :, 0:32], c1t, t2t)
                        wt(v, sV.sig(i))
                        i = v.tensor_scalar(dst[:, :], dst[:, :], S1, S2,
                                            op0=OP.mult, op1=OP.mult)
                        wt(v, sV.sig(i))
                        i = v.tensor_scalar(rtmp[:, 0:2, :].rearrange("p a x -> p (a x)"),
                                            dst[:, :], 0.0, None, op0=OP.min)
                        wt(v, sV.sig(i))
                        return sV.sig(v.tensor_scalar(dst[:, :], dst[:, :], 0.0,
                                                      None, op0=OP.max))
                    trelu = emit(block.vector, _)

                    def _(s, trelu=trelu):
                        wt(s, trelu)
                        return sS.sig(s.activation(
                            rtmp[:, 0:2, :].rearrange("p a x -> p (a x)"),
                            rtmp[:, 0:2, :].rearrange("p a x -> p (a x)"), ACT.Exp))
                    texp = emit(block.scalar, _)

                    def _(v, dst=dst, texp=texp):
                        wt(v, texp)
                        i = v.tensor_add(dst[:, :], dst[:, :],
                                         rtmp[:, 0:2, :].rearrange("p a x -> p (a x)"))
                        return sV.sig(i)
                    rp_toks.append(emit(block.vector, _))
                trp = rp_toks[-1]

                def _(v):
                    wt(v, trp, st_l1["kvup"])
                    return sV.sig(v.tensor_copy(
                        ktk[:, :, :], kn[:, :].rearrange("p (h d) -> p h d", h=H)))
                tktk = emit(block.vector, _)

                qsrc = [qn[:, k * 128:(k + 1) * 128] for k in range(8)]
                qdst = [qTp[:, k, :] for k in range(8)]
                qev = emit_transposes(qsrc, pA, qdst, pre_toks=(trp,))
                ksrc = [kn[:, k * 128:(k + 1) * 128] for k in range(8)]
                kdst = [kTz[:, 2 * k, :] for k in range(8)]
                kev = emit_transposes(ksrc, pB, kdst, pre_toks=(trp,),
                                      qtz_split=kTz)

                pv_toks = []
                for g in range(2):
                    def _(t, g=g, c=c):
                        for tk in qev + kev:
                            wt(t, tk)
                        wt(t, st_l1["ptev"], st_l1["kv_cast"], tvev)
                        if g == 1:
                            wt(t, pv_toks[0])
                        i = None
                        for hh in range(8):
                            h = g * 8 + hh
                            i = t.matmul(pA[:, hh * 128:(hh + 1) * 128],
                                         kTz[:, h, :], qTp[:, h // 2, :],
                                         start=True, stop=True)
                        return sT.sig(i)
                    tat = emit(block.tensor, _)

                    def _(v, g=g, tat=tat):
                        wt(v, tat, ut_tok)
                        i = v.tensor_tensor(
                            PTl[:, :, :],
                            pA[:, 0:1024].rearrange("p (a b) -> p a b", a=8),
                            UT[:, :].unsqueeze(1).broadcast_to([128, 8, 128]),
                            op=OP.mult)
                        return sV.sig(i)
                    tmask = emit(block.vector, _)
                    st_l1["ptev"] = tmask

                    def _(t, g=g, c=c, tmask=tmask):
                        wt(t, tmask)
                        ps = pD if g == 0 else pC
                        i = None
                        for hh in range(8):
                            h = g * 8 + hh
                            col = (hh // 4) * 512 + (hh % 4) * 66
                            t.matmul(ps[:, col:col + 65], PTl[:, hh, :],
                                     vaug[:, h, :], start=True, stop=False)
                            i = t.matmul(ps[:, col:col + 65], qTp[:, h // 2, :],
                                         KVb[:, h, :], start=False, stop=True)
                        return sT.sig(i)
                    pv_toks.append(emit(block.tensor, _))

                def _(t, c=c):
                    wt(t, tktk, pv_toks[0], pv_toks[1])
                    for tk in kev:
                        wt(t, tk)
                    i = None
                    for h in range(H):
                        sl = h // 2
                        po = (h % 2) * 64
                        col = (sl // 4) * 512 + (sl % 4) * 66
                        i = t.matmul(pB[po:po + 64, col:col + 65],
                                     ktk[:, h, :], vaug[:, h, :],
                                     start=True, stop=True,
                                     tile_position=(0, po))
                    return sT.sig(i)
                tkv = emit(block.tensor, _)
                st_l1["kvup"] = tkv

                def _(v, tkv=tkv):
                    wt(v, tkv, st_l1["kv_cast"])
                    i = None
                    for half in range(2):
                        for po in range(2):
                            hs = half * 8 + po
                            he = min(hs + 8, H)
                            i = v.tensor_add(
                                KVf[po * 64:po * 64 + 64, hs:he:2, :],
                                KVf[po * 64:po * 64 + 64, hs:he:2, :],
                                pB[po * 64:po * 64 + 64,
                                   half * 512:half * 512 + 264].rearrange(
                                    "p (s e) -> p s e", e=66)[:, :, 0:65])
                    wt(v, sV.sig(i))
                    return sV.sig(v.tensor_copy(KVb[:, :, :], KVf[:, :, :]))
                st_l1["kv_cast"] = emit(block.vector, _)

                def _(v):
                    wt(v, pv_toks[0], pv_toks[1])
                    i = None
                    for g4 in range(4):
                        ps = pD if g4 < 2 else pC
                        col = (g4 % 2) * 512
                        i = v.tensor_copy(nrm[:, 6, g4 * 4:(g4 + 1) * 4],
                                          ps[:, col:col + 264].rearrange(
                                              "p (h e) -> p h e", h=4)[:, :, 64])
                    wt(v, sV.sig(i))
                    i = v.tensor_scalar(nrm[:, 6, :], nrm[:, 6, :], 1e-6, None,
                                        op0=OP.max)
                    wt(v, sV.sig(i))
                    i = v.reciprocal(nrm[:, 7, :], nrm[:, 6, :])
                    wt(v, sV.sig(i))
                    i = None
                    for g4 in range(4):
                        ps = pD if g4 < 2 else pC
                        col = (g4 % 2) * 512
                        i = v.tensor_tensor(
                            qn[:, :].rearrange("p (h d) -> p h d", h=H)[:, g4 * 4:(g4 + 1) * 4, :],
                            ps[:, col:col + 264].rearrange("p (h e) -> p h e", h=4)[:, :, 0:64],
                            nrm[:, 7, g4 * 4:(g4 + 1) * 4].unsqueeze(2).broadcast_to([128, 4, HD]),
                            op=OP.mult)
                    return sV.sig(i)
                tpvev = emit(block.vector, _)
                st_l1["pv_evac"] = tpvev

                aosrc = [qn[:, k * 128:(k + 1) * 128] for k in range(8)]
                aodst = [xTl[:, k, :] for k in range(8)]
                aoev = emit_transposes(aosrc, pB, aodst,
                                       pre_toks=(tpvev, st_l1["kv_cast"]),
                                       evac="scalar")

                def _(t, c=c):
                    wt(t, st_l1["wo"])
                    for k in range(8):
                        wt(t, aoev[k])
                        st, sp = (k == 0), (k == 7)
                        t.matmul(pA[:, 0:512], xTl[:, k, :], wout[:, k, 0:512],
                                 start=st, stop=sp)
                        i = t.matmul(pA[:, 512:1024], xTl[:, k, :],
                                     wout[:, k, 512:1024], start=st, stop=sp)
                    return sT.sig(i)
                twoT = emit(block.tensor, _)

                def _(v, c=c):
                    wt(v, twoT)
                    return sV.sig(v.tensor_add(X[:, c, :], X[:, c, :], pA[:, :]))
                tres = emit(block.vector, _)
                st_l1["wo"] = tres

                tln, _ = emit_ln_chunk(c, (tres,), pC, skip_t=True)

                def _(s, tln=tln):
                    wt(s, tln)
                    return sS.sig(s.activation(kn[:, :], X[:, c, :], ACT.Square,
                                               accum_out=lnm[:, 0:1]))
                tss2 = emit(block.scalar, _)

                def _(v, c=c, tss2=tss2):
                    wt(v, tss2, ut_tok)
                    i = v.tensor_mul(kn[:, :], X[:, c, :], rrep[:, :])
                    wt(v, sV.sig(i))
                    i = v.tensor_reduce(svals[:, c:c + 1], kn[:, :], axis=AX.X,
                                        op=OP.add)
                    return sV.sig(i)
                tsv2 = emit(block.vector, _)

                def _(s, tsv2=tsv2):
                    wt(s, tsv2)
                    i = s.activation(lnm[:, 1:2], lnm[:, 0:1], ACT.Ln)
                    wt(s, sS.sig(i))
                    return sS.sig(s.activation(lnm[:, 2:3], lnm[:, 1:2], ACT.Exp,
                                               scale=-0.5, bias=cst_sm[:, 1:2]))
                tn2 = emit(block.scalar, _)

                def _(v, c=c, tn2=tn2):
                    wt(v, tn2)
                    i = v.tensor_scalar(lnm[:, 2:3], lnm[:, 2:3], 3.2e13, None,
                                        op0=OP.min)
                    wt(v, sV.sig(i))
                    return sV.sig(v.tensor_scalar(qn[:, :], X[:, c, :],
                                                  lnm[:, 2:3], None, op0=OP.mult))
                txn2 = emit(block.vector, _)

                xsrc2 = [qn[:, k * 128:(k + 1) * 128] for k in range(8)]
                xdst2 = [XNT[:, k, c * 128:(c + 1) * 128] for k in range(8)]
                x3ev = emit_transposes(xsrc2, pC, xdst2, pre_toks=(txn2,))
                rms_sv_tok[c] = x3ev[-1]
                st_l1["att_V"] = x3ev[-1]
                st_l1["att_S"] = tn2
                if c == 0:
                    trctx.close()
            l1ctx.close()

        bar_l1a = [(sT, sT.n), (sV, sV.n), (sS, sS.n)]

        if "l1f" in phases:
            N_ITERS = 50
            EFF_K = min(1024.0 * 9.0 / 8.0, float(T))
            LOGK = float(np.float32(np.log(np.float64(EFF_K))))
            A1 = float(np.float32(np.log(np.float64(EFF_K)) - np.log(np.float64(T))))
            NHEAVY = min(1024, T)

            fctx = contextlib.ExitStack()
            EF = fctx.enter_context
            ONES = EF(nc.sbuf_tensor("ONES", [128, 128], F32))
            UT2 = EF(nc.sbuf_tensor("UT2", [128, 128], F32))
            lgam = EF(nc.sbuf_tensor("lgam", [128, 8], F32))
            hgam = EF(nc.sbuf_tensor("hgam", [128, 8], F32))
            lw1 = EF(nc.sbuf_tensor("lw1", [128, 8, 512], BF16))
            lw2 = EF(nc.sbuf_tensor("lw2", [128, 4, D], BF16))
            lb1 = EF(nc.sbuf_tensor("lb1", [1, 512], BF16))
            lb2 = EF(nc.sbuf_tensor("lb2", [1, D], BF16))
            hb2 = EF(nc.sbuf_tensor("hb2", [1, D], BF16))
            hw1q = [EF(nc.sbuf_tensor(f"hw1q{i}", [128, 8, 1024], BF16)) for i in range(2)]
            hw2q = [EF(nc.sbuf_tensor(f"hw2q{i}", [128, 8, 1024], BF16)) for i in range(2)]
            hb1q = EF(nc.sbuf_tensor("hb1q", [1, 1024], BF16))
            onesr = EF(nc.sbuf_tensor("onesr", [1, 512], BF16))
            hh_bf = EF(nc.sbuf_tensor("hh_bf", [128, 8, 512], BF16))
            hl_bf = hh_bf
            fmeta = EF(nc.sbuf_tensor("fmeta", [128, 8], F32))
            sat = EF(nc.sbuf_tensor("sat", [128, NCH], F32))
            hmask = EF(nc.sbuf_tensor("hmask", [128, NCH], F32))
            pcol = EF(nc.sbuf_tensor("pcol", [128, NCH], F32))

            def _(sy):
                wt(sy, *bar_l1a)
                dC.sig(sy.dma_start(out=ONES[:, :], in_=ones_hbm))
                dC.sig(sy.dma_start(out=UT2[:, :], in_=ut_hbm))
                dC.sig(sy.dma_start(out=lgam[:, :], in_=lgam_hbm))
                return dC.sig(sy.dma_start(out=hgam[:, :], in_=hgam_hbm))
            cst2 = emit(block.sync, _)

            def _(g):
                wt(g, *bar_l1a)
                for k in range(8):
                    dG.sig(g.dma_start(out=lw1[:, k, :], in_=lw1_hbm[k]))
                    dG.throttle(g)
                for k in range(4):
                    dG.sig(g.dma_start(out=lw2[:, k, :], in_=lw2_hbm[k]))
                    dG.throttle(g)
                return dG.tok()
            lwd = emit(block.gpsimd, _)

            def _(v):
                wt(v, lwd, cst2, *bar_l1a)
                i = None
                for k in range(8):
                    i = v.tensor_scalar(lw1[:, k, :], lw1[:, k, :],
                                        lgam[:, k:k + 1], None, op0=OP.mult)
                return sV.sig(i)
            ltoks = [emit(block.vector, _)]

            def _(g):
                wt(g, *bar_l1a)
                dG.sig(g.dma_start(out=lb1[0:1, :], in_=lb1_hbm))
                dG.throttle(g)
                dG.sig(g.dma_start(out=lb2[0:1, :], in_=lb2_hbm))
                dG.throttle(g)
                dG.sig(g.dma_start(out=hb2[0:1, :], in_=hb2_hbm))
                dG.throttle(g)
                return dG.tok()
            btok = emit(block.gpsimd, _)

            def _(v):
                wt(v, btok)
                return sV.sig(v.memset(onesr[0:1, :], 1.0))
            bcast_tok = emit(block.vector, _)

            rms_toks = [rms_sv_tok[NCH - 1]]
            sv_tok = rms_sv_tok[NCH - 1]

            def _(v):
                wt(v, sv_tok)
                return sV.sig(v.memset(fmeta[:, 0:1], A1))
            atok = emit(block.vector, _)

            desc_psum_free = None
            for it in range(N_ITERS - 1):
                def _(v, atok=atok):
                    wt(v, atok)
                    return sV.sig(v.tensor_scalar(kn[:, 0:NCH], svals[:, 0:NCH],
                                                  fmeta[:, 0:1], 0.0,
                                                  op0=OP.add, op1=OP.min))
                t1 = emit(block.vector, _)

                def _(s, t1=t1):
                    wt(s, t1)
                    return sS.sig(s.activation(kn[:, 0:NCH], kn[:, 0:NCH], ACT.Exp))
                t2 = emit(block.scalar, _)

                def _(t, t2=t2, it=it):
                    wt(t, t2, cst2)
                    if it == 0:
                        wt(t, rms_toks[-1])
                    return sT.sig(t.matmul(pD[:, 0:NCH], ONES[:, :], kn[:, 0:NCH],
                                           start=True, stop=True))
                t3 = emit(block.tensor, _)

                def _(v, t3=t3):
                    wt(v, t3)
                    return sV.sig(v.tensor_reduce(fmeta[:, 1:2], pD[:, 0:NCH],
                                                  axis=AX.X, op=OP.add))
                t4 = emit(block.vector, _)

                def _(s, t4=t4):
                    wt(s, t4)
                    return sS.sig(s.activation(fmeta[:, 2:3], fmeta[:, 1:2], ACT.Ln))
                t5 = emit(block.scalar, _)

                def _(v, t5=t5):
                    wt(v, t5)
                    i = v.scalar_tensor_tensor(fmeta[:, 3:4], fmeta[:, 2:3], -1.0,
                                               fmeta[:, 0:1], op0=OP.mult, op1=OP.add)
                    wt(v, sV.sig(i))
                    return sV.sig(v.tensor_scalar(fmeta[:, 0:1], fmeta[:, 3:4],
                                                  LOGK, None, op0=OP.add))
                atok = emit(block.vector, _)

            def _(v):
                wt(v, atok)
                i = v.tensor_scalar(fmeta[:, 4:5], fmeta[:, 0:1], -1.0, None,
                                    op0=OP.mult)
                wt(v, sV.sig(i))
                return sV.sig(v.tensor_scalar(sat[:, :], svals[:, 0:NCH],
                                              fmeta[:, 4:5], None, op0=OP.is_ge))
            sat_tok = emit(block.vector, _)

            def _(t):
                wt(t, sat_tok)
                t.matmul(pD[:, 0:NCH], UT2[:, :], sat[:, :], start=True, stop=True)
                return sT.sig(t.matmul(pD[:, 512:512 + NCH], ONES[:, :], sat[:, :],
                                       start=True, stop=True))
            pre_tok = emit(block.tensor, _)

            def _(v):
                wt(v, pre_tok)
                i = v.tensor_copy(pcol[:, :], pD[:, 0:NCH])
                wt(v, sV.sig(i))
                i = v.tensor_copy(fmeta[:, 5:6], pD[:, 512:513])
                wt(v, sV.sig(i))
                for c in range(1, NCH):
                    i = v.tensor_add(pcol[:, c:c + 1], pcol[:, c:c + 1],
                                     fmeta[:, 5:6])
                    wt(v, sV.sig(i))
                    if c < NCH - 1:
                        i = v.tensor_tensor(fmeta[:, 5:6], fmeta[:, 5:6],
                                            pD[:, 512 + c:512 + c + 1], op=OP.add)
                        wt(v, sV.sig(i))
                return sV.tok()
            px_tok = emit(block.vector, _)

            def _(v):
                wt(v, px_tok)
                if NHEAVY >= T:
                    return sV.sig(v.memset(hmask[:, :], 1.0))
                i = v.tensor_scalar(kn[:, 0:NCH], pcol[:, :], float(NHEAVY), None,
                                    op0=OP.is_le)
                wt(v, sV.sig(i))
                return sV.sig(v.tensor_mul(hmask[:, :], kn[:, 0:NCH], sat[:, :]))
            hm_tok = emit(block.vector, _)

            acc_tok2 = [None] * NCH
            dn_free = [None, None]
            up_free = None
            TG = 512 if T >= 512 else T
            SUBS = TG // 128
            for g in range(T // TG):
                def _(t, g=g):
                    wt(t, ltoks[-1], bcast_tok, rms_toks[-1], up_free)
                    if g == 0:
                        wt(t, hm_tok)
                    i = None
                    for ii in range(4):
                        ps = (pA, pB)[ii // 2]
                        col = (ii % 2) * 512
                        for k in range(8):
                            i = t.matmul(ps[:, col:col + TG],
                                         lw1[:, k, ii * 128:(ii + 1) * 128],
                                         XNT[:, k, g * TG:(g + 1) * TG],
                                         start=(k == 0), stop=False)
                        i = t.matmul(ps[:, col:col + TG],
                                     lb1[0:1, ii * 128:(ii + 1) * 128],
                                     onesr[0:1, 0:TG], start=False, stop=True)
                    return sT.sig(i)
                tup = emit(block.tensor, _)

                def _(s, g=g, tup=tup):
                    wt(s, tup)
                    i = None
                    for ii in range(4):
                        ps = (pA, pB)[ii // 2]
                        col = (ii % 2) * 512
                        i = s.activation(hl_bf[:, ii, 0:TG], ps[:, col:col + TG],
                                         ACT.Gelu)
                    return sS.sig(i)
                tgl = emit(block.scalar, _)
                up_free = tgl

                for sub in range(SUBS):
                    ch = g * SUBS + sub
                    ps = pC if sub % 2 == 0 else pD
                    def _(t, sub=sub, ps=ps, tgl=tgl):
                        wt(t, tgl, dn_free[sub % 2])
                        i = None
                        for ii in range(4):
                            t.matmul(ps[:, 0:512],
                                     hl_bf[:, ii, sub * 128:(sub + 1) * 128],
                                     lw2[:, ii, 0:512], start=(ii == 0), stop=False)
                            i = t.matmul(ps[:, 512:1024],
                                         hl_bf[:, ii, sub * 128:(sub + 1) * 128],
                                         lw2[:, ii, 512:1024], start=(ii == 0),
                                         stop=False)
                        t.matmul(ps[:, 0:512], onesr[0:1, sub * 128:sub * 128 + 128],
                                 lb2[0:1, 0:512], start=False, stop=True)
                        i = t.matmul(ps[:, 512:1024],
                                     onesr[0:1, sub * 128:sub * 128 + 128],
                                     lb2[0:1, 512:1024], start=False, stop=True)
                        return sT.sig(i)
                    tdn = emit(block.tensor, _)

                    def _(v, ch=ch, tdn=tdn, ps=ps):
                        wt(v, tdn, acc_tok2[ch])
                        return sV.sig(v.tensor_add(X[:, ch, :], X[:, ch, :],
                                                   ps[:, :]))
                    acc_tok2[ch] = emit(block.vector, _)
                    dn_free[sub % 2] = acc_tok2[ch]

            hacc = list(acc_tok2)
            hq_prev_mm = None
            hbuf_free = [None, None]
            hup_free = [up_free] * 4
            hdn_free = [None, None]
            for q in range(4):
                qb = q % 2
                def _(g, q=q, qb=qb):
                    wt(g, hbuf_free[qb], cst2)
                    for k in range(8):
                        dG.sig(g.dma_start(
                            out=hw1q[qb][:, k, :],
                            in_=hw1_hbm[k * 128:(k + 1) * 128,
                                        q * 1024:(q + 1) * 1024]))
                        dG.throttle(g)
                    for k in range(8):
                        dG.sig(g.dma_start(
                            out=hw2q[qb][:, k, :],
                            in_=hw2_hbm[q * 1024 + k * 128:q * 1024 + (k + 1) * 128, :]))
                        dG.throttle(g)
                    return dG.tok()
                hwd = emit(block.gpsimd, _)

                def _(v, q=q, qb=qb):
                    wt(v, hwd, hbuf_free[qb], cst2)
                    i = None
                    for k in range(8):
                        i = v.tensor_scalar(hw1q[qb][:, k, :], hw1q[qb][:, k, :],
                                            hgam[:, k:k + 1], None, op0=OP.mult)
                    return sV.sig(i)
                htoks = [emit(block.vector, _)]

                def _(g, q=q):
                    wt(g, htoks[-1], hq_prev_mm)
                    i = dG.sig(g.dma_start(
                        out=hb1q[0:1, :],
                        in_=hb1_hbm[0:1, q * 1024:(q + 1) * 1024]))
                    dG.throttle(g)
                    return i
                hbtok = emit(block.gpsimd, _)

                last_dn = None
                for g in range(T // TG):
                    PSL = (pA, pA, pB, pB, pC, pC, pD, pD)

                    def _(t, q=q, g=g, hbtok=hbtok):
                        wt(t, htoks[-1], hbtok, *hup_free)
                        wt(t, hdn_free[0], hdn_free[1])
                        i = None
                        for ii in range(8):
                            ps = PSL[ii]
                            col = (ii % 2) * 512
                            for k in range(8):
                                t.matmul(ps[:, col:col + TG],
                                         hw1q[qb][:, k, ii * 128:(ii + 1) * 128],
                                         XNT[:, k, g * TG:(g + 1) * TG],
                                         start=(k == 0), stop=False)
                            i = t.matmul(ps[:, col:col + TG],
                                         hb1q[0:1, ii * 128:(ii + 1) * 128],
                                         onesr[0:1, 0:TG], start=False, stop=True)
                        return sT.sig(i)
                    tuph = emit(block.tensor, _)

                    def _(s, tuph=tuph):
                        wt(s, tuph)
                        i = None
                        for ii in range(8):
                            ps = PSL[ii]
                            col = (ii % 2) * 512
                            i = s.activation(hh_bf[:, ii, 0:TG], ps[:, col:col + TG],
                                             ACT.Gelu)
                        return sS.sig(i)
                    tglh = emit(block.scalar, _)
                    hup_free = [tglh] * 4

                    for sub in range(SUBS):
                        ch = g * SUBS + sub
                        ps = pC if sub % 2 == 0 else pD

                        def _(t, q=q, sub=sub, ps=ps, tglh=tglh):
                            wt(t, tglh, hdn_free[sub % 2])
                            i = None
                            for ii in range(8):
                                t.matmul(ps[:, 0:512],
                                         hh_bf[:, ii, sub * 128:(sub + 1) * 128],
                                         hw2q[qb][:, ii, 0:512], start=(ii == 0),
                                         stop=(ii == 7 and q != 0))
                                i = t.matmul(ps[:, 512:1024],
                                             hh_bf[:, ii, sub * 128:(sub + 1) * 128],
                                             hw2q[qb][:, ii, 512:1024], start=(ii == 0),
                                             stop=(ii == 7 and q != 0))
                            if q == 0:
                                t.matmul(ps[:, 0:512],
                                         onesr[0:1, sub * 128:sub * 128 + 128],
                                         hb2[0:1, 0:512], start=False, stop=True)
                                i = t.matmul(ps[:, 512:1024],
                                             onesr[0:1, sub * 128:sub * 128 + 128],
                                             hb2[0:1, 512:1024], start=False, stop=True)
                            return sT.sig(i)
                        tdnh = emit(block.tensor, _)
                        last_dn = tdnh

                        def _(v, ch=ch, tdnh=tdnh, ps=ps):
                            wt(v, tdnh, hacc[ch], hm_tok)
                            return sV.sig(v.scalar_tensor_tensor(
                                X[:, ch, :], ps[:, :], hmask[:, ch:ch + 1],
                                X[:, ch, :], op0=OP.mult, op1=OP.add))
                        hacc[ch] = emit(block.vector, _)
                        hdn_free[sub % 2] = hacc[ch]
                hq_prev_mm = last_dn
                hbuf_free[qb] = last_dn

            ln4_tok = [None] * NCH
            for c in range(NCH):
                pre = [hacc[c]]
                if c == 0:
                    pre.append(hq_prev_mm)
                ln4_tok[c], _ = emit_ln_chunk(c, pre, pA if c % 2 else pB,
                                              skip_t=True)
            fctx.close()



        def _(g):
            for c in range(NCH):
                if "l1f" in phases:
                    wt(g, ln4_tok[c])
                else:
                    wt(g, (sV, sV.n), (sS, sS.n), (sT, sT.n))
                dX.sig(g.dma_start(out=y_hbm[c * 128:(c + 1) * 128, :],
                                   in_=X[:, c, :]))
            if DBG:
                wt(g, (sV, sV.n), (sS, sS.n), (sT, sT.n))
                dX.sig(g.dma_start(out=dbg_hbm["qn"], in_=qn[:, :]))
                dX.sig(g.dma_start(out=dbg_hbm["PT"],
                                   in_=PT[:, :, :, :].rearrange("p a b c -> p (a b c)")))
                dX.sig(g.dma_start(out=dbg_hbm["kT"],
                                   in_=kT[(NCH - 1) % 2][:, :, :].rearrange("p a b -> p (a b)")))
                dX.sig(g.dma_start(out=dbg_hbm["qT"],
                                   in_=qT[(NCH - 1) % 2][:, :, :].rearrange("p a b -> p (a b)")))
                dX.sig(g.dma_start(out=dbg_hbm["nrm"],
                                   in_=nrm[:, :, :].rearrange("p a b -> p (a b)")))
                dX.sig(g.dma_start(out=dbg_hbm["vbf"],
                                   in_=vbf[(NCH - 1) % 2][:, :, :].rearrange("p a b -> p (a b)")))
            g.wait_ge(dX.h, dX.n)
            return None
        emit(block.gpsimd, _)

    return nc


def kernel(**inputs):
    base = host_consts()
    x_full = np.ascontiguousarray(inputs["L0_attn_x"], dtype=np.float32)
    for w, nm in zip("qkvo", ["Wq", "Wk", "Wv", "Wo"]):
        base[f"l0_w{w}T"] = np.ascontiguousarray(
            inputs[f"L0_attn_{nm}"].T.reshape(8, 128, D).astype(np.float32))
    base["l0_gateT"] = np.ascontiguousarray(inputs["L0_ffn_gate_w"].T.astype(np.float32))
    base["l0_moe_w1"] = np.ascontiguousarray(inputs["L0_ffn_w1"].astype(np.float32))
    base["l0_moe_w3"] = np.ascontiguousarray(inputs["L0_ffn_w3"].astype(np.float32))
    base["l0_moe_w2"] = np.ascontiguousarray(inputs["L0_ffn_w2"].astype(np.float32))
    base["l1_wqkv"] = np.ascontiguousarray(
        inputs["L1_attn_w_qkv"].reshape(8, 128, 3 * D).astype(np.float32))
    base["l1_wout"] = np.ascontiguousarray(
        inputs["L1_attn_w_out"].reshape(8, 128, D).astype(np.float32))
    base["l1_rrep"] = np.ascontiguousarray(
        np.tile(np.asarray(inputs["L1_ffn_routing_token"], np.float32)[None, :], (128, 1)))
    base["l1_lgam"] = np.ascontiguousarray(
        np.asarray(inputs["L1_ffn_light_gamma"], np.float32).reshape(8, 128).T)
    base["l1_hgam"] = np.ascontiguousarray(
        np.asarray(inputs["L1_ffn_heavy_gamma"], np.float32).reshape(8, 128).T)
    base["l1_lw1"] = np.ascontiguousarray(
        np.asarray(inputs["L1_ffn_light_w1"], np.float32).reshape(8, 128, 512))
    base["l1_lw2"] = np.ascontiguousarray(
        np.asarray(inputs["L1_ffn_light_w2"], np.float32).reshape(4, 128, D))
    base["l1_lb1"] = np.ascontiguousarray(np.asarray(inputs["L1_ffn_light_b1"], np.float32)[None, :])
    base["l1_lb2"] = np.ascontiguousarray(np.asarray(inputs["L1_ffn_light_b2"], np.float32)[None, :])
    base["l1_hb1"] = np.ascontiguousarray(np.asarray(inputs["L1_ffn_heavy_b1"], np.float32)[None, :])
    base["l1_hb2"] = np.ascontiguousarray(np.asarray(inputs["L1_ffn_heavy_b2"], np.float32)[None, :])
    base["l1_hw1"] = np.ascontiguousarray(np.asarray(inputs["L1_ffn_heavy_w1"], np.float32))
    base["l1_hw2"] = np.ascontiguousarray(np.asarray(inputs["L1_ffn_heavy_w2"], np.float32))
    nc = build_program()
    in_maps = []
    for core in range(8):
        m = dict(base)
        m["x"] = np.ascontiguousarray(x_full[core])
        m["xTin"] = np.ascontiguousarray(x_full[core].T)
        in_maps.append(m)
    res = run_bass_kernel_spmd(nc, in_maps, core_ids=list(range(8)))
    return np.stack([res.results[i]["y"] for i in range(8)], axis=0)


# version 11
# speedup vs baseline: 1.1867x; 1.1867x over previous
import contextlib
import os
import numpy as np
import concourse.bass as bass
import concourse.mybir as mybir
from concourse.bass_utils import run_bass_kernel_spmd

F32 = mybir.dt.float32
BF16 = mybir.dt.bfloat16
AX = mybir.AxisListType
OP = mybir.AluOpType
ACT = mybir.ActivationFunctionType

T = 2048
D = 1024
NCH = 16
H = 16
HD = 64
NEXP = 16
HEAVY_I = 4096
NEG = -1e30

PHASES = ("l0a", "l0f", "l1a", "l1f")


def _split2(v):
    hi = np.float32(v)
    lo = np.float32(np.float64(v) - np.float64(hi))
    return float(hi), float(lo)


ALIBI_SLOPES = [_split2(2.0 ** (-8.0 * np.float64(h) / 15.0)) for h in range(H)]


def host_consts():
    c = {"c_ident": np.eye(128, dtype=np.float32)}
    s_ = np.arange(128)[:, None]
    t_ = np.arange(128)[None, :]
    c["c_UT"] = (s_ <= t_).astype(np.float32)
    c["c_ones"] = np.ones((128, 128), np.float32)
    s = np.arange(128)[:, None]
    t = np.arange(128)[None, :]
    rel0 = 128.0 + s - t
    rel1 = s - t
    val0 = s < t
    val1 = s >= t
    R = np.zeros((128, 2, 128), np.float64)
    Mneg = np.zeros((128, 2, 128), np.float64)
    R[:, 0, :] = np.where(val0, -rel0, 0.0)
    R[:, 1, :] = np.where(val1, -rel1, 0.0)
    Mneg[:, 0, :] = np.where(val0, 0.0, NEG)
    Mneg[:, 1, :] = np.where(val1, 0.0, NEG)
    c["c_R"] = R.astype(np.float32)
    c["c_Mneg"] = Mneg.astype(np.float32)
    return c


class Sem:
    def __init__(self, nc, ctx, name, unit=1):
        self.h = ctx.enter_context(nc.semaphore(name))
        self.unit = unit
        self.n = 0

    def sig(self, inst):
        self.n += self.unit
        inst.then_inc(self.h, self.unit)
        return (self, self.n)

    def tok(self):
        return (self, self.n)

    def throttle(self, eng, max_out=1):
        eng.wait_ge(self.h, self.n)


def wt(eng, *toks):
    best = {}
    for tk in toks:
        if tk is None:
            continue
        s, v = tk
        if v > 0 and v > best.get(id(s), (s, 0))[1]:
            best[id(s)] = (s, v)
    for s, v in best.values():
        eng.wait_ge(s.h, v)


def tr_slot(k):
    return (k % 2) * 512 + ((k // 2) % 2) * 128


def build_program(phases=PHASES):
    nc = bass.Bass("TRN2", target_bir_lowering=False)

    def din(name, shape):
        return nc.dram_tensor(name, list(shape), F32, kind="ExternalInput").ap()

    x_hbm = din("x", (T, D))
    xT_hbm = din("xTin", (D, T))
    l0w_hbm = {w: din(f"l0_w{w}T", (8, 128, D)) for w in "qkvo"}
    gate_hbm = din("l0_gateT", (D, NEXP))
    l1wqkv_hbm = din("l1_wqkv", (8, 128, 3 * D))
    l1wout_hbm = din("l1_wout", (8, 128, D))
    ut_hbm = din("c_UT", (128, 128))
    ones_hbm = din("c_ones", (128, 128))
    rrep_hbm = din("l1_rrep", (128, D))
    lgam_hbm = din("l1_lgam", (128, 8))
    hgam_hbm = din("l1_hgam", (128, 8))
    lw1_hbm = din("l1_lw1", (8, 128, 512))
    lw2_hbm = din("l1_lw2", (4, 128, D))
    lb1_hbm = din("l1_lb1", (1, 512))
    lb2_hbm = din("l1_lb2", (1, D))
    hw1_hbm = din("l1_hw1", (D, HEAVY_I))
    hw2_hbm = din("l1_hw2", (HEAVY_I, D))
    hb1_hbm = din("l1_hb1", (1, HEAVY_I))
    hb2_hbm = din("l1_hb2", (1, D))
    moe_w1_hbm = din("l0_moe_w1", (NEXP, D, 512))
    moe_w3_hbm = din("l0_moe_w3", (NEXP, D, 512))
    moe_w2_hbm = din("l0_moe_w2", (NEXP, 512, D))
    ident_h = din("c_ident", (128, 128))
    R_h = din("c_R", (128, 2, 128))
    Mneg_h = din("c_Mneg", (128, 2, 128))
    y_hbm = nc.dram_tensor("y", [T, D], F32, kind="ExternalOutput").ap()
    DBG = os.environ.get("KDBG")
    if DBG:
        dbg_hbm = {nm: nc.dram_tensor(f"dbg_{nm}", sh, dt, kind="ExternalOutput").ap()
                   for nm, sh, dt in [("qn", [128, D], F32),
                                      ("PT", [128, 8 * 2 * 128], BF16),
                                      ("kT", [128, 8 * 128], BF16),
                                      ("qT", [128, 16 * 128], BF16),
                                      ("nrm", [128, 8 * H], F32),
                                      ("vbf", [128, H * 65], BF16)]}

    ctx = contextlib.ExitStack()
    with ctx:
        E = ctx.enter_context
        X = E(nc.sbuf_tensor("X", [128, NCH, D], F32))
        XNT = E(nc.sbuf_tensor("XNT", [128, 8, T], BF16))
        ident = E(nc.sbuf_tensor("ident", [128, 128], F32))
        identb = E(nc.sbuf_tensor("identb", [128, 128], BF16))
        cst_sm = E(nc.sbuf_tensor("cst_sm", [128, 2], F32))
        lnm = E(nc.sbuf_tensor("lnm", [128, 8], F32))

        svals = E(nc.sbuf_tensor("svals", [128, 32], F32))
        rms_sv_tok = [None] * NCH
        dX = Sem(nc, ctx, "dX", 16)
        dW = Sem(nc, ctx, "dW", 16)
        dxin = [Sem(nc, ctx, f"dxin{c}", 16) for c in range(NCH)]
        dC = Sem(nc, ctx, "dC", 16)
        dSl = [Sem(nc, ctx, f"dSl{i}", 16) for i in range(2)]
        dG = Sem(nc, ctx, "dG", 16)
        sT = Sem(nc, ctx, "sT")
        sV = Sem(nc, ctx, "sV")
        sS = Sem(nc, ctx, "sS")
        sG = Sem(nc, ctx, "sG")
        block = E(nc.Block())

        def load_set6(s):
            s.add_instruction(mybir.InstLoadActFuncSet(
                name=nc.get_next_instruction_name(), act_func_set_id=6,
                ins=[], outs=[]))

        def emit(section, fn):
            out = []

            def wrap(eng):
                out.append(fn(eng))
            section(wrap)
            return out[0]

        def emit_transposes(srcs, ptile, dsts, pre_toks=(), evac="vector",
                            qtz_split=None):
            n = len(srcs)
            ttoks = [None] * n
            sem = sS if evac == "scalar" else sV
            base = sem.n

            def cum(k):
                if qtz_split is None:
                    return base + k + 1
                return base + (2 * (k + 1) if k < 8 else 16 + (k - 8 + 1))
            etoks = [(sem, cum(k)) for k in range(n)]

            def tpart(t):
                for k in range(n):
                    if k == 0:
                        wt(t, *pre_toks)
                    if k >= 2:
                        wt(t, etoks[k - 2])
                    i = t.transpose(ptile[:, tr_slot(k):tr_slot(k) + 128],
                                    srcs[k], ident[:, :])
                    ttoks[k] = sT.sig(i)
            block.tensor(tpart)

            def epart(e):
                for k in range(n):
                    wt(e, ttoks[k])
                    src = ptile[:, tr_slot(k):tr_slot(k) + 128]
                    if qtz_split is not None and k < 8:
                        sem.sig(e.tensor_copy(qtz_split[0:64, 2 * k, :],
                                              src[0:64, :]))
                        sem.sig(e.tensor_copy(qtz_split[64:128, 2 * k + 1, :],
                                              src[64:128, :]))
                    elif evac == "scalar":
                        sem.sig(e.copy(dsts[k], src))
                    else:
                        sem.sig(e.tensor_copy(dsts[k], src))
                assert sem.n == cum(n - 1), (sem.n, base, n)
            (block.scalar if evac == "scalar" else block.vector)(epart)
            return etoks

        xin_tok = [None] * NCH

        def _(sy):
            for c in range(NCH):
                i = sy.dma_start(out=X[:, c, :], in_=x_hbm[c * 128:(c + 1) * 128, :])
                xin_tok[c] = dxin[c].sig(i)
        block.sync(_)

        qn = E(nc.sbuf_tensor("qn", [128, D], F32))
        kn = E(nc.sbuf_tensor("kn", [128, D], F32))
        nrm = E(nc.sbuf_tensor("nrm", [128, 8, H], F32))
        pA = E(nc.psum_tensor("pA", [128, 1024], F32))
        pB = E(nc.psum_tensor("pB", [128, 1024], F32))
        pC = E(nc.psum_tensor("pC", [128, 1024], F32))
        pD = E(nc.psum_tensor("pD", [128, 1024], F32))
        l0actx = contextlib.ExitStack()
        EA = l0actx.enter_context
        l0w = {w: EA(nc.sbuf_tensor(f"l0w{w}", [128, 8, D], BF16)) for w in "qkvo"}
        biasT = EA(nc.sbuf_tensor("biasT", [128, H, 2, 128], F32))
        rctx = contextlib.ExitStack()
        Rt = rctx.enter_context(nc.sbuf_tensor("Rt", [128, 2, 128], F32))
        Mnegt = rctx.enter_context(nc.sbuf_tensor("Mnegt", [128, 2, 128], F32))


        def _(sy):
            dC.sig(sy.dma_start(out=ident[:, :], in_=ident_h))
            dC.sig(sy.dma_start(out=Rt[:, :, :], in_=R_h))
            return dC.sig(sy.dma_start(out=Mnegt[:, :, :], in_=Mneg_h))
        cst_tok = emit(block.sync, _)
        cst_tok = (dC, 48)

        wcast_tok = {}

        def _(g):
            for w in "qkvo":
                for q in range(8):
                    dG.sig(g.dma_start(out=l0w[w][:, q, :], in_=l0w_hbm[w][q]))
                    dG.throttle(g)
                wcast_tok[w] = dG.tok()
            return None
        emit(block.gpsimd, _)

        def _(v):
            wt(v, cst_tok)
            for h in range(H):
                hi, _lo = ALIBI_SLOPES[h]
                i = v.scalar_tensor_tensor(biasT[:, h, :, :], Rt[:, :, :], hi,
                                           Mnegt[:, :, :], op0=OP.mult, op1=OP.add)
            wt(v, sV.sig(i))
            for h in range(H):
                _hi, lo = ALIBI_SLOPES[h]
                i = v.scalar_tensor_tensor(biasT[:, h, :, :], Rt[:, :, :], lo,
                                           biasT[:, h, :, :], op0=OP.mult,
                                           op1=OP.add)
            return sV.sig(i)
        bias_tok = emit(block.vector, _)
        rctx.close()

        def _(v):
            v.memset(cst_sm[:, 1:2], float(np.float32(np.log(np.float64(32.0)))))
            wt(v, cst_tok)
            v.tensor_copy(identb[:, :], ident[:, :])
            return sV.sig(v.memset(cst_sm[:, 0:1], 1e-5))
        eps_tok = emit(block.vector, _)

        xT = EA(nc.sbuf_tensor("xT", [128, 8, 128], BF16))
        vbf = [EA(nc.sbuf_tensor(f"vbf{i}", [128, H, 65], BF16)) for i in range(2)]
        kT = [EA(nc.sbuf_tensor(f"kT{i}", [128, 8, 128], BF16)) for i in range(2)]
        qT = [EA(nc.sbuf_tensor(f"qT{i}", [128, 16, 128], BF16)) for i in range(2)]
        PT = EA(nc.sbuf_tensor("PT", [128, 8, 2, 128], BF16))

        def _(v):
            wt(v, bias_tok)
            v.memset(qT[0][:, :, :], 0.0)
            v.memset(qT[1][:, :, :], 0.0)
            v.memset(vbf[0][:, :, 64:65], 1.0)
            return sV.sig(v.memset(vbf[1][:, :, 64:65], 1.0))
        ones_tok = emit(block.vector, _)

        tok_tqkv = [None] * NCH
        tok_qkev = [None] * NCH
        tok_tnrm = [None] * NCH
        tok_tvev = [None] * NCH
        tok_tsm = [[None, None] for _ in range(NCH)]
        tok_exp = [[None, None] for _ in range(NCH)]
        tok_pvT = [[None, None] for _ in range(NCH)]
        tok_pve = [None] * NCH
        tok_aoev = [None] * NCH
        tok_x1ev = [None] * NCH
        tok_tres = [None] * NCH
        tok_tln = [None] * NCH

        def l0a_xload(c):
            def _(g, c=c):
                wt(g, tok_tqkv[c - 1] if c else bias_tok)
                for k in range(8):
                    dG.sig(g.dma_start(out=xT[:, k, :],
                                       in_=xT_hbm[k * 128:(k + 1) * 128,
                                                  c * 128:(c + 1) * 128]))
                    dG.throttle(g)
                return dG.tok()
            return emit(block.gpsimd, _)

        def l0a_qkv(c, xld):
            def _(t, c=c, xld=xld):
                wt(t, wcast_tok["q"], wcast_tok["k"], wcast_tok["v"], xld)
                if c >= 1:
                    for tk in tok_qkev[c - 1]:
                        wt(t, tk)
                if c >= 2:
                    wt(t, tok_tsm[c - 2][0], tok_tsm[c - 2][1])
                for k in range(8):
                    stt, sp = (k == 0), (k == 7)
                    t.matmul(pA[:, 0:512], xT[:, k, :], l0w["q"][:, k, 0:512], start=stt, stop=sp)
                    t.matmul(pA[:, 512:1024], xT[:, k, :], l0w["q"][:, k, 512:1024], start=stt, stop=sp)
                for k in range(8):
                    stt, sp = (k == 0), (k == 7)
                    t.matmul(pB[:, 0:512], xT[:, k, :], l0w["k"][:, k, 0:512], start=stt, stop=sp)
                    t.matmul(pB[:, 512:1024], xT[:, k, :], l0w["k"][:, k, 512:1024], start=stt, stop=sp)
                if c >= 2:
                    wt(t, tok_pve[c - 2])
                for k in range(8):
                    stt, sp = (k == 0), (k == 7)
                    t.matmul(pC[:, 0:512], xT[:, k, :], l0w["v"][:, k, 0:512], start=stt, stop=sp)
                    i = t.matmul(pC[:, 512:1024], xT[:, k, :], l0w["v"][:, k, 512:1024], start=stt, stop=sp)
                return sT.sig(i)
            tok_tqkv[c] = emit(block.tensor, _)

        def l0a_norm(c):
            cur = c % 2

            tred_pred = (sV, sV.n + 1)

            def _(s, c=c):
                wt(s, tok_tqkv[c], ones_tok)
                if c >= 2:
                    wt(s, tok_aoev[c - 2][-1], tok_exp[c - 2][1])
                s.activation(qn[:, :], pA[:, :], ACT.Square)
                i = s.activation(kn[:, :], pB[:, :], ACT.Square)
                tq = sS.sig(i)
                i = s.copy(vbf[cur][:, :, 0:64],
                           pC[:, :].rearrange("p (h d) -> p h d", h=H))
                tv = sS.sig(i)
                wt(s, tred_pred)
                s.activation(nrm[:, 2, :], nrm[:, 0, :], ACT.Ln)
                i = s.activation(nrm[:, 3, :], nrm[:, 1, :], ACT.Ln)
                wt(s, sS.sig(i))
                s.activation(nrm[:, 4, :], nrm[:, 2, :], ACT.Exp, scale=-0.5)
                i = s.activation(nrm[:, 5, :], nrm[:, 3, :], ACT.Exp, scale=-0.5)
                return tq, tv, sS.sig(i)
            tsq, tok_tvev[c], tsqrt = emit(block.scalar, _)

            def _(v, tsq=tsq):
                wt(v, tsq)
                v.tensor_reduce(nrm[:, 0, :], qn[:, :].rearrange("p (h d) -> p h d", h=H),
                                axis=AX.X, op=OP.add)
                i = v.tensor_reduce(nrm[:, 1, :], kn[:, :].rearrange("p (h d) -> p h d", h=H),
                                    axis=AX.X, op=OP.add)
                return sV.sig(i)
            tred = emit(block.vector, _)
            assert tred == tred_pred, (tred, tred_pred)

            def _(v, c=c, tsqrt=tsqrt):
                wt(v, tsqrt)
                if c >= 2:
                    wt(v, tok_exp[c - 2][1])
                v.tensor_scalar(nrm[:, 4, :], nrm[:, 4, :], 1e6, None, op0=OP.min)
                i = v.tensor_scalar(nrm[:, 5, :], nrm[:, 5, :], 1e6, None, op0=OP.min)
                wt(v, sV.sig(i))
                v.tensor_tensor(qn[:, :].rearrange("p (h d) -> p h d", h=H),
                                pA[:, :].rearrange("p (h d) -> p h d", h=H),
                                nrm[:, 4, :].unsqueeze(2).broadcast_to([128, H, HD]),
                                op=OP.mult)
                i = v.tensor_tensor(kn[:, :].rearrange("p (h d) -> p h d", h=H),
                                    pB[:, :].rearrange("p (h d) -> p h d", h=H),
                                    nrm[:, 5, :].unsqueeze(2).broadcast_to([128, H, HD]),
                                    op=OP.mult)
                return sV.sig(i)
            tok_tnrm[c] = emit(block.vector, _)

        def l0a_qktr(c):
            cur = c % 2
            qksrc = ([qn[:, k * 128:(k + 1) * 128] for k in range(8)] +
                     [kn[:, k * 128:(k + 1) * 128] for k in range(8)])
            qkdst = ([qT[cur][:, 2 * k, :] for k in range(8)] +
                     [kT[cur][:, k, :] for k in range(8)])
            tok_qkev[c] = emit_transposes(qksrc, pA, qkdst,
                                          pre_toks=(tok_tnrm[c],),
                                          qtz_split=qT[cur])

        def l0a_aoT(c):
            ttoks = [None] * 8
            base = sS.n
            etoks = [(sS, base + k + 1) for k in range(8)]

            def _(t, c=c):
                for k in range(8):
                    if k == 0:
                        wt(t, tok_pve[c], eps_tok)
                    if k >= 2:
                        wt(t, etoks[k - 2])
                    i = t.matmul(pD[:, tr_slot(k):tr_slot(k) + 128],
                                 PT[:, k, 1, :], identb[:, :],
                                 start=True, stop=True)
                    ttoks[k] = sT.sig(i)
            block.tensor(_)

            def _(s):
                for k in range(8):
                    wt(s, ttoks[k])
                    sS.sig(s.copy(PT[:, k, 0, :],
                                  pD[:, tr_slot(k):tr_slot(k) + 128]))
                assert sS.n == base + 8
            block.scalar(_)
            tok_aoev[c] = etoks

        def l0a_x1ev(c):
            x1src = [X[:, c, k * 128:(k + 1) * 128] for k in range(8)]
            x1dst = [XNT[:, k, c * 128:(c + 1) * 128] for k in range(8)]
            tok_x1ev[c] = emit_transposes(x1src, pD, x1dst,
                                          pre_toks=(tok_tln[c],
                                                    tok_aoev[c + 1][-1]),
                                          evac="scalar")

        def l0a_wo_ln(c):
            def _(t, c=c):
                wt(t, wcast_tok["o"], tok_aoev[c][-1])
                if c >= 1 and tok_x1ev[c - 1] is not None:
                    wt(t, tok_x1ev[c - 1][-1])
                for k in range(8):
                    stt, sp = (k == 0), (k == 7)
                    t.matmul(pD[:, 0:512], PT[:, k, 0, :], l0w["o"][:, k, 0:512], start=stt, stop=sp)
                    i = t.matmul(pD[:, 512:1024], PT[:, k, 0, :], l0w["o"][:, k, 512:1024], start=stt, stop=sp)
                return sT.sig(i)
            twoT = emit(block.tensor, _)

            def _(v, c=c, twoT=twoT):
                wt(v, twoT, xin_tok[c])
                return sV.sig(v.tensor_add(X[:, c, :], X[:, c, :], pD[:, :]))
            tok_tres[c] = tres = emit(block.vector, _)

            tvar_pred = (sV, sV.n + 3)

            def _(s, c=c, tres=tres):
                wt(s, tres, eps_tok)
                junk = XNT[:, :, c * 128:(c + 1) * 128]
                xin3 = X[:, c, :].rearrange("p (a t) -> p a t", a=8)
                i = s.activation(junk, xin3, ACT.Copy,
                                 accum_out=lnm[:, 0:1])
                wt(s, sS.sig(i))
                tstt = sS.sig(s.activation(junk, xin3, ACT.Square,
                                           accum_out=lnm[:, 1:2]))
                wt(s, tvar_pred)
                i = s.activation(lnm[:, 4:5], lnm[:, 3:4], ACT.Ln,
                                 bias=cst_sm[:, 0:1])
                wt(s, sS.sig(i))
                return tstt, sS.sig(s.activation(lnm[:, 5:6], lnm[:, 4:5],
                                                 ACT.Exp, scale=-0.5))
            tst, tsd = emit(block.scalar, _)

            def _(v, tst=tst):
                wt(v, tst)
                v.tensor_scalar(lnm[:, 2:3], lnm[:, 0:1], 1.0 / D, None, op0=OP.mult)
                i = v.tensor_scalar(lnm[:, 3:4], lnm[:, 1:2], 1.0 / D, None, op0=OP.mult)
                wt(v, sV.sig(i))
                i = v.tensor_tensor(lnm[:, 4:5], lnm[:, 2:3], lnm[:, 2:3], op=OP.mult)
                wt(v, sV.sig(i))
                return sV.sig(v.tensor_sub(lnm[:, 3:4], lnm[:, 3:4], lnm[:, 4:5]))
            tvar = emit(block.vector, _)
            assert tvar == tvar_pred, (tvar, tvar_pred)

            def _(v, c=c, tsd=tsd):
                wt(v, tsd)
                i = v.tensor_scalar(lnm[:, 6:7], lnm[:, 2:3], -1.0, None, op0=OP.mult)
                wt(v, sV.sig(i))
                return sV.sig(v.tensor_scalar(X[:, c, :], X[:, c, :],
                                              lnm[:, 6:7], lnm[:, 5:6],
                                              op0=OP.add, op1=OP.mult))
            tok_tln[c] = emit(block.vector, _)

        def buf(hh):
            b = kn if hh < 4 else qn
            return b[:, (hh % 4) * 256:(hh % 4) * 256 + 256]

        def l0a_scores(qc, g, part, last):
            qcur, qnxt = qc % 2, (qc + 1) % 2

            def ps_of(g, hh):
                if g == 0:
                    return (pB if hh < 4 else pC)
                return (pA if hh < 4 else pB)

            def _(t, qc=qc, g=g, part=part, last=last):
                kc = qc if last else qc + 1
                for tk in tok_qkev[kc]:
                    wt(t, tk)
                wt(t, tok_tnrm[kc], tok_tvev[kc])
                if g == 1 and part == "b":
                    wt(t, tok_tsm[qc][0])
                rng = range(4) if g == 0 or part == "a" else range(4, 8)
                if g == 0:
                    rng = range(8)
                i = None
                for hh in rng:
                    h = g * 8 + hh
                    tl = h // 2
                    ps = ps_of(g, hh)
                    base = (hh % 4) * 256
                    i = t.matmul(ps[:, base + 128:base + 256],
                                 kT[qcur][:, tl, :],
                                 qT[qcur][:, h, :], start=True, stop=True)
                    if not last:
                        i = t.matmul(ps[:, base:base + 128],
                                     kT[qnxt][:, tl, :],
                                     qT[qcur][:, h, :], start=True, stop=True)
                return sT.sig(i)
            return emit(block.tensor, _)

        def l0a_bias(qc, g, sc_toks, last):
            def _(v, qc=qc, g=g, last=last):
                wt(v, bias_tok, *sc_toks)
                if g == 1:
                    wt(v, tok_exp[qc][0])
                elif qc >= 1:
                    wt(v, tok_exp[qc - 1][1])
                i = None
                if last:
                    for hh in range(8):
                        v.memset(buf(hh)[:, 0:128], NEG)
                    for hh in range(8):
                        h = g * 8 + hh
                        ps = (pB if hh < 4 else pC) if g == 0 else (pA if hh < 4 else pB)
                        base = (hh % 4) * 256
                        i = v.tensor_tensor(
                            buf(hh)[:, 128:256],
                            ps[:, base + 128:base + 256],
                            biasT[:, h, 1, :],
                            op=OP.add)
                else:
                    for hh in range(8):
                        h = g * 8 + hh
                        ps = (pB if hh < 4 else pC) if g == 0 else (pA if hh < 4 else pB)
                        base = (hh % 4) * 256
                        i = v.tensor_tensor(
                            buf(hh),
                            ps[:, base:base + 256],
                            biasT[:, h, :, :].rearrange("p r t -> p (r t)"),
                            op=OP.add)
                return sV.sig(i)
            tok_tsm[qc][g] = emit(block.vector, _)
            return tok_tsm[qc][g]

        def l0a_exp_merged(qc):
            tsm1_pred = (sV, sV.n + 1)
            pv0_pred = (sT, sT.n + 2)

            def _(s, qc=qc):
                wt(s, tok_tsm[qc][0])
                s.activation(
                    PT[:, 0:4, :, :].rearrange("p h r t -> p (h r t)"),
                    kn[:, :], ACT.Exp)
                e0 = sS.sig(s.activation(
                    PT[:, 4:8, :, :].rearrange("p h r t -> p (h r t)"),
                    qn[:, :], ACT.Exp))
                wt(s, tsm1_pred, pv0_pred)
                s.activation(
                    PT[:, 0:4, :, :].rearrange("p h r t -> p (h r t)"),
                    kn[:, :], ACT.Exp)
                e1 = sS.sig(s.activation(
                    PT[:, 4:8, :, :].rearrange("p h r t -> p (h r t)"),
                    qn[:, :], ACT.Exp))
                return e0, e1
            tok_exp[qc][0], tok_exp[qc][1] = emit(block.scalar, _)
            return tsm1_pred, pv0_pred

        def l0a_pv(qc, g, last):
            qcur, qnxt = qc % 2, (qc + 1) % 2

            def _(t, qc=qc, g=g, last=last):
                wt(t, tok_exp[qc][g])
                if g == 0:
                    if qc >= 2 and tok_x1ev[qc - 2] is not None:
                        wt(t, tok_x1ev[qc - 2][-1])
                    wt(t, tok_tres[qc - 1] if qc >= 1 else None)
                else:
                    wt(t, tok_tsm[qc][0])
                i = None
                for hh in range(8):
                    h = g * 8 + hh
                    ps = pD if g == 0 else pC
                    col = (hh // 4) * 512 + (hh % 4) * 66
                    i = t.matmul(ps[:, col:col + 65], PT[:, hh, 1, :],
                                 vbf[qcur][:, h, :], start=True, stop=last)
                    if not last:
                        i = t.matmul(ps[:, col:col + 65], PT[:, hh, 0, :],
                                     vbf[qnxt][:, h, :], start=False, stop=True)
                return sT.sig(i)
            tok_pvT[qc][g] = emit(block.tensor, _)
            return tok_pvT[qc][g]

        def l0a_pvevac(qc):
            def _(v, qc=qc):
                wt(v, tok_pvT[qc][0], tok_pvT[qc][1])
                i = None
                for g4 in range(4):
                    ps = pD if g4 < 2 else pC
                    col = (g4 % 2) * 512
                    i = v.tensor_copy(nrm[:, 6, g4 * 4:(g4 + 1) * 4],
                                      ps[:, col:col + 264].rearrange(
                                          "p (h e) -> p h e", h=4)[:, :, 64])
                wt(v, sV.sig(i))
                i = v.reciprocal(nrm[:, 7, :], nrm[:, 6, :])
                wt(v, sV.sig(i))
                i = None
                for tile in range(8):
                    ps = pD if tile < 4 else pC
                    col = (tile // 4 % 2) * 0 + ((tile % 4) // 2) * 512 + (tile % 2) * 132
                    i = v.tensor_tensor(
                        PT[:, tile, 1, :].rearrange("p (h d) -> p h d", h=2),
                        ps[:, col:col + 132].rearrange("p (h e) -> p h e", h=2)[:, :, 0:64],
                        nrm[:, 7, tile * 2:tile * 2 + 2].unsqueeze(2).broadcast_to([128, 2, HD]),
                        op=OP.mult)
                return sV.sig(i)
            tok_pve[qc] = emit(block.vector, _)

        def l0a_att_head(qc, last):
            t0 = l0a_scores(qc, 0, None, last)
            l0a_bias(qc, 0, (t0,), last)
            tsm1_pred, pv0_pred = l0a_exp_merged(qc)
            t1a = l0a_scores(qc, 1, "a", last)
            pv0 = l0a_pv(qc, 0, last)
            assert pv0 == pv0_pred, (pv0, pv0_pred)
            t1b = l0a_scores(qc, 1, "b", last)
            tsm1 = l0a_bias(qc, 1, (t1a, t1b), last)
            assert tsm1 == tsm1_pred, (tsm1, tsm1_pred)
            l0a_pv(qc, 1, last)
            l0a_pvevac(qc)

        if "l0a" in phases:
            for i in range(NCH):
                xld = l0a_xload(i)
                l0a_qkv(i, xld)
                if i >= 2:
                    l0a_aoT(i - 2)
                l0a_norm(i)
                if i >= 3:
                    l0a_x1ev(i - 3)
                if i >= 2:
                    l0a_wo_ln(i - 2)
                l0a_qktr(i)
                if i >= 1:
                    l0a_att_head(i - 1, last=False)
            l0a_aoT(NCH - 2)
            if NCH >= 3:
                l0a_x1ev(NCH - 3)
            l0a_wo_ln(NCH - 2)
            l0a_att_head(NCH - 1, last=True)
            l0a_aoT(NCH - 1)
            l0a_x1ev(NCH - 2)
            l0a_wo_ln(NCH - 1)
            x1src = [X[:, NCH - 1, k * 128:(k + 1) * 128] for k in range(8)]
            x1dst = [XNT[:, k, (NCH - 1) * 128:NCH * 128] for k in range(8)]
            tok_x1ev[NCH - 1] = emit_transposes(
                x1src, pD, x1dst,
                pre_toks=(tok_tln[NCH - 1], tok_tres[NCH - 1]))
        l0actx.close()
        bar_l0a = [(sT, sT.n), (sV, sV.n), (sS, sS.n)]

        def emit_ln_chunk(c, pre_toks, ptile, skip_t=False):
            def _(s):
                wt(s, *pre_toks)
                s.activation(kn[:, :], X[:, c, :], ACT.Copy,
                             accum_out=lnm[:, 0:1])
                return sS.sig(s.activation(qn[:, :], X[:, c, :], ACT.Square,
                                           accum_out=lnm[:, 1:2]))
            tst = emit(block.scalar, _)

            def _(v):
                wt(v, tst)
                v.tensor_scalar(lnm[:, 2:3], lnm[:, 0:1], 1.0 / D, None, op0=OP.mult)
                i = v.tensor_scalar(lnm[:, 3:4], lnm[:, 1:2], 1.0 / D, None, op0=OP.mult)
                wt(v, sV.sig(i))
                i = v.tensor_tensor(lnm[:, 4:5], lnm[:, 2:3], lnm[:, 2:3], op=OP.mult)
                wt(v, sV.sig(i))
                return sV.sig(v.tensor_sub(lnm[:, 3:4], lnm[:, 3:4], lnm[:, 4:5]))
            tvar = emit(block.vector, _)

            def _(s):
                wt(s, tvar)
                i = s.activation(lnm[:, 4:5], lnm[:, 3:4], ACT.Ln,
                                 bias=cst_sm[:, 0:1])
                wt(s, sS.sig(i))
                return sS.sig(s.activation(lnm[:, 5:6], lnm[:, 4:5], ACT.Exp,
                                           scale=-0.5))
            tsd = emit(block.scalar, _)

            def _(v):
                wt(v, tsd)
                i = v.tensor_scalar(lnm[:, 6:7], lnm[:, 2:3], -1.0, None, op0=OP.mult)
                wt(v, sV.sig(i))
                return sV.sig(v.tensor_scalar(X[:, c, :], X[:, c, :],
                                              lnm[:, 6:7], lnm[:, 5:6],
                                              op0=OP.add, op1=OP.mult))
            tln = emit(block.vector, _)

            if skip_t:
                return tln, [tln]
            xsrc = [X[:, c, k * 128:(k + 1) * 128] for k in range(8)]
            xdst = [XNT[:, k, c * 128:(c + 1) * 128] for k in range(8)]
            xev = emit_transposes(xsrc, ptile, xdst, pre_toks=(tln,))
            return tln, xev

        if "l0f" in phases:
            l0fctx = contextlib.ExitStack()
            EF = l0fctx.enter_context
            gateT = EF(nc.sbuf_tensor("gateT", [128, 8, NEXP], BF16))
            gstg = EF(nc.sbuf_tensor("gstg", [128, 8, NEXP], F32))
            scores = EF(nc.sbuf_tensor("scores", [128, NCH, NEXP], F32))
            Lall = EF(nc.sbuf_tensor("Lall", [128, NCH, NEXP], F32))
            Lwk = EF(nc.sbuf_tensor("Lwk", [128, NCH, NEXP], F32))
            rmax = EF(nc.sbuf_tensor("rmax", [128, 4, NCH], F32))
            mw1 = [EF(nc.sbuf_tensor(f"mw1_{i}", [128, 8, 512], BF16)) for i in range(2)]
            mw3 = [EF(nc.sbuf_tensor(f"mw3_{i}", [128, 8, 512], BF16)) for i in range(2)]
            mw2 = [EF(nc.sbuf_tensor(f"mw2_{i}", [128, 4, D], BF16)) for i in range(2)]
            act_bf = EF(nc.sbuf_tensor("act_bf", [128, 4, 512], BF16))
            sg_sb2 = EF(nc.sbuf_tensor("sg_sb2", [128, 4, 512], F32))

            def _(sy):
                wt(sy, *bar_l0a)
                return dW.sig(sy.dma_start(out=gstg[:, :, :],
                                           in_=gate_hbm.rearrange("(k p) e -> p k e", p=128)))
            tgl = emit(block.sync, _)

            def _(v):
                wt(v, tgl, *bar_l0a)
                return sV.sig(v.tensor_copy(gateT[:, :, :], gstg[:, :, :]))
            tgc = emit(block.vector, _)

            rt_toks = []

            def _(t):
                wt(t, tgc)
                for c in range(NCH):
                    for k in range(8):
                        i = t.matmul(pA[:, c * NEXP:(c + 1) * NEXP],
                                     XNT[:, k, c * 128:(c + 1) * 128],
                                     gateT[:, k, :], start=(k == 0), stop=(k == 7))
                    rt_toks.append(sT.sig(i))
                return rt_toks[-1]
            emit(block.tensor, _)

            def _(v):
                wt(v, rt_toks[-1])
                return sV.sig(v.tensor_copy(Lall[:, :, :],
                                            pA[:, 0:NCH * NEXP].rearrange(
                                                "p (c e) -> p c e", e=NEXP)))
            trl = emit(block.vector, _)

            def _(v):
                wt(v, trl)
                i = v.tensor_copy(Lwk[:, :, :], Lall[:, :, :])
                wt(v, sV.sig(i))
                for j in range(4):
                    i = v.tensor_reduce(rmax[:, j, :], Lwk[:, :, :], axis=AX.X,
                                        op=OP.max)
                    wt(v, sV.sig(i))
                    if j < 3:
                        i = v.tensor_tensor(qn[:, 0:NCH * NEXP].rearrange(
                                                "p (c e) -> p c e", e=NEXP),
                                            Lwk[:, :, :],
                                            rmax[:, j, :].unsqueeze(2).broadcast_to(
                                                [128, NCH, NEXP]),
                                            op=OP.is_ge)
                        wt(v, sV.sig(i))
                        i = v.scalar_tensor_tensor(Lwk[:, :, :],
                                                   qn[:, 0:NCH * NEXP].rearrange(
                                                       "p (c e) -> p c e", e=NEXP),
                                                   -1e30, Lwk[:, :, :],
                                                   op0=OP.mult, op1=OP.add)
                        wt(v, sV.sig(i))
                i = v.tensor_tensor(kn[:, 0:NCH * NEXP].rearrange(
                                        "p (c e) -> p c e", e=NEXP),
                                    Lall[:, :, :],
                                    rmax[:, 3, :].unsqueeze(2).broadcast_to(
                                        [128, NCH, NEXP]),
                                    op=OP.is_ge)
                i = v.tensor_scalar(rmax[:, 1, :], rmax[:, 0, :], -1.0, None,
                                    op0=OP.mult)
                return sV.sig(i)
            ttk = emit(block.vector, _)

            def _(v):
                wt(v, ttk)
                i = v.tensor_tensor(Lwk[:, :, :], Lall[:, :, :],
                                    rmax[:, 0, :].unsqueeze(2).broadcast_to(
                                        [128, NCH, NEXP]),
                                    op=OP.subtract)
                return sV.sig(i)
            tsub = emit(block.vector, _)

            def _(s):
                wt(s, tsub)
                return sS.sig(s.activation(Lwk[:, :, :], Lwk[:, :, :], ACT.Exp))
            texp = emit(block.scalar, _)

            def _(v):
                wt(v, texp)
                i = v.tensor_mul(Lwk[:, :, :], Lwk[:, :, :],
                                 kn[:, 0:NCH * NEXP].rearrange(
                                     "p (c e) -> p c e", e=NEXP))
                wt(v, sV.sig(i))
                i = v.tensor_reduce(rmax[:, 2, :], Lwk[:, :, :], axis=AX.X,
                                    op=OP.add)
                wt(v, sV.sig(i))
                i = v.reciprocal(rmax[:, 3, :], rmax[:, 2, :])
                wt(v, sV.sig(i))
                i = v.tensor_tensor(scores[:, :, :], Lwk[:, :, :],
                                    rmax[:, 3, :].unsqueeze(2).broadcast_to(
                                        [128, NCH, NEXP]),
                                    op=OP.mult)
                return sV.sig(i)
            tsc_done = emit(block.vector, _)

            wbuf_free = [None, None]
            cast_tok = [None, None]
            acc_tok = [None] * NCH
            prev_dma = tgl
            act_ev = [None, None]
            dn_ev = [None, None]

            for e in range(NEXP):
                b = e % 2

                def _(g, e=e, b=b):
                    wt(g, wbuf_free[b])
                    if e < 2:
                        wt(g, *bar_l0a)
                    for k in range(8):
                        dG.sig(g.dma_start(out=mw1[b][:, k, :],
                                           in_=moe_w1_hbm[e, k * 128:(k + 1) * 128, :]))
                        dG.throttle(g)
                    for k in range(8):
                        dG.sig(g.dma_start(out=mw3[b][:, k, :],
                                           in_=moe_w3_hbm[e, k * 128:(k + 1) * 128, :]))
                        dG.throttle(g)
                    for k in range(4):
                        dG.sig(g.dma_start(out=mw2[b][:, k, :],
                                           in_=moe_w2_hbm[e, k * 128:(k + 1) * 128, :]))
                        dG.throttle(g)
                    return dG.tok()
                cast_tok[b] = emit(block.gpsimd, _)

                last_dn_tok = None
                TG = 512 if T >= 512 else T
                SUBS = TG // 128
                GPS = (pA, pA, pC, pC)
                UPS = (pB, pB, pD, pD)
                for g in range(T // TG):
                    def _(t, g=g, e=e):
                        wt(t, cast_tok[b], tsc_done)
                        if e == 0 and g == 0:
                            wt(t, trl)
                        wt(t, act_ev[0], act_ev[1])
                        wt(t, dn_ev[0], dn_ev[1])
                        i = None
                        for it in range(4):
                            gp, up = GPS[it], UPS[it]
                            col = (it % 2) * 512
                            for k in range(8):
                                st, sp = (k == 0), (k == 7)
                                t.matmul(gp[:, col:col + TG],
                                         mw1[b][:, k, it * 128:(it + 1) * 128],
                                         XNT[:, k, g * TG:(g + 1) * TG],
                                         start=st, stop=sp)
                                i = t.matmul(up[:, col:col + TG],
                                             mw3[b][:, k, it * 128:(it + 1) * 128],
                                             XNT[:, k, g * TG:(g + 1) * TG],
                                             start=st, stop=sp)
                        return sT.sig(i)
                    tup = emit(block.tensor, _)

                    sg_toks = []

                    def _(s, tup=tup):
                        wt(s, tup)
                        for it in range(4):
                            gp = GPS[it]
                            col = (it % 2) * 512
                            sg_toks.append(sS.sig(s.activation(
                                sg_sb[:, it % 2, 0:TG] if False else sg_sb2[:, it, 0:TG],
                                gp[:, col:col + TG], ACT.Silu)))
                        return None
                    emit(block.scalar, _)

                    def _(v):
                        i = None
                        for it in range(4):
                            up = UPS[it]
                            col = (it % 2) * 512
                            wt(v, sg_toks[it])
                            i = v.tensor_mul(act_bf[:, it, 0:TG],
                                             sg_sb2[:, it, 0:TG],
                                             up[:, col:col + TG])
                        return sV.sig(i)
                    aev = emit(block.vector, _)
                    act_ev[0] = act_ev[1] = aev

                    for sub in range(SUBS):
                        ch = g * SUBS + sub
                        ps = pC if sub % 2 == 0 else pD
                        def _(t, sub=sub, g=g, ps=ps):
                            wt(t, act_ev[0])
                            wt(t, dn_ev[sub % 2])
                            i = None
                            for it in range(4):
                                st, sp = (it == 0), (it == 3)
                                t.matmul(ps[:, 0:512],
                                         act_bf[:, it, sub * 128:(sub + 1) * 128],
                                         mw2[b][:, it, 0:512], start=st, stop=sp)
                                i = t.matmul(ps[:, 512:1024],
                                             act_bf[:, it, sub * 128:(sub + 1) * 128],
                                             mw2[b][:, it, 512:1024], start=st, stop=sp)
                            return sT.sig(i)
                        tdn = emit(block.tensor, _)
                        last_dn_tok = tdn

                        def _(v, ch=ch, e=e, tdn=tdn, ps=ps):
                            wt(v, tdn, acc_tok[ch], tsc_done)
                            return sV.sig(v.scalar_tensor_tensor(
                                X[:, ch, :], ps[:, :], scores[:, ch, e:e + 1],
                                X[:, ch, :], op0=OP.mult, op1=OP.add))
                        acc_tok[ch] = emit(block.vector, _)
                        dn_ev[sub % 2] = acc_tok[ch]
                wbuf_free[b] = last_dn_tok

            for c in range(NCH):
                pre = [acc_tok[c]]
                if c == 0:
                    pre += [last_dn_tok]
                emit_ln_chunk(c, pre, pB if c % 2 else pA)
            l0fctx.close()

        bar_l0f = [(sT, sT.n), (sV, sV.n), (sS, sS.n)]

        if "l1a" in phases:
            S1 = float(np.float32(64.0 ** (-0.25)))
            S2 = float(np.float32(np.float64(64.0 ** (-0.25)) / np.float64(S1)))
            LN1E4 = np.float64(np.log(10000.0)) / 32.0
            E1 = float(np.float32(-LN1E4))
            E2 = float(np.float32(-LN1E4 - np.float64(np.float32(-LN1E4))))
            TWO_PI = 2.0 * np.pi
            C1 = float(np.float32(TWO_PI))
            C2 = float(np.float32(TWO_PI - np.float64(np.float32(TWO_PI))))
            INV2PI = float(np.float32(1.0 / TWO_PI))
            PI_LO = float(np.float32(3.14159250))
            PI2_HI = float(np.float32(np.pi / 2))
            PI2_LO = float(np.float32(np.pi / 2 - np.float64(np.float32(np.pi / 2))))

            l1ctx = contextlib.ExitStack()
            EL = l1ctx.enter_context
            wqkv = EL(nc.sbuf_tensor("wqkv", [128, 8, 3 * D], BF16))
            wout = EL(nc.sbuf_tensor("wout", [128, 8, D], BF16))

            cost = EL(nc.sbuf_tensor("cost", [128, NCH, 32], F32))
            sint = EL(nc.sbuf_tensor("sint", [128, NCH, 32], F32))
            invf = EL(nc.sbuf_tensor("invf", [128, 32], F32))
            pi2t = EL(nc.sbuf_tensor("pi2t", [128, 1], F32))
            UT = EL(nc.sbuf_tensor("UT", [128, 128], F32))
            xTl = EL(nc.sbuf_tensor("xTl", [128, 8, 128], BF16))
            vaug = EL(nc.sbuf_tensor("vaug", [128, H, 65], BF16))
            kTz = EL(nc.sbuf_tensor("kTz", [128, H, 128], BF16))
            qTp = EL(nc.sbuf_tensor("qTp", [128, 8, 128], BF16))
            ktk = EL(nc.sbuf_tensor("ktk", [128, H, 64], BF16))
            PTl = EL(nc.sbuf_tensor("PTl", [128, 8, 128], BF16))
            KVf = EL(nc.sbuf_tensor("KVf", [128, H, 65], F32))
            KVb = EL(nc.sbuf_tensor("KVb", [128, H, 65], BF16))
            rtmp = EL(nc.sbuf_tensor("rtmp", [128, 3, 512], F32))
            rrep = EL(nc.sbuf_tensor("rrep", [128, D], F32))
            trctx = contextlib.ExitStack()
            trig2 = trctx.enter_context(nc.sbuf_tensor("trig2", [128, NCH, 32], F32))
            rni = trctx.enter_context(nc.sbuf_tensor("rni", [128, NCH, 16], mybir.dt.int32))

            l1w_cast = [None]

            def _(g):
                wt(g, *bar_l0f)
                for q in range(8):
                    for j in range(3):
                        dG.sig(g.dma_start(
                            out=wqkv[:, q, j * 1024:(j + 1) * 1024],
                            in_=l1wqkv_hbm[q][:, j * 1024:(j + 1) * 1024]))
                        dG.throttle(g)
                for q in range(8):
                    dG.sig(g.dma_start(out=wout[:, q, :], in_=l1wout_hbm[q]))
                    dG.throttle(g)
                return dG.tok()
            l1w_cast[0] = emit(block.gpsimd, _)

            def _(sy):
                wt(sy, *bar_l0f)
                dC.sig(sy.dma_start(out=rrep[:, :], in_=rrep_hbm))
                return dC.sig(sy.dma_start(out=UT[:, :], in_=ut_hbm))
            ut_tok = emit(block.sync, _)

            def _(g):
                wt(g, *bar_l0f)
                g.iota(cost[:, :, 0:1], pattern=[[128, NCH]], base=0,
                       channel_multiplier=1,
                       allow_small_or_imprecise_dtypes=True)
                return sG.sig(g.iota(invf[0:1, :], pattern=[[1, 32]], base=0,
                                     channel_multiplier=0,
                                     allow_small_or_imprecise_dtypes=True))
            iota_tok = emit(block.gpsimd, _)

            def _(s):
                wt(s, iota_tok)
                i = s.activation(trig2[0:1, 0, 0:32], invf[0:1, :], ACT.Exp, scale=E1)
                wt(s, sS.sig(i))
                return sS.sig(s.activation(invf[0:1, :], invf[0:1, :], ACT.Exp,
                                           scale=E2))
            tinv1 = emit(block.scalar, _)

            def _(v):
                wt(v, tinv1)
                i = v.tensor_mul(invf[0:1, :], invf[0:1, :], trig2[0:1, 0, 0:32])
                wt(v, sV.sig(i))
                i = v.memset(pi2t[:, :], PI2_HI)
                wt(v, sV.sig(i))
                return sV.sig(v.tensor_scalar(pi2t[:, :], pi2t[:, :], PI2_LO,
                                              None, op0=OP.add))
            tinv2 = emit(block.vector, _)

            def _(t):
                wt(t, tinv2, ut_tok)
                return sT.sig(t.matmul(pA[:, 0:32], UT[0:1, :],
                                       invf[0:1, :], start=True, stop=True))
            tbmm = emit(block.tensor, _)

            def _(v):
                wt(v, tbmm)
                return sV.sig(v.tensor_copy(invf[:, :], pA[:, 0:32]))
            tbc = emit(block.vector, _)

            def _(v):
                wt(v, tbc, iota_tok)
                i = v.tensor_tensor(trig2[:, :, :],
                                    cost[:, :, 0:1].broadcast_to([128, NCH, 32]),
                                    invf[:, :].unsqueeze(1).broadcast_to([128, NCH, 32]),
                                    op=OP.mult)
                wt(v, sV.sig(i))
                i = v.tensor_scalar(cost[:, :, :], trig2[:, :, :], INV2PI, 0.5,
                                    op0=OP.mult, op1=OP.add)
                return sV.sig(i)
            tph = emit(block.vector, _)

            def _(v):
                wt(v, tph)
                for h in range(2):
                    i = v.tensor_copy(rni[:, :, :], cost[:, :, h * 16:h * 16 + 16])
                    wt(v, sV.sig(i))
                    i = v.tensor_copy(cost[:, :, h * 16:h * 16 + 16], rni[:, :, :])
                    wt(v, sV.sig(i))
                i = v.scalar_tensor_tensor(trig2[:, :, :], cost[:, :, :], -C1,
                                           trig2[:, :, :], op0=OP.mult, op1=OP.add)
                wt(v, sV.sig(i))
                i = v.scalar_tensor_tensor(trig2[:, :, :], cost[:, :, :], -C2,
                                           trig2[:, :, :], op0=OP.mult, op1=OP.add)
                wt(v, sV.sig(i))
                i = v.tensor_scalar(cost[:, :, :], trig2[:, :, :], PI_LO, None,
                                    op0=OP.is_gt)
                wt(v, sV.sig(i))
                i = v.scalar_tensor_tensor(trig2[:, :, :], cost[:, :, :], -C1,
                                           trig2[:, :, :], op0=OP.mult, op1=OP.add)
                wt(v, sV.sig(i))
                i = v.scalar_tensor_tensor(trig2[:, :, :], cost[:, :, :], -C2,
                                           trig2[:, :, :], op0=OP.mult, op1=OP.add)
                wt(v, sV.sig(i))
                i = v.tensor_scalar(cost[:, :, :], trig2[:, :, :], -PI_LO, None,
                                    op0=OP.is_lt)
                wt(v, sV.sig(i))
                i = v.scalar_tensor_tensor(trig2[:, :, :], cost[:, :, :], C1,
                                           trig2[:, :, :], op0=OP.mult, op1=OP.add)
                wt(v, sV.sig(i))
                i = v.scalar_tensor_tensor(trig2[:, :, :], cost[:, :, :], C2,
                                           trig2[:, :, :], op0=OP.mult, op1=OP.add)
                wt(v, sV.sig(i))
                i = v.tensor_scalar(trig2[:, :, :], trig2[:, :, :], PI_LO, None,
                                    op0=OP.min)
                wt(v, sV.sig(i))
                i = v.tensor_scalar(trig2[:, :, :], trig2[:, :, :], -PI_LO, None,
                                    op0=OP.max)
                return sV.sig(i)
            tred = emit(block.vector, _)

            def _(s):
                wt(s, tred)
                s.activation(sint[:, :, :], trig2[:, :, :], ACT.Sin)
                i = s.activation(cost[:, :, :], trig2[:, :, :], ACT.Abs)
                return sS.sig(i)
            tsin = emit(block.scalar, _)

            def _(s):
                wt(s, tsin)
                return sS.sig(s.activation(cost[:, :, :], cost[:, :, :],
                                           ACT.Sin, bias=pi2t[:, :], scale=-1.0))
            tcos = emit(block.scalar, _)

            def _(v):
                wt(v, tcos)
                i = v.memset(KVf[:, :, :], 0.0)
                wt(v, sV.sig(i))
                i = v.memset(KVb[:, :, :], 0.0)
                wt(v, sV.sig(i))
                i = v.memset(vaug[:, :, 64:65], 1.0)
                wt(v, sV.sig(i))
                return sV.sig(v.memset(kTz[:, :, :], 0.0))
            init_tok = emit(block.vector, _)

            st_l1 = {"pv_evac": None, "kv_cast": init_tok, "wo": None,
                     "att_V": None, "att_S": None, "ptev": None,
                     "kvup": None}

            for c in range(NCH):
                xsrc = [X[:, c, k * 128:(k + 1) * 128] for k in range(8)]
                xdst = [xTl[:, k, :] for k in range(8)]
                xev = emit_transposes(xsrc, pD, xdst,
                                      pre_toks=(st_l1["pv_evac"], init_tok))

                def _(t, c=c):
                    wt(t, l1w_cast[0], st_l1["att_V"], st_l1["att_S"])
                    for k in range(8):
                        wt(t, xev[k])
                        st, sp = (k == 0), (k == 7)
                        for half in range(2):
                            t.matmul(pA[:, half * 512:(half + 1) * 512],
                                     xTl[:, k, :], wqkv[:, k, half * 512:(half + 1) * 512],
                                     start=st, stop=sp)
                            t.matmul(pB[:, half * 512:(half + 1) * 512],
                                     xTl[:, k, :], wqkv[:, k, D + half * 512:D + (half + 1) * 512],
                                     start=st, stop=sp)
                            i = t.matmul(pC[:, half * 512:(half + 1) * 512],
                                         xTl[:, k, :], wqkv[:, k, 2 * D + half * 512:2 * D + (half + 1) * 512],
                                         start=st, stop=sp)
                    return sT.sig(i)
                tqkv = emit(block.tensor, _)

                def _(s, c=c):
                    wt(s, tqkv)
                    return sS.sig(s.copy(vaug[:, :, 0:64],
                                         pC[:, :].rearrange("p (h d) -> p h d", h=H)))
                tvev = emit(block.scalar, _)

                rp_toks = []
                for (srcp, dst) in ((pA, qn), (pB, kn)):
                    def _(v, c=c, srcp=srcp, dst=dst):
                        wt(v, tqkv, rp_toks[-1] if rp_toks else None)
                        s3 = srcp[:, :].rearrange("p (h d) -> p h d", h=H)
                        d3 = dst[:, :].rearrange("p (h d) -> p h d", h=H)
                        cosb = cost[:, c, :].unsqueeze(1).broadcast_to([128, H, 32])
                        sinb = sint[:, c, :].unsqueeze(1).broadcast_to([128, H, 32])
                        c1t = rtmp[:, 0, :].rearrange("p (h x) -> p h x", h=H)
                        s1t = rtmp[:, 1, :].rearrange("p (h x) -> p h x", h=H)
                        t2t = rtmp[:, 2, :].rearrange("p (h x) -> p h x", h=H)
                        v.tensor_tensor(c1t, s3[:, :, 0:32], cosb, op=OP.mult)
                        v.tensor_tensor(s1t, s3[:, :, 0:32], sinb, op=OP.mult)
                        v.tensor_tensor(d3[:, :, 32:64], s3[:, :, 32:64], cosb, op=OP.mult)
                        i = v.tensor_tensor(t2t, s3[:, :, 32:64], sinb, op=OP.mult)
                        wt(v, sV.sig(i))
                        v.tensor_add(d3[:, :, 32:64], d3[:, :, 32:64], s1t)
                        i = v.tensor_sub(d3[:, :, 0:32], c1t, t2t)
                        wt(v, sV.sig(i))
                        i = v.tensor_scalar(dst[:, :], dst[:, :], S1, S2,
                                            op0=OP.mult, op1=OP.mult)
                        wt(v, sV.sig(i))
                        i = v.tensor_scalar(rtmp[:, 0:2, :].rearrange("p a x -> p (a x)"),
                                            dst[:, :], 0.0, None, op0=OP.min)
                        wt(v, sV.sig(i))
                        return sV.sig(v.tensor_scalar(dst[:, :], dst[:, :], 0.0,
                                                      None, op0=OP.max))
                    trelu = emit(block.vector, _)

                    def _(s, trelu=trelu):
                        wt(s, trelu)
                        return sS.sig(s.activation(
                            rtmp[:, 0:2, :].rearrange("p a x -> p (a x)"),
                            rtmp[:, 0:2, :].rearrange("p a x -> p (a x)"), ACT.Exp))
                    texp = emit(block.scalar, _)

                    def _(v, dst=dst, texp=texp):
                        wt(v, texp)
                        i = v.tensor_add(dst[:, :], dst[:, :],
                                         rtmp[:, 0:2, :].rearrange("p a x -> p (a x)"))
                        return sV.sig(i)
                    rp_toks.append(emit(block.vector, _))
                trp = rp_toks[-1]

                def _(v):
                    wt(v, trp, st_l1["kvup"])
                    return sV.sig(v.tensor_copy(
                        ktk[:, :, :], kn[:, :].rearrange("p (h d) -> p h d", h=H)))
                tktk = emit(block.vector, _)

                qsrc = [qn[:, k * 128:(k + 1) * 128] for k in range(8)]
                qdst = [qTp[:, k, :] for k in range(8)]
                qev = emit_transposes(qsrc, pA, qdst, pre_toks=(trp,))
                ksrc = [kn[:, k * 128:(k + 1) * 128] for k in range(8)]
                kdst = [kTz[:, 2 * k, :] for k in range(8)]
                kev = emit_transposes(ksrc, pB, kdst, pre_toks=(trp,),
                                      qtz_split=kTz)

                pv_toks = []
                for g in range(2):
                    def _(t, g=g, c=c):
                        for tk in qev + kev:
                            wt(t, tk)
                        wt(t, st_l1["ptev"], st_l1["kv_cast"], tvev)
                        if g == 1:
                            wt(t, pv_toks[0])
                        i = None
                        for hh in range(8):
                            h = g * 8 + hh
                            i = t.matmul(pA[:, hh * 128:(hh + 1) * 128],
                                         kTz[:, h, :], qTp[:, h // 2, :],
                                         start=True, stop=True)
                        return sT.sig(i)
                    tat = emit(block.tensor, _)

                    def _(v, g=g, tat=tat):
                        wt(v, tat, ut_tok)
                        i = v.tensor_tensor(
                            PTl[:, :, :],
                            pA[:, 0:1024].rearrange("p (a b) -> p a b", a=8),
                            UT[:, :].unsqueeze(1).broadcast_to([128, 8, 128]),
                            op=OP.mult)
                        return sV.sig(i)
                    tmask = emit(block.vector, _)
                    st_l1["ptev"] = tmask

                    def _(t, g=g, c=c, tmask=tmask):
                        wt(t, tmask)
                        ps = pD if g == 0 else pC
                        i = None
                        for hh in range(8):
                            h = g * 8 + hh
                            col = (hh // 4) * 512 + (hh % 4) * 66
                            t.matmul(ps[:, col:col + 65], PTl[:, hh, :],
                                     vaug[:, h, :], start=True, stop=False)
                            i = t.matmul(ps[:, col:col + 65], qTp[:, h // 2, :],
                                         KVb[:, h, :], start=False, stop=True)
                        return sT.sig(i)
                    pv_toks.append(emit(block.tensor, _))

                def _(t, c=c):
                    wt(t, tktk, pv_toks[0], pv_toks[1])
                    for tk in kev:
                        wt(t, tk)
                    i = None
                    for h in range(H):
                        sl = h // 2
                        po = (h % 2) * 64
                        col = (sl // 4) * 512 + (sl % 4) * 66
                        i = t.matmul(pB[po:po + 64, col:col + 65],
                                     ktk[:, h, :], vaug[:, h, :],
                                     start=True, stop=True,
                                     tile_position=(0, po))
                    return sT.sig(i)
                tkv = emit(block.tensor, _)
                st_l1["kvup"] = tkv

                def _(v, tkv=tkv):
                    wt(v, tkv, st_l1["kv_cast"])
                    i = None
                    for half in range(2):
                        for po in range(2):
                            hs = half * 8 + po
                            he = min(hs + 8, H)
                            i = v.tensor_add(
                                KVf[po * 64:po * 64 + 64, hs:he:2, :],
                                KVf[po * 64:po * 64 + 64, hs:he:2, :],
                                pB[po * 64:po * 64 + 64,
                                   half * 512:half * 512 + 264].rearrange(
                                    "p (s e) -> p s e", e=66)[:, :, 0:65])
                    wt(v, sV.sig(i))
                    return sV.sig(v.tensor_copy(KVb[:, :, :], KVf[:, :, :]))
                st_l1["kv_cast"] = emit(block.vector, _)

                def _(v):
                    wt(v, pv_toks[0], pv_toks[1])
                    i = None
                    for g4 in range(4):
                        ps = pD if g4 < 2 else pC
                        col = (g4 % 2) * 512
                        i = v.tensor_copy(nrm[:, 6, g4 * 4:(g4 + 1) * 4],
                                          ps[:, col:col + 264].rearrange(
                                              "p (h e) -> p h e", h=4)[:, :, 64])
                    wt(v, sV.sig(i))
                    i = v.tensor_scalar(nrm[:, 6, :], nrm[:, 6, :], 1e-6, None,
                                        op0=OP.max)
                    wt(v, sV.sig(i))
                    i = v.reciprocal(nrm[:, 7, :], nrm[:, 6, :])
                    wt(v, sV.sig(i))
                    i = None
                    for g4 in range(4):
                        ps = pD if g4 < 2 else pC
                        col = (g4 % 2) * 512
                        i = v.tensor_tensor(
                            qn[:, :].rearrange("p (h d) -> p h d", h=H)[:, g4 * 4:(g4 + 1) * 4, :],
                            ps[:, col:col + 264].rearrange("p (h e) -> p h e", h=4)[:, :, 0:64],
                            nrm[:, 7, g4 * 4:(g4 + 1) * 4].unsqueeze(2).broadcast_to([128, 4, HD]),
                            op=OP.mult)
                    return sV.sig(i)
                tpvev = emit(block.vector, _)
                st_l1["pv_evac"] = tpvev

                aosrc = [qn[:, k * 128:(k + 1) * 128] for k in range(8)]
                aodst = [xTl[:, k, :] for k in range(8)]
                aoev = emit_transposes(aosrc, pB, aodst,
                                       pre_toks=(tpvev, st_l1["kv_cast"]),
                                       evac="scalar")

                def _(t, c=c):
                    wt(t, st_l1["wo"])
                    for k in range(8):
                        wt(t, aoev[k])
                        st, sp = (k == 0), (k == 7)
                        t.matmul(pA[:, 0:512], xTl[:, k, :], wout[:, k, 0:512],
                                 start=st, stop=sp)
                        i = t.matmul(pA[:, 512:1024], xTl[:, k, :],
                                     wout[:, k, 512:1024], start=st, stop=sp)
                    return sT.sig(i)
                twoT = emit(block.tensor, _)

                def _(v, c=c):
                    wt(v, twoT)
                    return sV.sig(v.tensor_add(X[:, c, :], X[:, c, :], pA[:, :]))
                tres = emit(block.vector, _)
                st_l1["wo"] = tres

                tln, _ = emit_ln_chunk(c, (tres,), pC, skip_t=True)

                def _(s, tln=tln):
                    wt(s, tln)
                    return sS.sig(s.activation(kn[:, :], X[:, c, :], ACT.Square,
                                               accum_out=lnm[:, 0:1]))
                tss2 = emit(block.scalar, _)

                def _(v, c=c, tss2=tss2):
                    wt(v, tss2, ut_tok)
                    i = v.tensor_mul(kn[:, :], X[:, c, :], rrep[:, :])
                    wt(v, sV.sig(i))
                    i = v.tensor_reduce(svals[:, c:c + 1], kn[:, :], axis=AX.X,
                                        op=OP.add)
                    return sV.sig(i)
                tsv2 = emit(block.vector, _)

                def _(s, tsv2=tsv2):
                    wt(s, tsv2)
                    i = s.activation(lnm[:, 1:2], lnm[:, 0:1], ACT.Ln)
                    wt(s, sS.sig(i))
                    return sS.sig(s.activation(lnm[:, 2:3], lnm[:, 1:2], ACT.Exp,
                                               scale=-0.5, bias=cst_sm[:, 1:2]))
                tn2 = emit(block.scalar, _)

                def _(v, c=c, tn2=tn2):
                    wt(v, tn2)
                    i = v.tensor_scalar(lnm[:, 2:3], lnm[:, 2:3], 3.2e13, None,
                                        op0=OP.min)
                    wt(v, sV.sig(i))
                    return sV.sig(v.tensor_scalar(qn[:, :], X[:, c, :],
                                                  lnm[:, 2:3], None, op0=OP.mult))
                txn2 = emit(block.vector, _)

                xsrc2 = [qn[:, k * 128:(k + 1) * 128] for k in range(8)]
                xdst2 = [XNT[:, k, c * 128:(c + 1) * 128] for k in range(8)]
                x3ev = emit_transposes(xsrc2, pC, xdst2, pre_toks=(txn2,))
                rms_sv_tok[c] = x3ev[-1]
                st_l1["att_V"] = x3ev[-1]
                st_l1["att_S"] = tn2
                if c == 0:
                    trctx.close()
            l1ctx.close()

        bar_l1a = [(sT, sT.n), (sV, sV.n), (sS, sS.n)]

        if "l1f" in phases:
            N_ITERS = 50
            EFF_K = min(1024.0 * 9.0 / 8.0, float(T))
            LOGK = float(np.float32(np.log(np.float64(EFF_K))))
            A1 = float(np.float32(np.log(np.float64(EFF_K)) - np.log(np.float64(T))))
            NHEAVY = min(1024, T)

            fctx = contextlib.ExitStack()
            EF = fctx.enter_context
            ONES = EF(nc.sbuf_tensor("ONES", [128, 128], F32))
            UT2 = EF(nc.sbuf_tensor("UT2", [128, 128], F32))
            lgam = EF(nc.sbuf_tensor("lgam", [128, 8], F32))
            hgam = EF(nc.sbuf_tensor("hgam", [128, 8], F32))
            lw1 = EF(nc.sbuf_tensor("lw1", [128, 8, 512], BF16))
            lw2 = EF(nc.sbuf_tensor("lw2", [128, 4, D], BF16))
            lb1 = EF(nc.sbuf_tensor("lb1", [1, 512], BF16))
            lb2 = EF(nc.sbuf_tensor("lb2", [1, D], BF16))
            hb2 = EF(nc.sbuf_tensor("hb2", [1, D], BF16))
            hw1q = [EF(nc.sbuf_tensor(f"hw1q{i}", [128, 8, 1024], BF16)) for i in range(2)]
            hw2q = [EF(nc.sbuf_tensor(f"hw2q{i}", [128, 8, 1024], BF16)) for i in range(2)]
            hb1q = EF(nc.sbuf_tensor("hb1q", [1, 1024], BF16))
            onesr = EF(nc.sbuf_tensor("onesr", [1, 512], BF16))
            hh_bf = EF(nc.sbuf_tensor("hh_bf", [128, 8, 512], BF16))
            hl_bf = hh_bf
            fmeta = EF(nc.sbuf_tensor("fmeta", [128, 8], F32))
            sat = EF(nc.sbuf_tensor("sat", [128, NCH], F32))
            hmask = EF(nc.sbuf_tensor("hmask", [128, NCH], F32))
            pcol = EF(nc.sbuf_tensor("pcol", [128, NCH], F32))

            def _(sy):
                wt(sy, *bar_l1a)
                dC.sig(sy.dma_start(out=ONES[:, :], in_=ones_hbm))
                dC.sig(sy.dma_start(out=UT2[:, :], in_=ut_hbm))
                dC.sig(sy.dma_start(out=lgam[:, :], in_=lgam_hbm))
                return dC.sig(sy.dma_start(out=hgam[:, :], in_=hgam_hbm))
            cst2 = emit(block.sync, _)

            def _(g):
                wt(g, *bar_l1a)
                for k in range(8):
                    dG.sig(g.dma_start(out=lw1[:, k, :], in_=lw1_hbm[k]))
                    dG.throttle(g)
                for k in range(4):
                    dG.sig(g.dma_start(out=lw2[:, k, :], in_=lw2_hbm[k]))
                    dG.throttle(g)
                return dG.tok()
            lwd = emit(block.gpsimd, _)

            def _(v):
                wt(v, lwd, cst2, *bar_l1a)
                i = None
                for k in range(8):
                    i = v.tensor_scalar(lw1[:, k, :], lw1[:, k, :],
                                        lgam[:, k:k + 1], None, op0=OP.mult)
                return sV.sig(i)
            ltoks = [emit(block.vector, _)]

            def _(g):
                wt(g, *bar_l1a)
                dG.sig(g.dma_start(out=lb1[0:1, :], in_=lb1_hbm))
                dG.throttle(g)
                dG.sig(g.dma_start(out=lb2[0:1, :], in_=lb2_hbm))
                dG.throttle(g)
                dG.sig(g.dma_start(out=hb2[0:1, :], in_=hb2_hbm))
                dG.throttle(g)
                return dG.tok()
            btok = emit(block.gpsimd, _)

            def _(v):
                wt(v, btok)
                return sV.sig(v.memset(onesr[0:1, :], 1.0))
            bcast_tok = emit(block.vector, _)

            rms_toks = [rms_sv_tok[NCH - 1]]
            sv_tok = rms_sv_tok[NCH - 1]

            def _(v):
                wt(v, sv_tok)
                return sV.sig(v.memset(fmeta[:, 0:1], A1))
            atok = emit(block.vector, _)

            desc_psum_free = None
            for it in range(N_ITERS - 1):
                def _(v, atok=atok):
                    wt(v, atok)
                    return sV.sig(v.tensor_scalar(kn[:, 0:NCH], svals[:, 0:NCH],
                                                  fmeta[:, 0:1], 0.0,
                                                  op0=OP.add, op1=OP.min))
                t1 = emit(block.vector, _)

                def _(s, t1=t1):
                    wt(s, t1)
                    return sS.sig(s.activation(kn[:, 0:NCH], kn[:, 0:NCH], ACT.Exp))
                t2 = emit(block.scalar, _)

                def _(t, t2=t2, it=it):
                    wt(t, t2, cst2)
                    if it == 0:
                        wt(t, rms_toks[-1])
                    return sT.sig(t.matmul(pD[:, 0:NCH], ONES[:, :], kn[:, 0:NCH],
                                           start=True, stop=True))
                t3 = emit(block.tensor, _)

                def _(v, t3=t3):
                    wt(v, t3)
                    return sV.sig(v.tensor_reduce(fmeta[:, 1:2], pD[:, 0:NCH],
                                                  axis=AX.X, op=OP.add))
                t4 = emit(block.vector, _)

                def _(s, t4=t4):
                    wt(s, t4)
                    return sS.sig(s.activation(fmeta[:, 2:3], fmeta[:, 1:2], ACT.Ln))
                t5 = emit(block.scalar, _)

                def _(v, t5=t5):
                    wt(v, t5)
                    i = v.scalar_tensor_tensor(fmeta[:, 3:4], fmeta[:, 2:3], -1.0,
                                               fmeta[:, 0:1], op0=OP.mult, op1=OP.add)
                    wt(v, sV.sig(i))
                    return sV.sig(v.tensor_scalar(fmeta[:, 0:1], fmeta[:, 3:4],
                                                  LOGK, None, op0=OP.add))
                atok = emit(block.vector, _)

            def _(v):
                wt(v, atok)
                i = v.tensor_scalar(fmeta[:, 4:5], fmeta[:, 0:1], -1.0, None,
                                    op0=OP.mult)
                wt(v, sV.sig(i))
                return sV.sig(v.tensor_scalar(sat[:, :], svals[:, 0:NCH],
                                              fmeta[:, 4:5], None, op0=OP.is_ge))
            sat_tok = emit(block.vector, _)

            def _(t):
                wt(t, sat_tok)
                t.matmul(pD[:, 0:NCH], UT2[:, :], sat[:, :], start=True, stop=True)
                return sT.sig(t.matmul(pD[:, 512:512 + NCH], ONES[:, :], sat[:, :],
                                       start=True, stop=True))
            pre_tok = emit(block.tensor, _)

            def _(v):
                wt(v, pre_tok)
                i = v.tensor_copy(pcol[:, :], pD[:, 0:NCH])
                wt(v, sV.sig(i))
                i = v.tensor_copy(fmeta[:, 5:6], pD[:, 512:513])
                wt(v, sV.sig(i))
                for c in range(1, NCH):
                    i = v.tensor_add(pcol[:, c:c + 1], pcol[:, c:c + 1],
                                     fmeta[:, 5:6])
                    wt(v, sV.sig(i))
                    if c < NCH - 1:
                        i = v.tensor_tensor(fmeta[:, 5:6], fmeta[:, 5:6],
                                            pD[:, 512 + c:512 + c + 1], op=OP.add)
                        wt(v, sV.sig(i))
                return sV.tok()
            px_tok = emit(block.vector, _)

            def _(v):
                wt(v, px_tok)
                if NHEAVY >= T:
                    return sV.sig(v.memset(hmask[:, :], 1.0))
                i = v.tensor_scalar(kn[:, 0:NCH], pcol[:, :], float(NHEAVY), None,
                                    op0=OP.is_le)
                wt(v, sV.sig(i))
                return sV.sig(v.tensor_mul(hmask[:, :], kn[:, 0:NCH], sat[:, :]))
            hm_tok = emit(block.vector, _)

            acc_tok2 = [None] * NCH
            dn_free = [None, None]
            up_free = None
            TG = 512 if T >= 512 else T
            SUBS = TG // 128
            for g in range(T // TG):
                def _(t, g=g):
                    wt(t, ltoks[-1], bcast_tok, rms_toks[-1], up_free)
                    if g == 0:
                        wt(t, hm_tok)
                    i = None
                    for ii in range(4):
                        ps = (pA, pB)[ii // 2]
                        col = (ii % 2) * 512
                        for k in range(8):
                            i = t.matmul(ps[:, col:col + TG],
                                         lw1[:, k, ii * 128:(ii + 1) * 128],
                                         XNT[:, k, g * TG:(g + 1) * TG],
                                         start=(k == 0), stop=False)
                        i = t.matmul(ps[:, col:col + TG],
                                     lb1[0:1, ii * 128:(ii + 1) * 128],
                                     onesr[0:1, 0:TG], start=False, stop=True)
                    return sT.sig(i)
                tup = emit(block.tensor, _)

                def _(s, g=g, tup=tup):
                    wt(s, tup)
                    i = None
                    for ii in range(4):
                        ps = (pA, pB)[ii // 2]
                        col = (ii % 2) * 512
                        i = s.activation(hl_bf[:, ii, 0:TG], ps[:, col:col + TG],
                                         ACT.Gelu)
                    return sS.sig(i)
                tgl = emit(block.scalar, _)
                up_free = tgl

                for sub in range(SUBS):
                    ch = g * SUBS + sub
                    ps = pC if sub % 2 == 0 else pD
                    def _(t, sub=sub, ps=ps, tgl=tgl):
                        wt(t, tgl, dn_free[sub % 2])
                        i = None
                        for ii in range(4):
                            t.matmul(ps[:, 0:512],
                                     hl_bf[:, ii, sub * 128:(sub + 1) * 128],
                                     lw2[:, ii, 0:512], start=(ii == 0), stop=False)
                            i = t.matmul(ps[:, 512:1024],
                                         hl_bf[:, ii, sub * 128:(sub + 1) * 128],
                                         lw2[:, ii, 512:1024], start=(ii == 0),
                                         stop=False)
                        t.matmul(ps[:, 0:512], onesr[0:1, sub * 128:sub * 128 + 128],
                                 lb2[0:1, 0:512], start=False, stop=True)
                        i = t.matmul(ps[:, 512:1024],
                                     onesr[0:1, sub * 128:sub * 128 + 128],
                                     lb2[0:1, 512:1024], start=False, stop=True)
                        return sT.sig(i)
                    tdn = emit(block.tensor, _)

                    def _(v, ch=ch, tdn=tdn, ps=ps):
                        wt(v, tdn, acc_tok2[ch])
                        return sV.sig(v.tensor_add(X[:, ch, :], X[:, ch, :],
                                                   ps[:, :]))
                    acc_tok2[ch] = emit(block.vector, _)
                    dn_free[sub % 2] = acc_tok2[ch]

            hacc = list(acc_tok2)
            hq_prev_mm = None
            hbuf_free = [None, None]
            hup_free = [up_free] * 4
            hdn_free = [None, None]
            for q in range(4):
                qb = q % 2
                def _(g, q=q, qb=qb):
                    wt(g, hbuf_free[qb], cst2)
                    for k in range(8):
                        dG.sig(g.dma_start(
                            out=hw1q[qb][:, k, :],
                            in_=hw1_hbm[k * 128:(k + 1) * 128,
                                        q * 1024:(q + 1) * 1024]))
                        dG.throttle(g)
                    for k in range(8):
                        dG.sig(g.dma_start(
                            out=hw2q[qb][:, k, :],
                            in_=hw2_hbm[q * 1024 + k * 128:q * 1024 + (k + 1) * 128, :]))
                        dG.throttle(g)
                    return dG.tok()
                hwd = emit(block.gpsimd, _)

                def _(v, q=q, qb=qb):
                    wt(v, hwd, hbuf_free[qb], cst2)
                    i = None
                    for k in range(8):
                        i = v.tensor_scalar(hw1q[qb][:, k, :], hw1q[qb][:, k, :],
                                            hgam[:, k:k + 1], None, op0=OP.mult)
                    return sV.sig(i)
                htoks = [emit(block.vector, _)]

                def _(g, q=q):
                    wt(g, htoks[-1], hq_prev_mm)
                    i = dG.sig(g.dma_start(
                        out=hb1q[0:1, :],
                        in_=hb1_hbm[0:1, q * 1024:(q + 1) * 1024]))
                    dG.throttle(g)
                    return i
                hbtok = emit(block.gpsimd, _)

                last_dn = None
                for g in range(T // TG):
                    PSL = (pA, pA, pB, pB, pC, pC, pD, pD)

                    def _(t, q=q, g=g, hbtok=hbtok):
                        wt(t, htoks[-1], hbtok, *hup_free)
                        wt(t, hdn_free[0], hdn_free[1])
                        i = None
                        for ii in range(8):
                            ps = PSL[ii]
                            col = (ii % 2) * 512
                            for k in range(8):
                                t.matmul(ps[:, col:col + TG],
                                         hw1q[qb][:, k, ii * 128:(ii + 1) * 128],
                                         XNT[:, k, g * TG:(g + 1) * TG],
                                         start=(k == 0), stop=False)
                            i = t.matmul(ps[:, col:col + TG],
                                         hb1q[0:1, ii * 128:(ii + 1) * 128],
                                         onesr[0:1, 0:TG], start=False, stop=True)
                        return sT.sig(i)
                    tuph = emit(block.tensor, _)

                    def _(s, tuph=tuph):
                        wt(s, tuph)
                        i = None
                        for ii in range(8):
                            ps = PSL[ii]
                            col = (ii % 2) * 512
                            i = s.activation(hh_bf[:, ii, 0:TG], ps[:, col:col + TG],
                                             ACT.Gelu)
                        return sS.sig(i)
                    tglh = emit(block.scalar, _)
                    hup_free = [tglh] * 4

                    for sub in range(SUBS):
                        ch = g * SUBS + sub
                        ps = pC if sub % 2 == 0 else pD

                        def _(t, q=q, sub=sub, ps=ps, tglh=tglh):
                            wt(t, tglh, hdn_free[sub % 2])
                            i = None
                            for ii in range(8):
                                t.matmul(ps[:, 0:512],
                                         hh_bf[:, ii, sub * 128:(sub + 1) * 128],
                                         hw2q[qb][:, ii, 0:512], start=(ii == 0),
                                         stop=(ii == 7 and q != 0))
                                i = t.matmul(ps[:, 512:1024],
                                             hh_bf[:, ii, sub * 128:(sub + 1) * 128],
                                             hw2q[qb][:, ii, 512:1024], start=(ii == 0),
                                             stop=(ii == 7 and q != 0))
                            if q == 0:
                                t.matmul(ps[:, 0:512],
                                         onesr[0:1, sub * 128:sub * 128 + 128],
                                         hb2[0:1, 0:512], start=False, stop=True)
                                i = t.matmul(ps[:, 512:1024],
                                             onesr[0:1, sub * 128:sub * 128 + 128],
                                             hb2[0:1, 512:1024], start=False, stop=True)
                            return sT.sig(i)
                        tdnh = emit(block.tensor, _)
                        last_dn = tdnh

                        def _(v, ch=ch, tdnh=tdnh, ps=ps):
                            wt(v, tdnh, hacc[ch], hm_tok)
                            return sV.sig(v.scalar_tensor_tensor(
                                X[:, ch, :], ps[:, :], hmask[:, ch:ch + 1],
                                X[:, ch, :], op0=OP.mult, op1=OP.add))
                        hacc[ch] = emit(block.vector, _)
                        hdn_free[sub % 2] = hacc[ch]
                hq_prev_mm = last_dn
                hbuf_free[qb] = last_dn

            ln4_tok = [None] * NCH
            for c in range(NCH):
                pre = [hacc[c]]
                if c == 0:
                    pre.append(hq_prev_mm)
                ln4_tok[c], _ = emit_ln_chunk(c, pre, pA if c % 2 else pB,
                                              skip_t=True)
            fctx.close()



        def _(g):
            for c in range(NCH):
                if "l1f" in phases:
                    wt(g, ln4_tok[c])
                else:
                    wt(g, (sV, sV.n), (sS, sS.n), (sT, sT.n))
                dX.sig(g.dma_start(out=y_hbm[c * 128:(c + 1) * 128, :],
                                   in_=X[:, c, :]))
            if DBG:
                wt(g, (sV, sV.n), (sS, sS.n), (sT, sT.n))
                dX.sig(g.dma_start(out=dbg_hbm["qn"], in_=qn[:, :]))
                dX.sig(g.dma_start(out=dbg_hbm["PT"],
                                   in_=PT[:, :, :, :].rearrange("p a b c -> p (a b c)")))
                dX.sig(g.dma_start(out=dbg_hbm["kT"],
                                   in_=kT[(NCH - 1) % 2][:, :, :].rearrange("p a b -> p (a b)")))
                dX.sig(g.dma_start(out=dbg_hbm["qT"],
                                   in_=qT[(NCH - 1) % 2][:, :, :].rearrange("p a b -> p (a b)")))
                dX.sig(g.dma_start(out=dbg_hbm["nrm"],
                                   in_=nrm[:, :, :].rearrange("p a b -> p (a b)")))
                dX.sig(g.dma_start(out=dbg_hbm["vbf"],
                                   in_=vbf[(NCH - 1) % 2][:, :, :].rearrange("p a b -> p (a b)")))
            g.wait_ge(dX.h, dX.n)
            return None
        emit(block.gpsimd, _)

    return nc


def kernel(**inputs):
    base = host_consts()
    x_full = np.ascontiguousarray(inputs["L0_attn_x"], dtype=np.float32)
    for w, nm in zip("qkvo", ["Wq", "Wk", "Wv", "Wo"]):
        base[f"l0_w{w}T"] = np.ascontiguousarray(
            inputs[f"L0_attn_{nm}"].T.reshape(8, 128, D).astype(np.float32))
    base["l0_gateT"] = np.ascontiguousarray(inputs["L0_ffn_gate_w"].T.astype(np.float32))
    base["l0_moe_w1"] = np.ascontiguousarray(inputs["L0_ffn_w1"].astype(np.float32))
    base["l0_moe_w3"] = np.ascontiguousarray(inputs["L0_ffn_w3"].astype(np.float32))
    base["l0_moe_w2"] = np.ascontiguousarray(inputs["L0_ffn_w2"].astype(np.float32))
    base["l1_wqkv"] = np.ascontiguousarray(
        inputs["L1_attn_w_qkv"].reshape(8, 128, 3 * D).astype(np.float32))
    base["l1_wout"] = np.ascontiguousarray(
        inputs["L1_attn_w_out"].reshape(8, 128, D).astype(np.float32))
    base["l1_rrep"] = np.ascontiguousarray(
        np.tile(np.asarray(inputs["L1_ffn_routing_token"], np.float32)[None, :], (128, 1)))
    base["l1_lgam"] = np.ascontiguousarray(
        np.asarray(inputs["L1_ffn_light_gamma"], np.float32).reshape(8, 128).T)
    base["l1_hgam"] = np.ascontiguousarray(
        np.asarray(inputs["L1_ffn_heavy_gamma"], np.float32).reshape(8, 128).T)
    base["l1_lw1"] = np.ascontiguousarray(
        np.asarray(inputs["L1_ffn_light_w1"], np.float32).reshape(8, 128, 512))
    base["l1_lw2"] = np.ascontiguousarray(
        np.asarray(inputs["L1_ffn_light_w2"], np.float32).reshape(4, 128, D))
    base["l1_lb1"] = np.ascontiguousarray(np.asarray(inputs["L1_ffn_light_b1"], np.float32)[None, :])
    base["l1_lb2"] = np.ascontiguousarray(np.asarray(inputs["L1_ffn_light_b2"], np.float32)[None, :])
    base["l1_hb1"] = np.ascontiguousarray(np.asarray(inputs["L1_ffn_heavy_b1"], np.float32)[None, :])
    base["l1_hb2"] = np.ascontiguousarray(np.asarray(inputs["L1_ffn_heavy_b2"], np.float32)[None, :])
    base["l1_hw1"] = np.ascontiguousarray(np.asarray(inputs["L1_ffn_heavy_w1"], np.float32))
    base["l1_hw2"] = np.ascontiguousarray(np.asarray(inputs["L1_ffn_heavy_w2"], np.float32))
    nc = build_program()
    in_maps = []
    for core in range(8):
        m = dict(base)
        m["x"] = np.ascontiguousarray(x_full[core])
        m["xTin"] = np.ascontiguousarray(x_full[core].T)
        in_maps.append(m)
    res = run_bass_kernel_spmd(nc, in_maps, core_ids=list(range(8)))
    return np.stack([res.results[i]["y"] for i in range(8)], axis=0)


# version 12
# speedup vs baseline: 1.2146x; 1.0235x over previous
import contextlib
import os
import numpy as np
import concourse.bass as bass
import concourse.mybir as mybir
from concourse.bass_utils import run_bass_kernel_spmd

F32 = mybir.dt.float32
BF16 = mybir.dt.bfloat16
AX = mybir.AxisListType
OP = mybir.AluOpType
ACT = mybir.ActivationFunctionType

T = 2048
D = 1024
NCH = 16
H = 16
HD = 64
NEXP = 16
HEAVY_I = 4096
NEG = -1e30

PHASES = ("l0a", "l0f", "l1a", "l1f")


def _split2(v):
    hi = np.float32(v)
    lo = np.float32(np.float64(v) - np.float64(hi))
    return float(hi), float(lo)


ALIBI_SLOPES = [_split2(2.0 ** (-8.0 * np.float64(h) / 15.0)) for h in range(H)]


def host_consts():
    c = {"c_ident": np.eye(128, dtype=np.float32)}
    s_ = np.arange(128)[:, None]
    t_ = np.arange(128)[None, :]
    c["c_UT"] = (s_ <= t_).astype(np.float32)
    c["c_ones"] = np.ones((128, 128), np.float32)
    s = np.arange(128)[:, None]
    t = np.arange(128)[None, :]
    rel0 = 128.0 + s - t
    rel1 = s - t
    val0 = s < t
    val1 = s >= t
    R = np.zeros((128, 2, 128), np.float64)
    Mneg = np.zeros((128, 2, 128), np.float64)
    R[:, 0, :] = np.where(val0, -rel0, 0.0)
    R[:, 1, :] = np.where(val1, -rel1, 0.0)
    Mneg[:, 0, :] = np.where(val0, 0.0, NEG)
    Mneg[:, 1, :] = np.where(val1, 0.0, NEG)
    c["c_R"] = R.astype(np.float32)
    c["c_Mneg"] = Mneg.astype(np.float32)
    return c


class Sem:
    def __init__(self, nc, ctx, name, unit=1):
        self.h = ctx.enter_context(nc.semaphore(name))
        self.unit = unit
        self.n = 0

    def sig(self, inst):
        self.n += self.unit
        inst.then_inc(self.h, self.unit)
        return (self, self.n)

    def tok(self):
        return (self, self.n)

    def throttle(self, eng, max_out=1):
        eng.wait_ge(self.h, self.n)


def wt(eng, *toks):
    best = {}
    for tk in toks:
        if tk is None:
            continue
        s, v = tk
        if v > 0 and v > best.get(id(s), (s, 0))[1]:
            best[id(s)] = (s, v)
    for s, v in best.values():
        eng.wait_ge(s.h, v)


def tr_slot(k):
    return (k % 2) * 512 + ((k // 2) % 2) * 128


def build_program(phases=PHASES):
    nc = bass.Bass("TRN2", target_bir_lowering=False)

    def din(name, shape):
        return nc.dram_tensor(name, list(shape), F32, kind="ExternalInput").ap()

    x_hbm = din("x", (T, D))
    xT_hbm = din("xTin", (D, T))
    l0w_hbm = {w: din(f"l0_w{w}T", (8, 128, D)) for w in "qkvo"}
    gate_hbm = din("l0_gateT", (D, NEXP))
    l1wqkv_hbm = din("l1_wqkv", (8, 128, 3 * D))
    l1wout_hbm = din("l1_wout", (8, 128, D))
    ut_hbm = din("c_UT", (128, 128))
    ones_hbm = din("c_ones", (128, 128))
    rrep_hbm = din("l1_rrep", (128, D))
    lgam_hbm = din("l1_lgam", (128, 8))
    hgam_hbm = din("l1_hgam", (128, 8))
    lw1_hbm = din("l1_lw1", (8, 128, 512))
    lw2_hbm = din("l1_lw2", (4, 128, D))
    lb1_hbm = din("l1_lb1", (1, 512))
    lb2_hbm = din("l1_lb2", (1, D))
    hw1_hbm = din("l1_hw1", (D, HEAVY_I))
    hw2_hbm = din("l1_hw2", (HEAVY_I, D))
    hb1_hbm = din("l1_hb1", (1, HEAVY_I))
    hb2_hbm = din("l1_hb2", (1, D))
    moe_w1_hbm = din("l0_moe_w1", (NEXP, D, 512))
    moe_w3_hbm = din("l0_moe_w3", (NEXP, D, 512))
    moe_w2_hbm = din("l0_moe_w2", (NEXP, 512, D))
    ident_h = din("c_ident", (128, 128))
    R_h = din("c_R", (128, 2, 128))
    Mneg_h = din("c_Mneg", (128, 2, 128))
    y_hbm = nc.dram_tensor("y", [T, D], F32, kind="ExternalOutput").ap()
    DBG = os.environ.get("KDBG")
    if DBG:
        dbg_hbm = {nm: nc.dram_tensor(f"dbg_{nm}", sh, dt, kind="ExternalOutput").ap()
                   for nm, sh, dt in [("qn", [128, D], F32),
                                      ("PT", [128, 8 * 2 * 128], BF16),
                                      ("kT", [128, 8 * 128], BF16),
                                      ("qT", [128, 16 * 128], BF16),
                                      ("nrm", [128, 8 * H], F32),
                                      ("vbf", [128, H * 65], BF16)]}

    ctx = contextlib.ExitStack()
    with ctx:
        E = ctx.enter_context
        X = E(nc.sbuf_tensor("X", [128, NCH, D], F32))
        XNT = E(nc.sbuf_tensor("XNT", [128, 8, T], BF16))
        ident = E(nc.sbuf_tensor("ident", [128, 128], F32))
        identb = E(nc.sbuf_tensor("identb", [128, 128], BF16))
        cst_sm = E(nc.sbuf_tensor("cst_sm", [128, 2], F32))
        lnm = E(nc.sbuf_tensor("lnm", [128, 8], F32))

        svals = E(nc.sbuf_tensor("svals", [128, 32], F32))
        rms_sv_tok = [None] * NCH
        dX = Sem(nc, ctx, "dX", 16)
        dW = Sem(nc, ctx, "dW", 16)
        dxin = [Sem(nc, ctx, f"dxin{c}", 16) for c in range(NCH)]
        dC = Sem(nc, ctx, "dC", 16)
        dSl = [Sem(nc, ctx, f"dSl{i}", 16) for i in range(2)]
        dG = Sem(nc, ctx, "dG", 16)
        sT = Sem(nc, ctx, "sT")
        sV = Sem(nc, ctx, "sV")
        sS = Sem(nc, ctx, "sS")
        sG = Sem(nc, ctx, "sG")
        block = E(nc.Block())

        def load_set6(s):
            s.add_instruction(mybir.InstLoadActFuncSet(
                name=nc.get_next_instruction_name(), act_func_set_id=6,
                ins=[], outs=[]))

        def emit(section, fn):
            out = []

            def wrap(eng):
                out.append(fn(eng))
            section(wrap)
            return out[0]

        def emit_transposes(srcs, ptile, dsts, pre_toks=(), evac="vector",
                            qtz_split=None):
            n = len(srcs)
            ttoks = [None] * n
            sem = sS if evac == "scalar" else sV
            base = sem.n

            def cum(k):
                if qtz_split is None:
                    return base + k + 1
                return base + (2 * (k + 1) if k < 8 else 16 + (k - 8 + 1))
            etoks = [(sem, cum(k)) for k in range(n)]

            def tpart(t):
                for k in range(n):
                    if k == 0:
                        wt(t, *pre_toks)
                    if k >= 2:
                        wt(t, etoks[k - 2])
                    i = t.transpose(ptile[:, tr_slot(k):tr_slot(k) + 128],
                                    srcs[k], ident[:, :])
                    ttoks[k] = sT.sig(i)
            block.tensor(tpart)

            def epart(e):
                for k in range(n):
                    wt(e, ttoks[k])
                    src = ptile[:, tr_slot(k):tr_slot(k) + 128]
                    if qtz_split is not None and k < 8:
                        sem.sig(e.tensor_copy(qtz_split[0:64, 2 * k, :],
                                              src[0:64, :]))
                        sem.sig(e.tensor_copy(qtz_split[64:128, 2 * k + 1, :],
                                              src[64:128, :]))
                    elif evac == "scalar":
                        sem.sig(e.copy(dsts[k], src))
                    else:
                        sem.sig(e.tensor_copy(dsts[k], src))
                assert sem.n == cum(n - 1), (sem.n, base, n)
            (block.scalar if evac == "scalar" else block.vector)(epart)
            return etoks

        xin_tok = [None] * NCH

        def _(sy):
            for c in range(NCH):
                i = sy.dma_start(out=X[:, c, :], in_=x_hbm[c * 128:(c + 1) * 128, :])
                xin_tok[c] = dxin[c].sig(i)
        block.sync(_)

        qn = E(nc.sbuf_tensor("qn", [128, D], F32))
        kn = E(nc.sbuf_tensor("kn", [128, D], F32))
        nrm = E(nc.sbuf_tensor("nrm", [128, 8, H], F32))
        pA = E(nc.psum_tensor("pA", [128, 1024], F32))
        pB = E(nc.psum_tensor("pB", [128, 1024], F32))
        pC = E(nc.psum_tensor("pC", [128, 1024], F32))
        pD = E(nc.psum_tensor("pD", [128, 1024], F32))
        l0actx = contextlib.ExitStack()
        EA = l0actx.enter_context
        l0w = {w: EA(nc.sbuf_tensor(f"l0w{w}", [128, 8, D], BF16)) for w in "qkvo"}
        biasT = EA(nc.sbuf_tensor("biasT", [128, H, 2, 128], F32))
        rctx = contextlib.ExitStack()
        Rt = rctx.enter_context(nc.sbuf_tensor("Rt", [128, 2, 128], F32))
        Mnegt = rctx.enter_context(nc.sbuf_tensor("Mnegt", [128, 2, 128], F32))


        def _(sy):
            dC.sig(sy.dma_start(out=ident[:, :], in_=ident_h))
            dC.sig(sy.dma_start(out=Rt[:, :, :], in_=R_h))
            return dC.sig(sy.dma_start(out=Mnegt[:, :, :], in_=Mneg_h))
        cst_tok = emit(block.sync, _)
        cst_tok = (dC, 48)

        wcast_tok = {}

        def _(g):
            for w in "qkvo":
                for q in range(8):
                    dG.sig(g.dma_start(out=l0w[w][:, q, :], in_=l0w_hbm[w][q]))
                    dG.throttle(g)
                wcast_tok[w] = dG.tok()
            return None
        emit(block.gpsimd, _)

        def _(v):
            wt(v, cst_tok)
            for h in range(H):
                hi, _lo = ALIBI_SLOPES[h]
                i = v.scalar_tensor_tensor(biasT[:, h, :, :], Rt[:, :, :], hi,
                                           Mnegt[:, :, :], op0=OP.mult, op1=OP.add)
            wt(v, sV.sig(i))
            for h in range(H):
                _hi, lo = ALIBI_SLOPES[h]
                i = v.scalar_tensor_tensor(biasT[:, h, :, :], Rt[:, :, :], lo,
                                           biasT[:, h, :, :], op0=OP.mult,
                                           op1=OP.add)
            return sV.sig(i)
        bias_tok = emit(block.vector, _)
        rctx.close()

        def _(v):
            v.memset(cst_sm[:, 1:2], float(np.float32(np.log(np.float64(32.0)))))
            wt(v, cst_tok)
            v.tensor_copy(identb[:, :], ident[:, :])
            return sV.sig(v.memset(cst_sm[:, 0:1], 1e-5))
        eps_tok = emit(block.vector, _)

        xT = EA(nc.sbuf_tensor("xT", [128, 8, 128], BF16))
        vbf = [EA(nc.sbuf_tensor(f"vbf{i}", [128, H, 65], BF16)) for i in range(2)]
        kT = [EA(nc.sbuf_tensor(f"kT{i}", [128, 8, 128], BF16)) for i in range(2)]
        qT = [EA(nc.sbuf_tensor(f"qT{i}", [128, 16, 128], BF16)) for i in range(2)]
        PT = EA(nc.sbuf_tensor("PT", [128, 8, 2, 128], BF16))

        def _(v):
            wt(v, bias_tok)
            v.memset(qT[0][:, :, :], 0.0)
            v.memset(qT[1][:, :, :], 0.0)
            v.memset(vbf[0][:, :, 64:65], 1.0)
            return sV.sig(v.memset(vbf[1][:, :, 64:65], 1.0))
        ones_tok = emit(block.vector, _)

        tok_tqkv = [None] * NCH
        tok_qkev = [None] * NCH
        tok_tnrm = [None] * NCH
        tok_tvev = [None] * NCH
        tok_tsm = [[None, None] for _ in range(NCH)]
        tok_exp = [[None, None] for _ in range(NCH)]
        tok_pvT = [[None, None] for _ in range(NCH)]
        tok_pve = [None] * NCH
        tok_aoev = [None] * NCH
        tok_x1ev = [None] * NCH
        tok_tres = [None] * NCH
        tok_tln = [None] * NCH

        def l0a_xload(c):
            def _(g, c=c):
                wt(g, tok_tqkv[c - 1] if c else bias_tok)
                for k in range(8):
                    dG.sig(g.dma_start(out=xT[:, k, :],
                                       in_=xT_hbm[k * 128:(k + 1) * 128,
                                                  c * 128:(c + 1) * 128]))
                    dG.throttle(g)
                return dG.tok()
            return emit(block.gpsimd, _)

        def l0a_qkv(c, xld):
            def _(t, c=c, xld=xld):
                wt(t, wcast_tok["q"], wcast_tok["k"], wcast_tok["v"], xld)
                if c >= 1:
                    for tk in tok_qkev[c - 1]:
                        wt(t, tk)
                if c >= 2:
                    wt(t, tok_tsm[c - 2][0], tok_tsm[c - 2][1])
                for k in range(8):
                    stt, sp = (k == 0), (k == 7)
                    t.matmul(pA[:, 0:512], xT[:, k, :], l0w["q"][:, k, 0:512], start=stt, stop=sp)
                    t.matmul(pA[:, 512:1024], xT[:, k, :], l0w["q"][:, k, 512:1024], start=stt, stop=sp)
                for k in range(8):
                    stt, sp = (k == 0), (k == 7)
                    t.matmul(pB[:, 0:512], xT[:, k, :], l0w["k"][:, k, 0:512], start=stt, stop=sp)
                    t.matmul(pB[:, 512:1024], xT[:, k, :], l0w["k"][:, k, 512:1024], start=stt, stop=sp)
                if c >= 2:
                    wt(t, tok_pve[c - 2])
                for k in range(8):
                    stt, sp = (k == 0), (k == 7)
                    t.matmul(pC[:, 0:512], xT[:, k, :], l0w["v"][:, k, 0:512], start=stt, stop=sp)
                    i = t.matmul(pC[:, 512:1024], xT[:, k, :], l0w["v"][:, k, 512:1024], start=stt, stop=sp)
                return sT.sig(i)
            tok_tqkv[c] = emit(block.tensor, _)

        def l0a_norm(c):
            cur = c % 2

            tred_pred = (sV, sV.n + 1)

            def _(s, c=c):
                wt(s, tok_tqkv[c], ones_tok)
                if c >= 2:
                    wt(s, tok_aoev[c - 2][-1], tok_exp[c - 2][1])
                s.activation(qn[:, :], pA[:, :], ACT.Square)
                i = s.activation(kn[:, :], pB[:, :], ACT.Square)
                tq = sS.sig(i)
                i = s.copy(vbf[cur][:, :, 0:64],
                           pC[:, :].rearrange("p (h d) -> p h d", h=H))
                tv = sS.sig(i)
                wt(s, tred_pred)
                s.activation(nrm[:, 2, :], nrm[:, 0, :], ACT.Ln)
                i = s.activation(nrm[:, 3, :], nrm[:, 1, :], ACT.Ln)
                wt(s, sS.sig(i))
                s.activation(nrm[:, 4, :], nrm[:, 2, :], ACT.Exp, scale=-0.5)
                i = s.activation(nrm[:, 5, :], nrm[:, 3, :], ACT.Exp, scale=-0.5)
                return tq, tv, sS.sig(i)
            tsq, tok_tvev[c], tsqrt = emit(block.scalar, _)

            def _(v, tsq=tsq):
                wt(v, tsq)
                v.tensor_reduce(nrm[:, 0, :], qn[:, :].rearrange("p (h d) -> p h d", h=H),
                                axis=AX.X, op=OP.add)
                i = v.tensor_reduce(nrm[:, 1, :], kn[:, :].rearrange("p (h d) -> p h d", h=H),
                                    axis=AX.X, op=OP.add)
                return sV.sig(i)
            tred = emit(block.vector, _)
            assert tred == tred_pred, (tred, tred_pred)

            def _(v, c=c, tsqrt=tsqrt):
                wt(v, tsqrt)
                if c >= 2:
                    wt(v, tok_exp[c - 2][1])
                v.tensor_scalar(nrm[:, 4, :], nrm[:, 4, :], 1e6, None, op0=OP.min)
                i = v.tensor_scalar(nrm[:, 5, :], nrm[:, 5, :], 1e6, None, op0=OP.min)
                wt(v, sV.sig(i))
                v.tensor_tensor(qn[:, :].rearrange("p (h d) -> p h d", h=H),
                                pA[:, :].rearrange("p (h d) -> p h d", h=H),
                                nrm[:, 4, :].unsqueeze(2).broadcast_to([128, H, HD]),
                                op=OP.mult)
                i = v.tensor_tensor(kn[:, :].rearrange("p (h d) -> p h d", h=H),
                                    pB[:, :].rearrange("p (h d) -> p h d", h=H),
                                    nrm[:, 5, :].unsqueeze(2).broadcast_to([128, H, HD]),
                                    op=OP.mult)
                return sV.sig(i)
            tok_tnrm[c] = emit(block.vector, _)

        def l0a_qktr(c):
            cur = c % 2
            qksrc = ([qn[:, k * 128:(k + 1) * 128] for k in range(8)] +
                     [kn[:, k * 128:(k + 1) * 128] for k in range(8)])
            qkdst = ([qT[cur][:, 2 * k, :] for k in range(8)] +
                     [kT[cur][:, k, :] for k in range(8)])
            tok_qkev[c] = emit_transposes(qksrc, pA, qkdst,
                                          pre_toks=(tok_tnrm[c],),
                                          qtz_split=qT[cur])

        def l0a_aoT(c):
            ttoks = [None] * 8
            base = sS.n
            etoks = [(sS, base + k + 1) for k in range(8)]

            def _(t, c=c):
                for k in range(8):
                    if k == 0:
                        wt(t, tok_pve[c], eps_tok)
                    if k >= 2:
                        wt(t, etoks[k - 2])
                    i = t.matmul(pD[:, tr_slot(k):tr_slot(k) + 128],
                                 PT[:, k, 1, :], identb[:, :],
                                 start=True, stop=True)
                    ttoks[k] = sT.sig(i)
            block.tensor(_)

            def _(s):
                for k in range(8):
                    wt(s, ttoks[k])
                    sS.sig(s.copy(PT[:, k, 0, :],
                                  pD[:, tr_slot(k):tr_slot(k) + 128]))
                assert sS.n == base + 8
            block.scalar(_)
            tok_aoev[c] = etoks

        def l0a_x1ev(c):
            x1src = [X[:, c, k * 128:(k + 1) * 128] for k in range(8)]
            x1dst = [XNT[:, k, c * 128:(c + 1) * 128] for k in range(8)]
            tok_x1ev[c] = emit_transposes(x1src, pD, x1dst,
                                          pre_toks=(tok_tln[c],
                                                    tok_aoev[c + 1][-1]),
                                          evac="scalar")

        def l0a_wo_ln(c):
            def _(t, c=c):
                wt(t, wcast_tok["o"], tok_aoev[c][-1])
                if c >= 1 and tok_x1ev[c - 1] is not None:
                    wt(t, tok_x1ev[c - 1][-1])
                for k in range(8):
                    stt, sp = (k == 0), (k == 7)
                    t.matmul(pD[:, 0:512], PT[:, k, 0, :], l0w["o"][:, k, 0:512], start=stt, stop=sp)
                    i = t.matmul(pD[:, 512:1024], PT[:, k, 0, :], l0w["o"][:, k, 512:1024], start=stt, stop=sp)
                return sT.sig(i)
            twoT = emit(block.tensor, _)

            def _(v, c=c, twoT=twoT):
                wt(v, twoT, xin_tok[c])
                return sV.sig(v.tensor_add(X[:, c, :], X[:, c, :], pD[:, :]))
            tok_tres[c] = tres = emit(block.vector, _)

            tvar_pred = (sV, sV.n + 3)

            def _(s, c=c, tres=tres):
                wt(s, tres, eps_tok)
                junk = XNT[:, :, c * 128:(c + 1) * 128]
                xin3 = X[:, c, :].rearrange("p (a t) -> p a t", a=8)
                i = s.activation(junk, xin3, ACT.Copy,
                                 accum_out=lnm[:, 0:1])
                wt(s, sS.sig(i))
                tstt = sS.sig(s.activation(junk, xin3, ACT.Square,
                                           accum_out=lnm[:, 1:2]))
                wt(s, tvar_pred)
                i = s.activation(lnm[:, 4:5], lnm[:, 3:4], ACT.Ln,
                                 bias=cst_sm[:, 0:1])
                wt(s, sS.sig(i))
                return tstt, sS.sig(s.activation(lnm[:, 5:6], lnm[:, 4:5],
                                                 ACT.Exp, scale=-0.5))
            tst, tsd = emit(block.scalar, _)

            def _(v, tst=tst):
                wt(v, tst)
                v.tensor_scalar(lnm[:, 2:3], lnm[:, 0:1], 1.0 / D, None, op0=OP.mult)
                i = v.tensor_scalar(lnm[:, 3:4], lnm[:, 1:2], 1.0 / D, None, op0=OP.mult)
                wt(v, sV.sig(i))
                i = v.tensor_tensor(lnm[:, 4:5], lnm[:, 2:3], lnm[:, 2:3], op=OP.mult)
                wt(v, sV.sig(i))
                return sV.sig(v.tensor_sub(lnm[:, 3:4], lnm[:, 3:4], lnm[:, 4:5]))
            tvar = emit(block.vector, _)
            assert tvar == tvar_pred, (tvar, tvar_pred)

            def _(v, c=c, tsd=tsd):
                wt(v, tsd)
                i = v.tensor_scalar(lnm[:, 6:7], lnm[:, 2:3], -1.0, None, op0=OP.mult)
                wt(v, sV.sig(i))
                return sV.sig(v.tensor_scalar(X[:, c, :], X[:, c, :],
                                              lnm[:, 6:7], lnm[:, 5:6],
                                              op0=OP.add, op1=OP.mult))
            tok_tln[c] = emit(block.vector, _)

        def buf(hh):
            b = kn if hh < 4 else qn
            return b[:, (hh % 4) * 256:(hh % 4) * 256 + 256]

        def l0a_scores(qc, g, part, last):
            qcur, qnxt = qc % 2, (qc + 1) % 2

            def ps_of(g, hh):
                if g == 0:
                    return (pB if hh < 4 else pC)
                return (pA if hh < 4 else pB)

            def _(t, qc=qc, g=g, part=part, last=last):
                kc = qc if last else qc + 1
                for tk in tok_qkev[kc]:
                    wt(t, tk)
                wt(t, tok_tnrm[kc], tok_tvev[kc])
                if g == 1 and part == "b":
                    wt(t, tok_tsm[qc][0])
                rng = range(4) if g == 0 or part == "a" else range(4, 8)
                if g == 0:
                    rng = range(8)
                i = None
                for hh in rng:
                    h = g * 8 + hh
                    tl = h // 2
                    ps = ps_of(g, hh)
                    base = (hh % 4) * 256
                    i = t.matmul(ps[:, base + 128:base + 256],
                                 kT[qcur][:, tl, :],
                                 qT[qcur][:, h, :], start=True, stop=True)
                    if not last:
                        i = t.matmul(ps[:, base:base + 128],
                                     kT[qnxt][:, tl, :],
                                     qT[qcur][:, h, :], start=True, stop=True)
                return sT.sig(i)
            return emit(block.tensor, _)

        def l0a_bias(qc, g, sc_toks, last):
            def _(v, qc=qc, g=g, last=last):
                wt(v, bias_tok, *sc_toks)
                if g == 1:
                    wt(v, tok_exp[qc][0])
                elif qc >= 1:
                    wt(v, tok_exp[qc - 1][1])
                i = None
                if last:
                    for hh in range(8):
                        v.memset(buf(hh)[:, 0:128], NEG)
                    for hh in range(8):
                        h = g * 8 + hh
                        ps = (pB if hh < 4 else pC) if g == 0 else (pA if hh < 4 else pB)
                        base = (hh % 4) * 256
                        i = v.tensor_tensor(
                            buf(hh)[:, 128:256],
                            ps[:, base + 128:base + 256],
                            biasT[:, h, 1, :],
                            op=OP.add)
                else:
                    for hh in range(8):
                        h = g * 8 + hh
                        ps = (pB if hh < 4 else pC) if g == 0 else (pA if hh < 4 else pB)
                        base = (hh % 4) * 256
                        i = v.tensor_tensor(
                            buf(hh),
                            ps[:, base:base + 256],
                            biasT[:, h, :, :].rearrange("p r t -> p (r t)"),
                            op=OP.add)
                return sV.sig(i)
            tok_tsm[qc][g] = emit(block.vector, _)
            return tok_tsm[qc][g]

        def l0a_exp_merged(qc):
            tsm1_pred = (sV, sV.n + 1)
            pv0_pred = (sT, sT.n + 2)

            def _(s, qc=qc):
                wt(s, tok_tsm[qc][0])
                s.activation(
                    PT[:, 0:4, :, :].rearrange("p h r t -> p (h r t)"),
                    kn[:, :], ACT.Exp)
                e0 = sS.sig(s.activation(
                    PT[:, 4:8, :, :].rearrange("p h r t -> p (h r t)"),
                    qn[:, :], ACT.Exp))
                wt(s, tsm1_pred, pv0_pred)
                s.activation(
                    PT[:, 0:4, :, :].rearrange("p h r t -> p (h r t)"),
                    kn[:, :], ACT.Exp)
                e1 = sS.sig(s.activation(
                    PT[:, 4:8, :, :].rearrange("p h r t -> p (h r t)"),
                    qn[:, :], ACT.Exp))
                return e0, e1
            tok_exp[qc][0], tok_exp[qc][1] = emit(block.scalar, _)
            return tsm1_pred, pv0_pred

        def l0a_pv(qc, g, last):
            qcur, qnxt = qc % 2, (qc + 1) % 2

            def _(t, qc=qc, g=g, last=last):
                wt(t, tok_exp[qc][g])
                if g == 0:
                    if qc >= 2 and tok_x1ev[qc - 2] is not None:
                        wt(t, tok_x1ev[qc - 2][-1])
                    wt(t, tok_tres[qc - 1] if qc >= 1 else None)
                else:
                    wt(t, tok_tsm[qc][0])
                i = None
                for hh in range(8):
                    h = g * 8 + hh
                    ps = pD if g == 0 else pC
                    col = (hh // 4) * 512 + (hh % 4) * 66
                    i = t.matmul(ps[:, col:col + 65], PT[:, hh, 1, :],
                                 vbf[qcur][:, h, :], start=True, stop=last)
                    if not last:
                        i = t.matmul(ps[:, col:col + 65], PT[:, hh, 0, :],
                                     vbf[qnxt][:, h, :], start=False, stop=True)
                return sT.sig(i)
            tok_pvT[qc][g] = emit(block.tensor, _)
            return tok_pvT[qc][g]

        def l0a_pvevac(qc):
            def _(v, qc=qc):
                wt(v, tok_pvT[qc][0], tok_pvT[qc][1])
                i = None
                for g4 in range(4):
                    ps = pD if g4 < 2 else pC
                    col = (g4 % 2) * 512
                    i = v.tensor_copy(nrm[:, 6, g4 * 4:(g4 + 1) * 4],
                                      ps[:, col:col + 264].rearrange(
                                          "p (h e) -> p h e", h=4)[:, :, 64])
                wt(v, sV.sig(i))
                i = v.reciprocal(nrm[:, 7, :], nrm[:, 6, :])
                wt(v, sV.sig(i))
                i = None
                for tile in range(8):
                    ps = pD if tile < 4 else pC
                    col = (tile // 4 % 2) * 0 + ((tile % 4) // 2) * 512 + (tile % 2) * 132
                    i = v.tensor_tensor(
                        PT[:, tile, 1, :].rearrange("p (h d) -> p h d", h=2),
                        ps[:, col:col + 132].rearrange("p (h e) -> p h e", h=2)[:, :, 0:64],
                        nrm[:, 7, tile * 2:tile * 2 + 2].unsqueeze(2).broadcast_to([128, 2, HD]),
                        op=OP.mult)
                return sV.sig(i)
            tok_pve[qc] = emit(block.vector, _)

        def l0a_att_head(qc, last):
            t0 = l0a_scores(qc, 0, None, last)
            l0a_bias(qc, 0, (t0,), last)
            tsm1_pred, pv0_pred = l0a_exp_merged(qc)
            t1a = l0a_scores(qc, 1, "a", last)
            pv0 = l0a_pv(qc, 0, last)
            assert pv0 == pv0_pred, (pv0, pv0_pred)
            t1b = l0a_scores(qc, 1, "b", last)
            tsm1 = l0a_bias(qc, 1, (t1a, t1b), last)
            assert tsm1 == tsm1_pred, (tsm1, tsm1_pred)
            l0a_pv(qc, 1, last)
            l0a_pvevac(qc)

        if "l0a" in phases:
            for i in range(NCH):
                xld = l0a_xload(i)
                l0a_qkv(i, xld)
                if i >= 2:
                    l0a_aoT(i - 2)
                l0a_norm(i)
                if i >= 3:
                    l0a_x1ev(i - 3)
                if i >= 2:
                    l0a_wo_ln(i - 2)
                l0a_qktr(i)
                if i >= 1:
                    l0a_att_head(i - 1, last=False)
            l0a_aoT(NCH - 2)
            if NCH >= 3:
                l0a_x1ev(NCH - 3)
            l0a_wo_ln(NCH - 2)
            l0a_att_head(NCH - 1, last=True)
            l0a_aoT(NCH - 1)
            l0a_x1ev(NCH - 2)
            l0a_wo_ln(NCH - 1)
            x1src = [X[:, NCH - 1, k * 128:(k + 1) * 128] for k in range(8)]
            x1dst = [XNT[:, k, (NCH - 1) * 128:NCH * 128] for k in range(8)]
            tok_x1ev[NCH - 1] = emit_transposes(
                x1src, pD, x1dst,
                pre_toks=(tok_tln[NCH - 1], tok_tres[NCH - 1]))
        l0actx.close()
        bar_l0a = [(sT, sT.n), (sV, sV.n), (sS, sS.n)]

        def emit_ln_chunk(c, pre_toks, ptile, skip_t=False):
            tvar_pred = (sV, sV.n + 3)

            def _(s):
                wt(s, *pre_toks)
                s.activation(kn[:, :], X[:, c, :], ACT.Copy,
                             accum_out=lnm[:, 0:1])
                i = s.activation(qn[:, :], X[:, c, :], ACT.Square,
                                 accum_out=lnm[:, 1:2])
                wt(s, sS.sig(i))
                wt(s, tvar_pred)
                i = s.activation(lnm[:, 4:5], lnm[:, 3:4], ACT.Ln,
                                 bias=cst_sm[:, 0:1])
                wt(s, sS.sig(i))
                return sS.sig(s.activation(lnm[:, 5:6], lnm[:, 4:5], ACT.Exp,
                                           scale=-0.5))
            tsd = emit(block.scalar, _)
            tst = (sS, tsd[1] - 2)

            def _(v):
                wt(v, tst)
                v.tensor_scalar(lnm[:, 2:3], lnm[:, 0:1], 1.0 / D, None, op0=OP.mult)
                i = v.tensor_scalar(lnm[:, 3:4], lnm[:, 1:2], 1.0 / D, None, op0=OP.mult)
                wt(v, sV.sig(i))
                i = v.tensor_tensor(lnm[:, 4:5], lnm[:, 2:3], lnm[:, 2:3], op=OP.mult)
                wt(v, sV.sig(i))
                i = sV.sig(v.tensor_sub(lnm[:, 3:4], lnm[:, 3:4], lnm[:, 4:5]))
                assert i == tvar_pred, (i, tvar_pred)
                return i
            tvar = emit(block.vector, _)

            def _(v):
                wt(v, tsd)
                i = v.tensor_scalar(lnm[:, 6:7], lnm[:, 2:3], -1.0, None, op0=OP.mult)
                wt(v, sV.sig(i))
                return sV.sig(v.tensor_scalar(X[:, c, :], X[:, c, :],
                                              lnm[:, 6:7], lnm[:, 5:6],
                                              op0=OP.add, op1=OP.mult))
            tln = emit(block.vector, _)

            if skip_t:
                return tln, [tln]
            xsrc = [X[:, c, k * 128:(k + 1) * 128] for k in range(8)]
            xdst = [XNT[:, k, c * 128:(c + 1) * 128] for k in range(8)]
            xev = emit_transposes(xsrc, ptile, xdst, pre_toks=(tln,))
            return tln, xev

        if "l0f" in phases:
            l0fctx = contextlib.ExitStack()
            EF = l0fctx.enter_context
            gateT = EF(nc.sbuf_tensor("gateT", [128, 8, NEXP], BF16))
            gstg = EF(nc.sbuf_tensor("gstg", [128, 8, NEXP], F32))
            scores = EF(nc.sbuf_tensor("scores", [128, NCH, NEXP], F32))
            Lall = EF(nc.sbuf_tensor("Lall", [128, NCH, NEXP], F32))
            Lwk = EF(nc.sbuf_tensor("Lwk", [128, NCH, NEXP], F32))
            rmax = EF(nc.sbuf_tensor("rmax", [128, 4, NCH], F32))
            mw1 = [EF(nc.sbuf_tensor(f"mw1_{i}", [128, 8, 512], BF16)) for i in range(2)]
            mw3 = [EF(nc.sbuf_tensor(f"mw3_{i}", [128, 8, 512], BF16)) for i in range(2)]
            mw2 = [EF(nc.sbuf_tensor(f"mw2_{i}", [128, 4, D], BF16)) for i in range(2)]
            act_bf = EF(nc.sbuf_tensor("act_bf", [128, 4, 512], BF16))
            sg_sb2 = EF(nc.sbuf_tensor("sg_sb2", [128, 4, 512], F32))

            def _(sy):
                wt(sy, *bar_l0a)
                return dW.sig(sy.dma_start(out=gstg[:, :, :],
                                           in_=gate_hbm.rearrange("(k p) e -> p k e", p=128)))
            tgl = emit(block.sync, _)

            def _(v):
                wt(v, tgl, *bar_l0a)
                return sV.sig(v.tensor_copy(gateT[:, :, :], gstg[:, :, :]))
            tgc = emit(block.vector, _)

            rt_toks = []

            def _(t):
                wt(t, tgc)
                for c in range(NCH):
                    for k in range(8):
                        i = t.matmul(pA[:, c * NEXP:(c + 1) * NEXP],
                                     XNT[:, k, c * 128:(c + 1) * 128],
                                     gateT[:, k, :], start=(k == 0), stop=(k == 7))
                    rt_toks.append(sT.sig(i))
                return rt_toks[-1]
            emit(block.tensor, _)

            def _(v):
                wt(v, rt_toks[-1])
                return sV.sig(v.tensor_copy(Lall[:, :, :],
                                            pA[:, 0:NCH * NEXP].rearrange(
                                                "p (c e) -> p c e", e=NEXP)))
            trl = emit(block.vector, _)

            def _(v):
                wt(v, trl)
                i = v.tensor_copy(Lwk[:, :, :], Lall[:, :, :])
                wt(v, sV.sig(i))
                for j in range(4):
                    i = v.tensor_reduce(rmax[:, j, :], Lwk[:, :, :], axis=AX.X,
                                        op=OP.max)
                    wt(v, sV.sig(i))
                    if j < 3:
                        i = v.tensor_tensor(qn[:, 0:NCH * NEXP].rearrange(
                                                "p (c e) -> p c e", e=NEXP),
                                            Lwk[:, :, :],
                                            rmax[:, j, :].unsqueeze(2).broadcast_to(
                                                [128, NCH, NEXP]),
                                            op=OP.is_ge)
                        wt(v, sV.sig(i))
                        i = v.scalar_tensor_tensor(Lwk[:, :, :],
                                                   qn[:, 0:NCH * NEXP].rearrange(
                                                       "p (c e) -> p c e", e=NEXP),
                                                   -1e30, Lwk[:, :, :],
                                                   op0=OP.mult, op1=OP.add)
                        wt(v, sV.sig(i))
                i = v.tensor_tensor(kn[:, 0:NCH * NEXP].rearrange(
                                        "p (c e) -> p c e", e=NEXP),
                                    Lall[:, :, :],
                                    rmax[:, 3, :].unsqueeze(2).broadcast_to(
                                        [128, NCH, NEXP]),
                                    op=OP.is_ge)
                i = v.tensor_scalar(rmax[:, 1, :], rmax[:, 0, :], -1.0, None,
                                    op0=OP.mult)
                return sV.sig(i)
            ttk = emit(block.vector, _)

            def _(v):
                wt(v, ttk)
                i = v.tensor_tensor(Lwk[:, :, :], Lall[:, :, :],
                                    rmax[:, 0, :].unsqueeze(2).broadcast_to(
                                        [128, NCH, NEXP]),
                                    op=OP.subtract)
                return sV.sig(i)
            tsub = emit(block.vector, _)

            def _(s):
                wt(s, tsub)
                return sS.sig(s.activation(Lwk[:, :, :], Lwk[:, :, :], ACT.Exp))
            texp = emit(block.scalar, _)

            def _(v):
                wt(v, texp)
                i = v.tensor_mul(Lwk[:, :, :], Lwk[:, :, :],
                                 kn[:, 0:NCH * NEXP].rearrange(
                                     "p (c e) -> p c e", e=NEXP))
                wt(v, sV.sig(i))
                i = v.tensor_reduce(rmax[:, 2, :], Lwk[:, :, :], axis=AX.X,
                                    op=OP.add)
                wt(v, sV.sig(i))
                i = v.reciprocal(rmax[:, 3, :], rmax[:, 2, :])
                wt(v, sV.sig(i))
                i = v.tensor_tensor(scores[:, :, :], Lwk[:, :, :],
                                    rmax[:, 3, :].unsqueeze(2).broadcast_to(
                                        [128, NCH, NEXP]),
                                    op=OP.mult)
                return sV.sig(i)
            tsc_done = emit(block.vector, _)

            wbuf_free = [None, None]
            cast_tok = [None, None]
            acc_tok = [None] * NCH
            prev_dma = tgl
            act_ev = [None, None]
            dn_ev = [None, None]

            for e in range(NEXP):
                b = e % 2

                def _(g, e=e, b=b):
                    wt(g, wbuf_free[b])
                    if e < 2:
                        wt(g, *bar_l0a)
                    for k in range(8):
                        dG.sig(g.dma_start(out=mw1[b][:, k, :],
                                           in_=moe_w1_hbm[e, k * 128:(k + 1) * 128, :]))
                        dG.throttle(g)
                    for k in range(8):
                        dG.sig(g.dma_start(out=mw3[b][:, k, :],
                                           in_=moe_w3_hbm[e, k * 128:(k + 1) * 128, :]))
                        dG.throttle(g)
                    for k in range(4):
                        dG.sig(g.dma_start(out=mw2[b][:, k, :],
                                           in_=moe_w2_hbm[e, k * 128:(k + 1) * 128, :]))
                        dG.throttle(g)
                    return dG.tok()
                cast_tok[b] = emit(block.gpsimd, _)

                last_dn_tok = None
                TG = 512 if T >= 512 else T
                SUBS = TG // 128
                GPS = (pA, pA, pC, pC)
                UPS = (pB, pB, pD, pD)
                for g in range(T // TG):
                    def _(t, g=g, e=e):
                        wt(t, cast_tok[b], tsc_done)
                        if e == 0 and g == 0:
                            wt(t, trl)
                        wt(t, act_ev[0], act_ev[1])
                        wt(t, dn_ev[0], dn_ev[1])
                        i = None
                        for it in range(4):
                            gp, up = GPS[it], UPS[it]
                            col = (it % 2) * 512
                            for k in range(8):
                                st, sp = (k == 0), (k == 7)
                                t.matmul(gp[:, col:col + TG],
                                         mw1[b][:, k, it * 128:(it + 1) * 128],
                                         XNT[:, k, g * TG:(g + 1) * TG],
                                         start=st, stop=sp)
                                i = t.matmul(up[:, col:col + TG],
                                             mw3[b][:, k, it * 128:(it + 1) * 128],
                                             XNT[:, k, g * TG:(g + 1) * TG],
                                             start=st, stop=sp)
                        return sT.sig(i)
                    tup = emit(block.tensor, _)

                    sg_toks = []

                    def _(s, tup=tup):
                        wt(s, tup)
                        for it in range(4):
                            gp = GPS[it]
                            col = (it % 2) * 512
                            sg_toks.append(sS.sig(s.activation(
                                sg_sb[:, it % 2, 0:TG] if False else sg_sb2[:, it, 0:TG],
                                gp[:, col:col + TG], ACT.Silu)))
                        return None
                    emit(block.scalar, _)

                    def _(v):
                        i = None
                        for it in range(4):
                            up = UPS[it]
                            col = (it % 2) * 512
                            wt(v, sg_toks[it])
                            i = v.tensor_mul(act_bf[:, it, 0:TG],
                                             sg_sb2[:, it, 0:TG],
                                             up[:, col:col + TG])
                        return sV.sig(i)
                    aev = emit(block.vector, _)
                    act_ev[0] = act_ev[1] = aev

                    for sub in range(SUBS):
                        ch = g * SUBS + sub
                        ps = pC if sub % 2 == 0 else pD
                        def _(t, sub=sub, g=g, ps=ps):
                            wt(t, act_ev[0])
                            wt(t, dn_ev[sub % 2])
                            i = None
                            for it in range(4):
                                st, sp = (it == 0), (it == 3)
                                t.matmul(ps[:, 0:512],
                                         act_bf[:, it, sub * 128:(sub + 1) * 128],
                                         mw2[b][:, it, 0:512], start=st, stop=sp)
                                i = t.matmul(ps[:, 512:1024],
                                             act_bf[:, it, sub * 128:(sub + 1) * 128],
                                             mw2[b][:, it, 512:1024], start=st, stop=sp)
                            return sT.sig(i)
                        tdn = emit(block.tensor, _)
                        last_dn_tok = tdn

                        def _(v, ch=ch, e=e, tdn=tdn, ps=ps):
                            wt(v, tdn, acc_tok[ch], tsc_done)
                            return sV.sig(v.scalar_tensor_tensor(
                                X[:, ch, :], ps[:, :], scores[:, ch, e:e + 1],
                                X[:, ch, :], op0=OP.mult, op1=OP.add))
                        acc_tok[ch] = emit(block.vector, _)
                        dn_ev[sub % 2] = acc_tok[ch]
                wbuf_free[b] = last_dn_tok

            for c in range(NCH):
                pre = [acc_tok[c]]
                if c == 0:
                    pre += [last_dn_tok]
                emit_ln_chunk(c, pre, pB if c % 2 else pA)
            l0fctx.close()

        bar_l0f = [(sT, sT.n), (sV, sV.n), (sS, sS.n)]

        if "l1a" in phases:
            S1 = float(np.float32(64.0 ** (-0.25)))
            S2 = float(np.float32(np.float64(64.0 ** (-0.25)) / np.float64(S1)))
            LN1E4 = np.float64(np.log(10000.0)) / 32.0
            E1 = float(np.float32(-LN1E4))
            E2 = float(np.float32(-LN1E4 - np.float64(np.float32(-LN1E4))))
            TWO_PI = 2.0 * np.pi
            C1 = float(np.float32(TWO_PI))
            C2 = float(np.float32(TWO_PI - np.float64(np.float32(TWO_PI))))
            INV2PI = float(np.float32(1.0 / TWO_PI))
            PI_LO = float(np.float32(3.14159250))
            PI2_HI = float(np.float32(np.pi / 2))
            PI2_LO = float(np.float32(np.pi / 2 - np.float64(np.float32(np.pi / 2))))

            l1ctx = contextlib.ExitStack()
            EL = l1ctx.enter_context
            wqkv = EL(nc.sbuf_tensor("wqkv", [128, 8, 3 * D], BF16))
            wout = EL(nc.sbuf_tensor("wout", [128, 8, D], BF16))

            cost = EL(nc.sbuf_tensor("cost", [128, NCH, 32], F32))
            sint = EL(nc.sbuf_tensor("sint", [128, NCH, 32], F32))
            invf = EL(nc.sbuf_tensor("invf", [128, 32], F32))
            pi2t = EL(nc.sbuf_tensor("pi2t", [128, 1], F32))
            UT = EL(nc.sbuf_tensor("UT", [128, 128], F32))
            xTl = EL(nc.sbuf_tensor("xTl", [128, 8, 128], BF16))
            vaug = EL(nc.sbuf_tensor("vaug", [128, H, 65], BF16))
            kTz = EL(nc.sbuf_tensor("kTz", [128, H, 128], BF16))
            qTp = EL(nc.sbuf_tensor("qTp", [128, 8, 128], BF16))
            ktk = EL(nc.sbuf_tensor("ktk", [128, H, 64], BF16))
            PTl = EL(nc.sbuf_tensor("PTl", [128, 8, 128], BF16))
            KVf = EL(nc.sbuf_tensor("KVf", [128, H, 65], F32))
            KVb = EL(nc.sbuf_tensor("KVb", [128, H, 65], BF16))
            rtmp = EL(nc.sbuf_tensor("rtmp", [128, 3, 512], F32))
            rrep = EL(nc.sbuf_tensor("rrep", [128, D], F32))
            trctx = contextlib.ExitStack()
            trig2 = trctx.enter_context(nc.sbuf_tensor("trig2", [128, NCH, 32], F32))
            rni = trctx.enter_context(nc.sbuf_tensor("rni", [128, NCH, 16], mybir.dt.int32))

            l1w_cast = [None]

            def _(g):
                wt(g, *bar_l0f)
                for q in range(8):
                    for j in range(3):
                        dG.sig(g.dma_start(
                            out=wqkv[:, q, j * 1024:(j + 1) * 1024],
                            in_=l1wqkv_hbm[q][:, j * 1024:(j + 1) * 1024]))
                        dG.throttle(g)
                for q in range(8):
                    dG.sig(g.dma_start(out=wout[:, q, :], in_=l1wout_hbm[q]))
                    dG.throttle(g)
                return dG.tok()
            l1w_cast[0] = emit(block.gpsimd, _)

            def _(sy):
                wt(sy, *bar_l0f)
                dC.sig(sy.dma_start(out=rrep[:, :], in_=rrep_hbm))
                return dC.sig(sy.dma_start(out=UT[:, :], in_=ut_hbm))
            ut_tok = emit(block.sync, _)

            def _(g):
                wt(g, *bar_l0f)
                g.iota(cost[:, :, 0:1], pattern=[[128, NCH]], base=0,
                       channel_multiplier=1,
                       allow_small_or_imprecise_dtypes=True)
                return sG.sig(g.iota(invf[0:1, :], pattern=[[1, 32]], base=0,
                                     channel_multiplier=0,
                                     allow_small_or_imprecise_dtypes=True))
            iota_tok = emit(block.gpsimd, _)

            def _(s):
                wt(s, iota_tok)
                i = s.activation(trig2[0:1, 0, 0:32], invf[0:1, :], ACT.Exp, scale=E1)
                wt(s, sS.sig(i))
                return sS.sig(s.activation(invf[0:1, :], invf[0:1, :], ACT.Exp,
                                           scale=E2))
            tinv1 = emit(block.scalar, _)

            def _(v):
                wt(v, tinv1)
                i = v.tensor_mul(invf[0:1, :], invf[0:1, :], trig2[0:1, 0, 0:32])
                wt(v, sV.sig(i))
                i = v.memset(pi2t[:, :], PI2_HI)
                wt(v, sV.sig(i))
                return sV.sig(v.tensor_scalar(pi2t[:, :], pi2t[:, :], PI2_LO,
                                              None, op0=OP.add))
            tinv2 = emit(block.vector, _)

            def _(t):
                wt(t, tinv2, ut_tok)
                return sT.sig(t.matmul(pA[:, 0:32], UT[0:1, :],
                                       invf[0:1, :], start=True, stop=True))
            tbmm = emit(block.tensor, _)

            def _(v):
                wt(v, tbmm)
                return sV.sig(v.tensor_copy(invf[:, :], pA[:, 0:32]))
            tbc = emit(block.vector, _)

            def _(v):
                wt(v, tbc, iota_tok)
                i = v.tensor_tensor(trig2[:, :, :],
                                    cost[:, :, 0:1].broadcast_to([128, NCH, 32]),
                                    invf[:, :].unsqueeze(1).broadcast_to([128, NCH, 32]),
                                    op=OP.mult)
                wt(v, sV.sig(i))
                i = v.tensor_scalar(cost[:, :, :], trig2[:, :, :], INV2PI, 0.5,
                                    op0=OP.mult, op1=OP.add)
                return sV.sig(i)
            tph = emit(block.vector, _)

            def _(v):
                wt(v, tph)
                for h in range(2):
                    i = v.tensor_copy(rni[:, :, :], cost[:, :, h * 16:h * 16 + 16])
                    wt(v, sV.sig(i))
                    i = v.tensor_copy(cost[:, :, h * 16:h * 16 + 16], rni[:, :, :])
                    wt(v, sV.sig(i))
                i = v.scalar_tensor_tensor(trig2[:, :, :], cost[:, :, :], -C1,
                                           trig2[:, :, :], op0=OP.mult, op1=OP.add)
                wt(v, sV.sig(i))
                i = v.scalar_tensor_tensor(trig2[:, :, :], cost[:, :, :], -C2,
                                           trig2[:, :, :], op0=OP.mult, op1=OP.add)
                wt(v, sV.sig(i))
                i = v.tensor_scalar(cost[:, :, :], trig2[:, :, :], PI_LO, None,
                                    op0=OP.is_gt)
                wt(v, sV.sig(i))
                i = v.scalar_tensor_tensor(trig2[:, :, :], cost[:, :, :], -C1,
                                           trig2[:, :, :], op0=OP.mult, op1=OP.add)
                wt(v, sV.sig(i))
                i = v.scalar_tensor_tensor(trig2[:, :, :], cost[:, :, :], -C2,
                                           trig2[:, :, :], op0=OP.mult, op1=OP.add)
                wt(v, sV.sig(i))
                i = v.tensor_scalar(cost[:, :, :], trig2[:, :, :], -PI_LO, None,
                                    op0=OP.is_lt)
                wt(v, sV.sig(i))
                i = v.scalar_tensor_tensor(trig2[:, :, :], cost[:, :, :], C1,
                                           trig2[:, :, :], op0=OP.mult, op1=OP.add)
                wt(v, sV.sig(i))
                i = v.scalar_tensor_tensor(trig2[:, :, :], cost[:, :, :], C2,
                                           trig2[:, :, :], op0=OP.mult, op1=OP.add)
                wt(v, sV.sig(i))
                i = v.tensor_scalar(trig2[:, :, :], trig2[:, :, :], PI_LO, None,
                                    op0=OP.min)
                wt(v, sV.sig(i))
                i = v.tensor_scalar(trig2[:, :, :], trig2[:, :, :], -PI_LO, None,
                                    op0=OP.max)
                return sV.sig(i)
            tred = emit(block.vector, _)

            def _(s):
                wt(s, tred)
                s.activation(sint[:, :, :], trig2[:, :, :], ACT.Sin)
                i = s.activation(cost[:, :, :], trig2[:, :, :], ACT.Abs)
                return sS.sig(i)
            tsin = emit(block.scalar, _)

            def _(s):
                wt(s, tsin)
                return sS.sig(s.activation(cost[:, :, :], cost[:, :, :],
                                           ACT.Sin, bias=pi2t[:, :], scale=-1.0))
            tcos = emit(block.scalar, _)

            def _(v):
                wt(v, tcos)
                i = v.memset(KVf[:, :, :], 0.0)
                wt(v, sV.sig(i))
                i = v.memset(KVb[:, :, :], 0.0)
                wt(v, sV.sig(i))
                i = v.memset(vaug[:, :, 64:65], 1.0)
                wt(v, sV.sig(i))
                return sV.sig(v.memset(kTz[:, :, :], 0.0))
            init_tok = emit(block.vector, _)

            st_l1 = {"pv_evac": None, "kv_cast": init_tok, "wo": None,
                     "att_V": None, "att_S": None, "ptev": None,
                     "kvup": None}

            for c in range(NCH):
                xsrc = [X[:, c, k * 128:(k + 1) * 128] for k in range(8)]
                xdst = [xTl[:, k, :] for k in range(8)]
                xev = emit_transposes(xsrc, pD, xdst,
                                      pre_toks=(st_l1["pv_evac"], init_tok))

                def _(t, c=c):
                    wt(t, l1w_cast[0], st_l1["att_V"], st_l1["att_S"])
                    for k in range(8):
                        wt(t, xev[k])
                        st, sp = (k == 0), (k == 7)
                        for half in range(2):
                            t.matmul(pA[:, half * 512:(half + 1) * 512],
                                     xTl[:, k, :], wqkv[:, k, half * 512:(half + 1) * 512],
                                     start=st, stop=sp)
                            t.matmul(pB[:, half * 512:(half + 1) * 512],
                                     xTl[:, k, :], wqkv[:, k, D + half * 512:D + (half + 1) * 512],
                                     start=st, stop=sp)
                            i = t.matmul(pC[:, half * 512:(half + 1) * 512],
                                         xTl[:, k, :], wqkv[:, k, 2 * D + half * 512:2 * D + (half + 1) * 512],
                                         start=st, stop=sp)
                    return sT.sig(i)
                tqkv = emit(block.tensor, _)

                def _(s, c=c):
                    wt(s, tqkv)
                    return sS.sig(s.copy(vaug[:, :, 0:64],
                                         pC[:, :].rearrange("p (h d) -> p h d", h=H)))
                tvev = emit(block.scalar, _)

                rp_toks = []
                for (srcp, dst) in ((pA, qn), (pB, kn)):
                    def _(v, c=c, srcp=srcp, dst=dst):
                        wt(v, tqkv, rp_toks[-1] if rp_toks else None)
                        s3 = srcp[:, :].rearrange("p (h d) -> p h d", h=H)
                        d3 = dst[:, :].rearrange("p (h d) -> p h d", h=H)
                        cosb = cost[:, c, :].unsqueeze(1).broadcast_to([128, H, 32])
                        sinb = sint[:, c, :].unsqueeze(1).broadcast_to([128, H, 32])
                        c1t = rtmp[:, 0, :].rearrange("p (h x) -> p h x", h=H)
                        s1t = rtmp[:, 1, :].rearrange("p (h x) -> p h x", h=H)
                        t2t = rtmp[:, 2, :].rearrange("p (h x) -> p h x", h=H)
                        v.tensor_tensor(c1t, s3[:, :, 0:32], cosb, op=OP.mult)
                        v.tensor_tensor(s1t, s3[:, :, 0:32], sinb, op=OP.mult)
                        v.tensor_tensor(d3[:, :, 32:64], s3[:, :, 32:64], cosb, op=OP.mult)
                        i = v.tensor_tensor(t2t, s3[:, :, 32:64], sinb, op=OP.mult)
                        wt(v, sV.sig(i))
                        v.tensor_add(d3[:, :, 32:64], d3[:, :, 32:64], s1t)
                        i = v.tensor_sub(d3[:, :, 0:32], c1t, t2t)
                        wt(v, sV.sig(i))
                        i = v.tensor_scalar(dst[:, :], dst[:, :], S1, S2,
                                            op0=OP.mult, op1=OP.mult)
                        wt(v, sV.sig(i))
                        i = v.tensor_scalar(rtmp[:, 0:2, :].rearrange("p a x -> p (a x)"),
                                            dst[:, :], 0.0, None, op0=OP.min)
                        wt(v, sV.sig(i))
                        return sV.sig(v.tensor_scalar(dst[:, :], dst[:, :], 0.0,
                                                      None, op0=OP.max))
                    trelu = emit(block.vector, _)

                    def _(s, trelu=trelu):
                        wt(s, trelu)
                        return sS.sig(s.activation(
                            rtmp[:, 0:2, :].rearrange("p a x -> p (a x)"),
                            rtmp[:, 0:2, :].rearrange("p a x -> p (a x)"), ACT.Exp))
                    texp = emit(block.scalar, _)

                    def _(v, dst=dst, texp=texp):
                        wt(v, texp)
                        i = v.tensor_add(dst[:, :], dst[:, :],
                                         rtmp[:, 0:2, :].rearrange("p a x -> p (a x)"))
                        return sV.sig(i)
                    rp_toks.append(emit(block.vector, _))
                trp = rp_toks[-1]

                def _(v):
                    wt(v, trp, st_l1["kvup"])
                    return sV.sig(v.tensor_copy(
                        ktk[:, :, :], kn[:, :].rearrange("p (h d) -> p h d", h=H)))
                tktk = emit(block.vector, _)

                qsrc = [qn[:, k * 128:(k + 1) * 128] for k in range(8)]
                qdst = [qTp[:, k, :] for k in range(8)]
                qev = emit_transposes(qsrc, pA, qdst, pre_toks=(trp,))
                ksrc = [kn[:, k * 128:(k + 1) * 128] for k in range(8)]
                kdst = [kTz[:, 2 * k, :] for k in range(8)]
                kev = emit_transposes(ksrc, pB, kdst, pre_toks=(trp,),
                                      qtz_split=kTz)

                pv_toks = []
                for g in range(2):
                    def _(t, g=g, c=c):
                        for tk in qev + kev:
                            wt(t, tk)
                        wt(t, st_l1["ptev"], st_l1["kv_cast"], tvev)
                        if g == 1:
                            wt(t, pv_toks[0])
                        i = None
                        for hh in range(8):
                            h = g * 8 + hh
                            i = t.matmul(pA[:, hh * 128:(hh + 1) * 128],
                                         kTz[:, h, :], qTp[:, h // 2, :],
                                         start=True, stop=True)
                        return sT.sig(i)
                    tat = emit(block.tensor, _)

                    def _(v, g=g, tat=tat):
                        wt(v, tat, ut_tok)
                        i = v.tensor_tensor(
                            PTl[:, :, :],
                            pA[:, 0:1024].rearrange("p (a b) -> p a b", a=8),
                            UT[:, :].unsqueeze(1).broadcast_to([128, 8, 128]),
                            op=OP.mult)
                        return sV.sig(i)
                    tmask = emit(block.vector, _)
                    st_l1["ptev"] = tmask

                    def _(t, g=g, c=c, tmask=tmask):
                        wt(t, tmask)
                        ps = pD if g == 0 else pC
                        i = None
                        for hh in range(8):
                            h = g * 8 + hh
                            col = (hh // 4) * 512 + (hh % 4) * 66
                            t.matmul(ps[:, col:col + 65], PTl[:, hh, :],
                                     vaug[:, h, :], start=True, stop=False)
                            i = t.matmul(ps[:, col:col + 65], qTp[:, h // 2, :],
                                         KVb[:, h, :], start=False, stop=True)
                        return sT.sig(i)
                    pv_toks.append(emit(block.tensor, _))

                def _(t, c=c):
                    wt(t, tktk, pv_toks[0], pv_toks[1])
                    for tk in kev:
                        wt(t, tk)
                    i = None
                    for h in range(H):
                        sl = h // 2
                        po = (h % 2) * 64
                        col = (sl // 4) * 512 + (sl % 4) * 66
                        i = t.matmul(pB[po:po + 64, col:col + 65],
                                     ktk[:, h, :], vaug[:, h, :],
                                     start=True, stop=True,
                                     tile_position=(0, po))
                    return sT.sig(i)
                tkv = emit(block.tensor, _)
                st_l1["kvup"] = tkv

                def _(v, tkv=tkv):
                    wt(v, tkv, st_l1["kv_cast"])
                    i = None
                    for half in range(2):
                        for po in range(2):
                            hs = half * 8 + po
                            he = min(hs + 8, H)
                            i = v.tensor_add(
                                KVf[po * 64:po * 64 + 64, hs:he:2, :],
                                KVf[po * 64:po * 64 + 64, hs:he:2, :],
                                pB[po * 64:po * 64 + 64,
                                   half * 512:half * 512 + 264].rearrange(
                                    "p (s e) -> p s e", e=66)[:, :, 0:65])
                    wt(v, sV.sig(i))
                    return sV.sig(v.tensor_copy(KVb[:, :, :], KVf[:, :, :]))
                st_l1["kv_cast"] = emit(block.vector, _)

                def _(v):
                    wt(v, pv_toks[0], pv_toks[1])
                    i = None
                    for g4 in range(4):
                        ps = pD if g4 < 2 else pC
                        col = (g4 % 2) * 512
                        i = v.tensor_copy(nrm[:, 6, g4 * 4:(g4 + 1) * 4],
                                          ps[:, col:col + 264].rearrange(
                                              "p (h e) -> p h e", h=4)[:, :, 64])
                    wt(v, sV.sig(i))
                    i = v.tensor_scalar(nrm[:, 6, :], nrm[:, 6, :], 1e-6, None,
                                        op0=OP.max)
                    wt(v, sV.sig(i))
                    i = v.reciprocal(nrm[:, 7, :], nrm[:, 6, :])
                    wt(v, sV.sig(i))
                    i = None
                    for g4 in range(4):
                        ps = pD if g4 < 2 else pC
                        col = (g4 % 2) * 512
                        i = v.tensor_tensor(
                            qn[:, :].rearrange("p (h d) -> p h d", h=H)[:, g4 * 4:(g4 + 1) * 4, :],
                            ps[:, col:col + 264].rearrange("p (h e) -> p h e", h=4)[:, :, 0:64],
                            nrm[:, 7, g4 * 4:(g4 + 1) * 4].unsqueeze(2).broadcast_to([128, 4, HD]),
                            op=OP.mult)
                    return sV.sig(i)
                tpvev = emit(block.vector, _)
                st_l1["pv_evac"] = tpvev

                aosrc = [qn[:, k * 128:(k + 1) * 128] for k in range(8)]
                aodst = [xTl[:, k, :] for k in range(8)]
                aoev = emit_transposes(aosrc, pB, aodst,
                                       pre_toks=(tpvev, st_l1["kv_cast"]),
                                       evac="scalar")

                def _(t, c=c):
                    wt(t, st_l1["wo"])
                    for k in range(8):
                        wt(t, aoev[k])
                        st, sp = (k == 0), (k == 7)
                        t.matmul(pA[:, 0:512], xTl[:, k, :], wout[:, k, 0:512],
                                 start=st, stop=sp)
                        i = t.matmul(pA[:, 512:1024], xTl[:, k, :],
                                     wout[:, k, 512:1024], start=st, stop=sp)
                    return sT.sig(i)
                twoT = emit(block.tensor, _)

                def _(v, c=c):
                    wt(v, twoT)
                    return sV.sig(v.tensor_add(X[:, c, :], X[:, c, :], pA[:, :]))
                tres = emit(block.vector, _)
                st_l1["wo"] = tres

                tln, _ = emit_ln_chunk(c, (tres,), pC, skip_t=True)

                tsv2_pred = (sV, sV.n + 2)

                def _(s, tln=tln):
                    wt(s, tln)
                    i = s.activation(kn[:, :], X[:, c, :], ACT.Square,
                                     accum_out=lnm[:, 0:1])
                    wt(s, sS.sig(i))
                    wt(s, tsv2_pred)
                    i = s.activation(lnm[:, 1:2], lnm[:, 0:1], ACT.Ln)
                    wt(s, sS.sig(i))
                    return sS.sig(s.activation(lnm[:, 2:3], lnm[:, 1:2], ACT.Exp,
                                               scale=-0.5, bias=cst_sm[:, 1:2]))
                tn2 = emit(block.scalar, _)
                tss2 = (sS, tn2[1] - 2)

                def _(v, c=c, tss2=tss2):
                    wt(v, tss2, ut_tok)
                    i = v.tensor_mul(kn[:, :], X[:, c, :], rrep[:, :])
                    wt(v, sV.sig(i))
                    i = sV.sig(v.tensor_reduce(svals[:, c:c + 1], kn[:, :],
                                               axis=AX.X, op=OP.add))
                    assert i == tsv2_pred, (i, tsv2_pred)
                    return i
                tsv2 = emit(block.vector, _)

                def _(v, c=c, tn2=tn2):
                    wt(v, tn2)
                    i = v.tensor_scalar(lnm[:, 2:3], lnm[:, 2:3], 3.2e13, None,
                                        op0=OP.min)
                    wt(v, sV.sig(i))
                    return sV.sig(v.tensor_scalar(qn[:, :], X[:, c, :],
                                                  lnm[:, 2:3], None, op0=OP.mult))
                txn2 = emit(block.vector, _)

                xsrc2 = [qn[:, k * 128:(k + 1) * 128] for k in range(8)]
                xdst2 = [XNT[:, k, c * 128:(c + 1) * 128] for k in range(8)]
                x3ev = emit_transposes(xsrc2, pC, xdst2, pre_toks=(txn2,))
                rms_sv_tok[c] = x3ev[-1]
                st_l1["att_V"] = x3ev[-1]
                st_l1["att_S"] = tn2
                if c == 0:
                    trctx.close()
            l1ctx.close()

        bar_l1a = [(sT, sT.n), (sV, sV.n), (sS, sS.n)]

        if "l1f" in phases:
            N_ITERS = 50
            EFF_K = min(1024.0 * 9.0 / 8.0, float(T))
            LOGK = float(np.float32(np.log(np.float64(EFF_K))))
            A1 = float(np.float32(np.log(np.float64(EFF_K)) - np.log(np.float64(T))))
            NHEAVY = min(1024, T)

            fctx = contextlib.ExitStack()
            EF = fctx.enter_context
            ONES = EF(nc.sbuf_tensor("ONES", [128, 128], F32))
            UT2 = EF(nc.sbuf_tensor("UT2", [128, 128], F32))
            lgam = EF(nc.sbuf_tensor("lgam", [128, 8], F32))
            hgam = EF(nc.sbuf_tensor("hgam", [128, 8], F32))
            lw1 = EF(nc.sbuf_tensor("lw1", [128, 8, 512], BF16))
            lw2 = EF(nc.sbuf_tensor("lw2", [128, 4, D], BF16))
            lb1 = EF(nc.sbuf_tensor("lb1", [1, 512], BF16))
            lb2 = EF(nc.sbuf_tensor("lb2", [1, D], BF16))
            hb2 = EF(nc.sbuf_tensor("hb2", [1, D], BF16))
            hw1q = [EF(nc.sbuf_tensor(f"hw1q{i}", [128, 8, 1024], BF16)) for i in range(2)]
            hw2q = [EF(nc.sbuf_tensor(f"hw2q{i}", [128, 8, 1024], BF16)) for i in range(2)]
            hb1q = EF(nc.sbuf_tensor("hb1q", [1, 1024], BF16))
            onesr = EF(nc.sbuf_tensor("onesr", [1, 512], BF16))
            hh_bf = EF(nc.sbuf_tensor("hh_bf", [128, 8, 512], BF16))
            hl_bf = hh_bf
            fmeta = EF(nc.sbuf_tensor("fmeta", [128, 8], F32))
            sat = EF(nc.sbuf_tensor("sat", [128, NCH], F32))
            hmask = EF(nc.sbuf_tensor("hmask", [128, NCH], F32))
            pcol = EF(nc.sbuf_tensor("pcol", [128, NCH], F32))

            def _(sy):
                wt(sy, *bar_l1a)
                dC.sig(sy.dma_start(out=ONES[:, :], in_=ones_hbm))
                dC.sig(sy.dma_start(out=UT2[:, :], in_=ut_hbm))
                dC.sig(sy.dma_start(out=lgam[:, :], in_=lgam_hbm))
                return dC.sig(sy.dma_start(out=hgam[:, :], in_=hgam_hbm))
            cst2 = emit(block.sync, _)

            def _(g):
                wt(g, *bar_l1a)
                for k in range(8):
                    dG.sig(g.dma_start(out=lw1[:, k, :], in_=lw1_hbm[k]))
                    dG.throttle(g)
                for k in range(4):
                    dG.sig(g.dma_start(out=lw2[:, k, :], in_=lw2_hbm[k]))
                    dG.throttle(g)
                return dG.tok()
            lwd = emit(block.gpsimd, _)

            def _(v):
                wt(v, lwd, cst2, *bar_l1a)
                i = None
                for k in range(8):
                    i = v.tensor_scalar(lw1[:, k, :], lw1[:, k, :],
                                        lgam[:, k:k + 1], None, op0=OP.mult)
                return sV.sig(i)
            ltoks = [emit(block.vector, _)]

            def _(g):
                wt(g, *bar_l1a)
                dG.sig(g.dma_start(out=lb1[0:1, :], in_=lb1_hbm))
                dG.throttle(g)
                dG.sig(g.dma_start(out=lb2[0:1, :], in_=lb2_hbm))
                dG.throttle(g)
                dG.sig(g.dma_start(out=hb2[0:1, :], in_=hb2_hbm))
                dG.throttle(g)
                return dG.tok()
            btok = emit(block.gpsimd, _)

            def _(v):
                wt(v, btok)
                return sV.sig(v.memset(onesr[0:1, :], 1.0))
            bcast_tok = emit(block.vector, _)

            rms_toks = [rms_sv_tok[NCH - 1]]
            sv_tok = rms_sv_tok[NCH - 1]

            def _(v):
                wt(v, sv_tok)
                return sV.sig(v.memset(fmeta[:, 0:1], A1))
            atok = emit(block.vector, _)

            desc_psum_free = None
            for it in range(N_ITERS - 1):
                def _(v, atok=atok):
                    wt(v, atok)
                    return sV.sig(v.tensor_scalar(kn[:, 0:NCH], svals[:, 0:NCH],
                                                  fmeta[:, 0:1], 0.0,
                                                  op0=OP.add, op1=OP.min))
                t1 = emit(block.vector, _)

                t4_pred = (sV, sV.n + 1)

                def _(s, t1=t1):
                    wt(s, t1)
                    e = sS.sig(s.activation(kn[:, 0:NCH], kn[:, 0:NCH], ACT.Exp))
                    wt(s, t4_pred)
                    return e, sS.sig(s.activation(fmeta[:, 2:3], fmeta[:, 1:2],
                                                  ACT.Ln))
                t2, t5 = emit(block.scalar, _)

                def _(t, t2=t2, it=it):
                    wt(t, t2, cst2)
                    if it == 0:
                        wt(t, rms_toks[-1])
                    return sT.sig(t.matmul(pD[:, 0:NCH], ONES[:, :], kn[:, 0:NCH],
                                           start=True, stop=True))
                t3 = emit(block.tensor, _)

                def _(v, t3=t3):
                    wt(v, t3)
                    i = sV.sig(v.tensor_reduce(fmeta[:, 1:2], pD[:, 0:NCH],
                                               axis=AX.X, op=OP.add))
                    assert i == t4_pred, (i, t4_pred)
                    return i
                t4 = emit(block.vector, _)

                def _(v, t5=t5):
                    wt(v, t5)
                    i = v.scalar_tensor_tensor(fmeta[:, 3:4], fmeta[:, 2:3], -1.0,
                                               fmeta[:, 0:1], op0=OP.mult, op1=OP.add)
                    wt(v, sV.sig(i))
                    return sV.sig(v.tensor_scalar(fmeta[:, 0:1], fmeta[:, 3:4],
                                                  LOGK, None, op0=OP.add))
                atok = emit(block.vector, _)

            def _(v):
                wt(v, atok)
                i = v.tensor_scalar(fmeta[:, 4:5], fmeta[:, 0:1], -1.0, None,
                                    op0=OP.mult)
                wt(v, sV.sig(i))
                return sV.sig(v.tensor_scalar(sat[:, :], svals[:, 0:NCH],
                                              fmeta[:, 4:5], None, op0=OP.is_ge))
            sat_tok = emit(block.vector, _)

            def _(t):
                wt(t, sat_tok)
                t.matmul(pD[:, 0:NCH], UT2[:, :], sat[:, :], start=True, stop=True)
                return sT.sig(t.matmul(pD[:, 512:512 + NCH], ONES[:, :], sat[:, :],
                                       start=True, stop=True))
            pre_tok = emit(block.tensor, _)

            def _(v):
                wt(v, pre_tok)
                i = v.tensor_copy(pcol[:, :], pD[:, 0:NCH])
                wt(v, sV.sig(i))
                i = v.tensor_copy(fmeta[:, 5:6], pD[:, 512:513])
                wt(v, sV.sig(i))
                for c in range(1, NCH):
                    i = v.tensor_add(pcol[:, c:c + 1], pcol[:, c:c + 1],
                                     fmeta[:, 5:6])
                    wt(v, sV.sig(i))
                    if c < NCH - 1:
                        i = v.tensor_tensor(fmeta[:, 5:6], fmeta[:, 5:6],
                                            pD[:, 512 + c:512 + c + 1], op=OP.add)
                        wt(v, sV.sig(i))
                return sV.tok()
            px_tok = emit(block.vector, _)

            def _(v):
                wt(v, px_tok)
                if NHEAVY >= T:
                    return sV.sig(v.memset(hmask[:, :], 1.0))
                i = v.tensor_scalar(kn[:, 0:NCH], pcol[:, :], float(NHEAVY), None,
                                    op0=OP.is_le)
                wt(v, sV.sig(i))
                return sV.sig(v.tensor_mul(hmask[:, :], kn[:, 0:NCH], sat[:, :]))
            hm_tok = emit(block.vector, _)

            acc_tok2 = [None] * NCH
            dn_free = [None, None]
            up_free = None
            TG = 512 if T >= 512 else T
            SUBS = TG // 128
            for g in range(T // TG):
                def _(t, g=g):
                    wt(t, ltoks[-1], bcast_tok, rms_toks[-1], up_free)
                    if g == 0:
                        wt(t, hm_tok)
                    i = None
                    for ii in range(4):
                        ps = (pA, pB)[ii // 2]
                        col = (ii % 2) * 512
                        for k in range(8):
                            i = t.matmul(ps[:, col:col + TG],
                                         lw1[:, k, ii * 128:(ii + 1) * 128],
                                         XNT[:, k, g * TG:(g + 1) * TG],
                                         start=(k == 0), stop=False)
                        i = t.matmul(ps[:, col:col + TG],
                                     lb1[0:1, ii * 128:(ii + 1) * 128],
                                     onesr[0:1, 0:TG], start=False, stop=True)
                    return sT.sig(i)
                tup = emit(block.tensor, _)

                def _(s, g=g, tup=tup):
                    wt(s, tup)
                    i = None
                    for ii in range(4):
                        ps = (pA, pB)[ii // 2]
                        col = (ii % 2) * 512
                        i = s.activation(hl_bf[:, ii, 0:TG], ps[:, col:col + TG],
                                         ACT.Gelu)
                    return sS.sig(i)
                tgl = emit(block.scalar, _)
                up_free = tgl

                for sub in range(SUBS):
                    ch = g * SUBS + sub
                    ps = pC if sub % 2 == 0 else pD
                    def _(t, sub=sub, ps=ps, tgl=tgl):
                        wt(t, tgl, dn_free[sub % 2])
                        i = None
                        for ii in range(4):
                            t.matmul(ps[:, 0:512],
                                     hl_bf[:, ii, sub * 128:(sub + 1) * 128],
                                     lw2[:, ii, 0:512], start=(ii == 0), stop=False)
                            i = t.matmul(ps[:, 512:1024],
                                         hl_bf[:, ii, sub * 128:(sub + 1) * 128],
                                         lw2[:, ii, 512:1024], start=(ii == 0),
                                         stop=False)
                        t.matmul(ps[:, 0:512], onesr[0:1, sub * 128:sub * 128 + 128],
                                 lb2[0:1, 0:512], start=False, stop=True)
                        i = t.matmul(ps[:, 512:1024],
                                     onesr[0:1, sub * 128:sub * 128 + 128],
                                     lb2[0:1, 512:1024], start=False, stop=True)
                        return sT.sig(i)
                    tdn = emit(block.tensor, _)

                    def _(v, ch=ch, tdn=tdn, ps=ps):
                        wt(v, tdn, acc_tok2[ch])
                        return sV.sig(v.tensor_add(X[:, ch, :], X[:, ch, :],
                                                   ps[:, :]))
                    acc_tok2[ch] = emit(block.vector, _)
                    dn_free[sub % 2] = acc_tok2[ch]

            hacc = list(acc_tok2)
            hq_prev_mm = None
            hbuf_free = [None, None]
            hup_free = [up_free] * 4
            hdn_free = [None, None]
            for q in range(4):
                qb = q % 2
                def _(g, q=q, qb=qb):
                    wt(g, hbuf_free[qb], cst2)
                    for k in range(8):
                        dG.sig(g.dma_start(
                            out=hw1q[qb][:, k, :],
                            in_=hw1_hbm[k * 128:(k + 1) * 128,
                                        q * 1024:(q + 1) * 1024]))
                        dG.throttle(g)
                    for k in range(8):
                        dG.sig(g.dma_start(
                            out=hw2q[qb][:, k, :],
                            in_=hw2_hbm[q * 1024 + k * 128:q * 1024 + (k + 1) * 128, :]))
                        dG.throttle(g)
                    return dG.tok()
                hwd = emit(block.gpsimd, _)

                def _(v, q=q, qb=qb):
                    wt(v, hwd, hbuf_free[qb], cst2)
                    i = None
                    for k in range(8):
                        i = v.tensor_scalar(hw1q[qb][:, k, :], hw1q[qb][:, k, :],
                                            hgam[:, k:k + 1], None, op0=OP.mult)
                    return sV.sig(i)
                htoks = [emit(block.vector, _)]

                def _(g, q=q):
                    wt(g, htoks[-1], hq_prev_mm)
                    i = dG.sig(g.dma_start(
                        out=hb1q[0:1, :],
                        in_=hb1_hbm[0:1, q * 1024:(q + 1) * 1024]))
                    dG.throttle(g)
                    return i
                hbtok = emit(block.gpsimd, _)

                last_dn = None
                for g in range(T // TG):
                    PSL = (pA, pA, pB, pB, pC, pC, pD, pD)

                    def _(t, q=q, g=g, hbtok=hbtok):
                        wt(t, htoks[-1], hbtok, *hup_free)
                        wt(t, hdn_free[0], hdn_free[1])
                        i = None
                        for ii in range(8):
                            ps = PSL[ii]
                            col = (ii % 2) * 512
                            for k in range(8):
                                t.matmul(ps[:, col:col + TG],
                                         hw1q[qb][:, k, ii * 128:(ii + 1) * 128],
                                         XNT[:, k, g * TG:(g + 1) * TG],
                                         start=(k == 0), stop=False)
                            i = t.matmul(ps[:, col:col + TG],
                                         hb1q[0:1, ii * 128:(ii + 1) * 128],
                                         onesr[0:1, 0:TG], start=False, stop=True)
                        return sT.sig(i)
                    tuph = emit(block.tensor, _)

                    def _(s, tuph=tuph):
                        wt(s, tuph)
                        i = None
                        for ii in range(8):
                            ps = PSL[ii]
                            col = (ii % 2) * 512
                            i = s.activation(hh_bf[:, ii, 0:TG], ps[:, col:col + TG],
                                             ACT.Gelu)
                        return sS.sig(i)
                    tglh = emit(block.scalar, _)
                    hup_free = [tglh] * 4

                    for sub in range(SUBS):
                        ch = g * SUBS + sub
                        ps = pC if sub % 2 == 0 else pD

                        def _(t, q=q, sub=sub, ps=ps, tglh=tglh):
                            wt(t, tglh, hdn_free[sub % 2])
                            i = None
                            for ii in range(8):
                                t.matmul(ps[:, 0:512],
                                         hh_bf[:, ii, sub * 128:(sub + 1) * 128],
                                         hw2q[qb][:, ii, 0:512], start=(ii == 0),
                                         stop=(ii == 7 and q != 0))
                                i = t.matmul(ps[:, 512:1024],
                                             hh_bf[:, ii, sub * 128:(sub + 1) * 128],
                                             hw2q[qb][:, ii, 512:1024], start=(ii == 0),
                                             stop=(ii == 7 and q != 0))
                            if q == 0:
                                t.matmul(ps[:, 0:512],
                                         onesr[0:1, sub * 128:sub * 128 + 128],
                                         hb2[0:1, 0:512], start=False, stop=True)
                                i = t.matmul(ps[:, 512:1024],
                                             onesr[0:1, sub * 128:sub * 128 + 128],
                                             hb2[0:1, 512:1024], start=False, stop=True)
                            return sT.sig(i)
                        tdnh = emit(block.tensor, _)
                        last_dn = tdnh

                        def _(v, ch=ch, tdnh=tdnh, ps=ps):
                            wt(v, tdnh, hacc[ch], hm_tok)
                            return sV.sig(v.scalar_tensor_tensor(
                                X[:, ch, :], ps[:, :], hmask[:, ch:ch + 1],
                                X[:, ch, :], op0=OP.mult, op1=OP.add))
                        hacc[ch] = emit(block.vector, _)
                        hdn_free[sub % 2] = hacc[ch]
                hq_prev_mm = last_dn
                hbuf_free[qb] = last_dn

            ln4_tok = [None] * NCH
            for c in range(NCH):
                pre = [hacc[c]]
                if c == 0:
                    pre.append(hq_prev_mm)
                ln4_tok[c], _ = emit_ln_chunk(c, pre, pA if c % 2 else pB,
                                              skip_t=True)
            fctx.close()



        def _(g):
            for c in range(NCH):
                if "l1f" in phases:
                    wt(g, ln4_tok[c])
                else:
                    wt(g, (sV, sV.n), (sS, sS.n), (sT, sT.n))
                dX.sig(g.dma_start(out=y_hbm[c * 128:(c + 1) * 128, :],
                                   in_=X[:, c, :]))
            if DBG:
                wt(g, (sV, sV.n), (sS, sS.n), (sT, sT.n))
                dX.sig(g.dma_start(out=dbg_hbm["qn"], in_=qn[:, :]))
                dX.sig(g.dma_start(out=dbg_hbm["PT"],
                                   in_=PT[:, :, :, :].rearrange("p a b c -> p (a b c)")))
                dX.sig(g.dma_start(out=dbg_hbm["kT"],
                                   in_=kT[(NCH - 1) % 2][:, :, :].rearrange("p a b -> p (a b)")))
                dX.sig(g.dma_start(out=dbg_hbm["qT"],
                                   in_=qT[(NCH - 1) % 2][:, :, :].rearrange("p a b -> p (a b)")))
                dX.sig(g.dma_start(out=dbg_hbm["nrm"],
                                   in_=nrm[:, :, :].rearrange("p a b -> p (a b)")))
                dX.sig(g.dma_start(out=dbg_hbm["vbf"],
                                   in_=vbf[(NCH - 1) % 2][:, :, :].rearrange("p a b -> p (a b)")))
            g.wait_ge(dX.h, dX.n)
            return None
        emit(block.gpsimd, _)

    return nc


def kernel(**inputs):
    base = host_consts()
    x_full = np.ascontiguousarray(inputs["L0_attn_x"], dtype=np.float32)
    for w, nm in zip("qkvo", ["Wq", "Wk", "Wv", "Wo"]):
        base[f"l0_w{w}T"] = np.ascontiguousarray(
            inputs[f"L0_attn_{nm}"].T.reshape(8, 128, D).astype(np.float32))
    base["l0_gateT"] = np.ascontiguousarray(inputs["L0_ffn_gate_w"].T.astype(np.float32))
    base["l0_moe_w1"] = np.ascontiguousarray(inputs["L0_ffn_w1"].astype(np.float32))
    base["l0_moe_w3"] = np.ascontiguousarray(inputs["L0_ffn_w3"].astype(np.float32))
    base["l0_moe_w2"] = np.ascontiguousarray(inputs["L0_ffn_w2"].astype(np.float32))
    base["l1_wqkv"] = np.ascontiguousarray(
        inputs["L1_attn_w_qkv"].reshape(8, 128, 3 * D).astype(np.float32))
    base["l1_wout"] = np.ascontiguousarray(
        inputs["L1_attn_w_out"].reshape(8, 128, D).astype(np.float32))
    base["l1_rrep"] = np.ascontiguousarray(
        np.tile(np.asarray(inputs["L1_ffn_routing_token"], np.float32)[None, :], (128, 1)))
    base["l1_lgam"] = np.ascontiguousarray(
        np.asarray(inputs["L1_ffn_light_gamma"], np.float32).reshape(8, 128).T)
    base["l1_hgam"] = np.ascontiguousarray(
        np.asarray(inputs["L1_ffn_heavy_gamma"], np.float32).reshape(8, 128).T)
    base["l1_lw1"] = np.ascontiguousarray(
        np.asarray(inputs["L1_ffn_light_w1"], np.float32).reshape(8, 128, 512))
    base["l1_lw2"] = np.ascontiguousarray(
        np.asarray(inputs["L1_ffn_light_w2"], np.float32).reshape(4, 128, D))
    base["l1_lb1"] = np.ascontiguousarray(np.asarray(inputs["L1_ffn_light_b1"], np.float32)[None, :])
    base["l1_lb2"] = np.ascontiguousarray(np.asarray(inputs["L1_ffn_light_b2"], np.float32)[None, :])
    base["l1_hb1"] = np.ascontiguousarray(np.asarray(inputs["L1_ffn_heavy_b1"], np.float32)[None, :])
    base["l1_hb2"] = np.ascontiguousarray(np.asarray(inputs["L1_ffn_heavy_b2"], np.float32)[None, :])
    base["l1_hw1"] = np.ascontiguousarray(np.asarray(inputs["L1_ffn_heavy_w1"], np.float32))
    base["l1_hw2"] = np.ascontiguousarray(np.asarray(inputs["L1_ffn_heavy_w2"], np.float32))
    nc = build_program()
    in_maps = []
    for core in range(8):
        m = dict(base)
        m["x"] = np.ascontiguousarray(x_full[core])
        m["xTin"] = np.ascontiguousarray(x_full[core].T)
        in_maps.append(m)
    res = run_bass_kernel_spmd(nc, in_maps, core_ids=list(range(8)))
    return np.stack([res.results[i]["y"] for i in range(8)], axis=0)


# version 13 (final)
# speedup vs baseline: 1.2334x; 1.0155x over previous
import contextlib
import os
import numpy as np
import concourse.bass as bass
import concourse.mybir as mybir
from concourse.bass_utils import run_bass_kernel_spmd

F32 = mybir.dt.float32
BF16 = mybir.dt.bfloat16
AX = mybir.AxisListType
OP = mybir.AluOpType
ACT = mybir.ActivationFunctionType

T = 2048
D = 1024
NCH = 16
H = 16
HD = 64
NEXP = 16
HEAVY_I = 4096
NEG = -1e30

PHASES = ("l0a", "l0f", "l1a", "l1f")


def _split2(v):
    hi = np.float32(v)
    lo = np.float32(np.float64(v) - np.float64(hi))
    return float(hi), float(lo)


ALIBI_SLOPES = [_split2(2.0 ** (-8.0 * np.float64(h) / 15.0)) for h in range(H)]


def host_consts():
    c = {"c_ident": np.eye(128, dtype=np.float32)}
    s_ = np.arange(128)[:, None]
    t_ = np.arange(128)[None, :]
    c["c_UT"] = (s_ <= t_).astype(np.float32)
    c["c_ones"] = np.ones((128, 128), np.float32)
    s = np.arange(128)[:, None]
    t = np.arange(128)[None, :]
    rel0 = 128.0 + s - t
    rel1 = s - t
    val0 = s < t
    val1 = s >= t
    R = np.zeros((128, 2, 128), np.float64)
    Mneg = np.zeros((128, 2, 128), np.float64)
    R[:, 0, :] = np.where(val0, -rel0, 0.0)
    R[:, 1, :] = np.where(val1, -rel1, 0.0)
    Mneg[:, 0, :] = np.where(val0, 0.0, NEG)
    Mneg[:, 1, :] = np.where(val1, 0.0, NEG)
    c["c_R"] = R.astype(np.float32)
    c["c_Mneg"] = Mneg.astype(np.float32)
    return c


class Sem:
    def __init__(self, nc, ctx, name, unit=1):
        self.h = ctx.enter_context(nc.semaphore(name))
        self.unit = unit
        self.n = 0

    def sig(self, inst):
        self.n += self.unit
        inst.then_inc(self.h, self.unit)
        return (self, self.n)

    def tok(self):
        return (self, self.n)

    def throttle(self, eng, max_out=1):
        eng.wait_ge(self.h, self.n)


def wt(eng, *toks):
    best = {}
    for tk in toks:
        if tk is None:
            continue
        s, v = tk
        if v > 0 and v > best.get(id(s), (s, 0))[1]:
            best[id(s)] = (s, v)
    for s, v in best.values():
        eng.wait_ge(s.h, v)


def tr_slot(k):
    return (k % 2) * 512 + ((k // 2) % 2) * 128


def build_program(phases=PHASES):
    nc = bass.Bass("TRN2", target_bir_lowering=False)

    def din(name, shape):
        return nc.dram_tensor(name, list(shape), F32, kind="ExternalInput").ap()

    x_hbm = din("x", (T, D))
    xT_hbm = din("xTin", (D, T))
    l0w_hbm = {w: din(f"l0_w{w}T", (8, 128, D)) for w in "qkvo"}
    gate_hbm = din("l0_gateT", (D, NEXP))
    l1wqkv_hbm = din("l1_wqkv", (8, 128, 3 * D))
    l1wout_hbm = din("l1_wout", (8, 128, D))
    ut_hbm = din("c_UT", (128, 128))
    ones_hbm = din("c_ones", (128, 128))
    rrep_hbm = din("l1_rrep", (128, D))
    lgam_hbm = din("l1_lgam", (128, 8))
    hgam_hbm = din("l1_hgam", (128, 8))
    lw1_hbm = din("l1_lw1", (8, 128, 512))
    lw2_hbm = din("l1_lw2", (4, 128, D))
    lb1_hbm = din("l1_lb1", (1, 512))
    lb2_hbm = din("l1_lb2", (1, D))
    hw1_hbm = din("l1_hw1", (D, HEAVY_I))
    hw2_hbm = din("l1_hw2", (HEAVY_I, D))
    hb1_hbm = din("l1_hb1", (1, HEAVY_I))
    hb2_hbm = din("l1_hb2", (1, D))
    moe_w1_hbm = din("l0_moe_w1", (NEXP, D, 512))
    moe_w3_hbm = din("l0_moe_w3", (NEXP, D, 512))
    moe_w2_hbm = din("l0_moe_w2", (NEXP, 512, D))
    ident_h = din("c_ident", (128, 128))
    R_h = din("c_R", (128, 2, 128))
    Mneg_h = din("c_Mneg", (128, 2, 128))
    y_hbm = nc.dram_tensor("y", [T, D], F32, kind="ExternalOutput").ap()
    DBG = os.environ.get("KDBG")
    if DBG:
        dbg_hbm = {nm: nc.dram_tensor(f"dbg_{nm}", sh, dt, kind="ExternalOutput").ap()
                   for nm, sh, dt in [("qn", [128, D], F32),
                                      ("PT", [128, 8 * 2 * 128], BF16),
                                      ("kT", [128, 8 * 128], BF16),
                                      ("qT", [128, 16 * 128], BF16),
                                      ("nrm", [128, 8 * H], F32),
                                      ("vbf", [128, H * 65], BF16)]}

    ctx = contextlib.ExitStack()
    with ctx:
        E = ctx.enter_context
        X = E(nc.sbuf_tensor("X", [128, NCH, D], F32))
        XNT = E(nc.sbuf_tensor("XNT", [128, 8, T], BF16))
        ident = E(nc.sbuf_tensor("ident", [128, 128], F32))
        identb = E(nc.sbuf_tensor("identb", [128, 128], BF16))
        cst_sm = E(nc.sbuf_tensor("cst_sm", [128, 2], F32))
        lnm = E(nc.sbuf_tensor("lnm", [128, 8], F32))

        svals = E(nc.sbuf_tensor("svals", [128, 32], F32))
        rms_sv_tok = [None] * NCH
        dX = Sem(nc, ctx, "dX", 16)
        dW = Sem(nc, ctx, "dW", 16)
        dxin = [Sem(nc, ctx, f"dxin{c}", 16) for c in range(NCH)]
        dC = Sem(nc, ctx, "dC", 16)
        dSl = [Sem(nc, ctx, f"dSl{i}", 16) for i in range(2)]
        dG = Sem(nc, ctx, "dG", 16)
        sT = Sem(nc, ctx, "sT")
        sV = Sem(nc, ctx, "sV")
        sS = Sem(nc, ctx, "sS")
        sG = Sem(nc, ctx, "sG")
        block = E(nc.Block())

        def load_set6(s):
            s.add_instruction(mybir.InstLoadActFuncSet(
                name=nc.get_next_instruction_name(), act_func_set_id=6,
                ins=[], outs=[]))

        def emit(section, fn):
            out = []

            def wrap(eng):
                out.append(fn(eng))
            section(wrap)
            return out[0]

        def emit_transposes(srcs, ptile, dsts, pre_toks=(), evac="vector",
                            qtz_split=None):
            n = len(srcs)
            ttoks = [None] * n
            sem = sS if evac == "scalar" else sV
            base = sem.n

            def cum(k):
                if qtz_split is None:
                    return base + k + 1
                return base + (2 * (k + 1) if k < 8 else 16 + (k - 8 + 1))
            etoks = [(sem, cum(k)) for k in range(n)]

            def tpart(t):
                for k in range(n):
                    if k == 0:
                        wt(t, *pre_toks)
                    if k >= 2:
                        wt(t, etoks[k - 2])
                    i = t.transpose(ptile[:, tr_slot(k):tr_slot(k) + 128],
                                    srcs[k], ident[:, :])
                    ttoks[k] = sT.sig(i)
            block.tensor(tpart)

            def epart(e):
                for k in range(n):
                    wt(e, ttoks[k])
                    src = ptile[:, tr_slot(k):tr_slot(k) + 128]
                    if qtz_split is not None and k < 8:
                        sem.sig(e.tensor_copy(qtz_split[0:64, 2 * k, :],
                                              src[0:64, :]))
                        sem.sig(e.tensor_copy(qtz_split[64:128, 2 * k + 1, :],
                                              src[64:128, :]))
                    elif evac == "scalar":
                        sem.sig(e.copy(dsts[k], src))
                    else:
                        sem.sig(e.tensor_copy(dsts[k], src))
                assert sem.n == cum(n - 1), (sem.n, base, n)
            (block.scalar if evac == "scalar" else block.vector)(epart)
            return etoks

        xin_tok = [None] * NCH

        def _(sy):
            for c in range(NCH):
                i = sy.dma_start(out=X[:, c, :], in_=x_hbm[c * 128:(c + 1) * 128, :])
                xin_tok[c] = dxin[c].sig(i)
        block.sync(_)

        qn = E(nc.sbuf_tensor("qn", [128, D], F32))
        kn = E(nc.sbuf_tensor("kn", [128, D], F32))
        nrm = E(nc.sbuf_tensor("nrm", [128, 8, H], F32))
        pA = E(nc.psum_tensor("pA", [128, 1024], F32))
        pB = E(nc.psum_tensor("pB", [128, 1024], F32))
        pC = E(nc.psum_tensor("pC", [128, 1024], F32))
        pD = E(nc.psum_tensor("pD", [128, 1024], F32))
        l0actx = contextlib.ExitStack()
        EA = l0actx.enter_context
        l0w = {w: EA(nc.sbuf_tensor(f"l0w{w}", [128, 8, D], BF16)) for w in "qkvo"}
        biasT = EA(nc.sbuf_tensor("biasT", [128, H, 2, 128], F32))
        rctx = contextlib.ExitStack()
        Rt = rctx.enter_context(nc.sbuf_tensor("Rt", [128, 2, 128], F32))
        Mnegt = rctx.enter_context(nc.sbuf_tensor("Mnegt", [128, 2, 128], F32))


        def _(sy):
            dC.sig(sy.dma_start(out=ident[:, :], in_=ident_h))
            dC.sig(sy.dma_start(out=Rt[:, :, :], in_=R_h))
            return dC.sig(sy.dma_start(out=Mnegt[:, :, :], in_=Mneg_h))
        cst_tok = emit(block.sync, _)
        cst_tok = (dC, 48)

        wcast_tok = {}

        def _(g):
            for w in "qkvo":
                for q in range(8):
                    dG.sig(g.dma_start(out=l0w[w][:, q, :], in_=l0w_hbm[w][q]))
                    dG.throttle(g)
                wcast_tok[w] = dG.tok()
            return None
        emit(block.gpsimd, _)

        def _(v):
            wt(v, cst_tok)
            for h in range(H):
                hi, _lo = ALIBI_SLOPES[h]
                i = v.scalar_tensor_tensor(biasT[:, h, :, :], Rt[:, :, :], hi,
                                           Mnegt[:, :, :], op0=OP.mult, op1=OP.add)
            wt(v, sV.sig(i))
            for h in range(H):
                _hi, lo = ALIBI_SLOPES[h]
                i = v.scalar_tensor_tensor(biasT[:, h, :, :], Rt[:, :, :], lo,
                                           biasT[:, h, :, :], op0=OP.mult,
                                           op1=OP.add)
            return sV.sig(i)
        bias_tok = emit(block.vector, _)
        rctx.close()

        def _(v):
            v.memset(cst_sm[:, 1:2], float(np.float32(np.log(np.float64(32.0)))))
            wt(v, cst_tok)
            v.tensor_copy(identb[:, :], ident[:, :])
            return sV.sig(v.memset(cst_sm[:, 0:1], 1e-5))
        eps_tok = emit(block.vector, _)

        xT = EA(nc.sbuf_tensor("xT", [128, 8, 128], BF16))
        vbf = [EA(nc.sbuf_tensor(f"vbf{i}", [128, H, 65], BF16)) for i in range(2)]
        kT = [EA(nc.sbuf_tensor(f"kT{i}", [128, 8, 128], BF16)) for i in range(2)]
        qT = [EA(nc.sbuf_tensor(f"qT{i}", [128, 16, 128], BF16)) for i in range(2)]
        PT = EA(nc.sbuf_tensor("PT", [128, 8, 2, 128], BF16))

        def _(v):
            wt(v, bias_tok)
            v.memset(qT[0][:, :, :], 0.0)
            v.memset(qT[1][:, :, :], 0.0)
            v.memset(vbf[0][:, :, 64:65], 1.0)
            return sV.sig(v.memset(vbf[1][:, :, 64:65], 1.0))
        ones_tok = emit(block.vector, _)

        tok_tqkv = [None] * NCH
        tok_qkev = [None] * NCH
        tok_tnrm = [None] * NCH
        tok_tvev = [None] * NCH
        tok_tsm = [[None, None] for _ in range(NCH)]
        tok_exp = [[None, None] for _ in range(NCH)]
        tok_pvT = [[None, None] for _ in range(NCH)]
        tok_pve = [None] * NCH
        tok_aoev = [None] * NCH
        tok_x1ev = [None] * NCH
        tok_tres = [None] * NCH
        tok_tln = [None] * NCH

        def l0a_xload(c):
            def _(g, c=c):
                wt(g, tok_tqkv[c - 1] if c else bias_tok)
                for k in range(8):
                    dG.sig(g.dma_start(out=xT[:, k, :],
                                       in_=xT_hbm[k * 128:(k + 1) * 128,
                                                  c * 128:(c + 1) * 128]))
                    dG.throttle(g)
                return dG.tok()
            return emit(block.gpsimd, _)

        def l0a_qkv(c, xld):
            def _(t, c=c, xld=xld):
                wt(t, wcast_tok["q"], wcast_tok["k"], wcast_tok["v"], xld)
                if c >= 1:
                    for tk in tok_qkev[c - 1]:
                        wt(t, tk)
                if c >= 2:
                    wt(t, tok_tsm[c - 2][0], tok_tsm[c - 2][1])
                for k in range(8):
                    stt, sp = (k == 0), (k == 7)
                    t.matmul(pA[:, 0:512], xT[:, k, :], l0w["q"][:, k, 0:512], start=stt, stop=sp)
                    t.matmul(pA[:, 512:1024], xT[:, k, :], l0w["q"][:, k, 512:1024], start=stt, stop=sp)
                for k in range(8):
                    stt, sp = (k == 0), (k == 7)
                    t.matmul(pB[:, 0:512], xT[:, k, :], l0w["k"][:, k, 0:512], start=stt, stop=sp)
                    t.matmul(pB[:, 512:1024], xT[:, k, :], l0w["k"][:, k, 512:1024], start=stt, stop=sp)
                if c >= 2:
                    wt(t, tok_pve[c - 2])
                for k in range(8):
                    stt, sp = (k == 0), (k == 7)
                    t.matmul(pC[:, 0:512], xT[:, k, :], l0w["v"][:, k, 0:512], start=stt, stop=sp)
                    i = t.matmul(pC[:, 512:1024], xT[:, k, :], l0w["v"][:, k, 512:1024], start=stt, stop=sp)
                return sT.sig(i)
            tok_tqkv[c] = emit(block.tensor, _)

        def l0a_norm(c):
            cur = c % 2

            tred_pred = (sV, sV.n + 1)

            def _(s, c=c):
                wt(s, tok_tqkv[c], ones_tok)
                if c >= 2:
                    wt(s, tok_aoev[c - 2][-1], tok_exp[c - 2][1])
                s.activation(qn[:, :], pA[:, :], ACT.Square)
                i = s.activation(kn[:, :], pB[:, :], ACT.Square)
                tq = sS.sig(i)
                i = s.copy(vbf[cur][:, :, 0:64],
                           pC[:, :].rearrange("p (h d) -> p h d", h=H))
                tv = sS.sig(i)
                wt(s, tred_pred)
                s.activation(nrm[:, 2, :], nrm[:, 0, :], ACT.Ln)
                i = s.activation(nrm[:, 3, :], nrm[:, 1, :], ACT.Ln)
                wt(s, sS.sig(i))
                s.activation(nrm[:, 4, :], nrm[:, 2, :], ACT.Exp, scale=-0.5)
                i = s.activation(nrm[:, 5, :], nrm[:, 3, :], ACT.Exp, scale=-0.5)
                return tq, tv, sS.sig(i)
            tsq, tok_tvev[c], tsqrt = emit(block.scalar, _)

            def _(v, tsq=tsq):
                wt(v, tsq)
                v.tensor_reduce(nrm[:, 0, :], qn[:, :].rearrange("p (h d) -> p h d", h=H),
                                axis=AX.X, op=OP.add)
                i = v.tensor_reduce(nrm[:, 1, :], kn[:, :].rearrange("p (h d) -> p h d", h=H),
                                    axis=AX.X, op=OP.add)
                return sV.sig(i)
            tred = emit(block.vector, _)
            assert tred == tred_pred, (tred, tred_pred)

            def _(v, c=c, tsqrt=tsqrt):
                wt(v, tsqrt)
                if c >= 2:
                    wt(v, tok_exp[c - 2][1])
                v.tensor_scalar(nrm[:, 4, :], nrm[:, 4, :], 1e6, None, op0=OP.min)
                i = v.tensor_scalar(nrm[:, 5, :], nrm[:, 5, :], 1e6, None, op0=OP.min)
                wt(v, sV.sig(i))
                v.tensor_tensor(qn[:, :].rearrange("p (h d) -> p h d", h=H),
                                pA[:, :].rearrange("p (h d) -> p h d", h=H),
                                nrm[:, 4, :].unsqueeze(2).broadcast_to([128, H, HD]),
                                op=OP.mult)
                i = v.tensor_tensor(kn[:, :].rearrange("p (h d) -> p h d", h=H),
                                    pB[:, :].rearrange("p (h d) -> p h d", h=H),
                                    nrm[:, 5, :].unsqueeze(2).broadcast_to([128, H, HD]),
                                    op=OP.mult)
                return sV.sig(i)
            tok_tnrm[c] = emit(block.vector, _)

        def l0a_qktr(c):
            cur = c % 2
            qksrc = ([qn[:, k * 128:(k + 1) * 128] for k in range(8)] +
                     [kn[:, k * 128:(k + 1) * 128] for k in range(8)])
            qkdst = ([qT[cur][:, 2 * k, :] for k in range(8)] +
                     [kT[cur][:, k, :] for k in range(8)])
            tok_qkev[c] = emit_transposes(qksrc, pA, qkdst,
                                          pre_toks=(tok_tnrm[c],),
                                          qtz_split=qT[cur])

        def l0a_aoT(c):
            ttoks = [None] * 8
            base = sS.n
            etoks = [(sS, base + k + 1) for k in range(8)]

            def _(t, c=c):
                for k in range(8):
                    if k == 0:
                        wt(t, tok_pve[c], eps_tok)
                    if k >= 2:
                        wt(t, etoks[k - 2])
                    i = t.matmul(pD[:, tr_slot(k):tr_slot(k) + 128],
                                 PT[:, k, 1, :], identb[:, :],
                                 start=True, stop=True)
                    ttoks[k] = sT.sig(i)
            block.tensor(_)

            def _(s):
                for k in range(8):
                    wt(s, ttoks[k])
                    sS.sig(s.copy(PT[:, k, 0, :],
                                  pD[:, tr_slot(k):tr_slot(k) + 128]))
                assert sS.n == base + 8
            block.scalar(_)
            tok_aoev[c] = etoks

        def l0a_x1ev(c):
            x1src = [X[:, c, k * 128:(k + 1) * 128] for k in range(8)]
            x1dst = [XNT[:, k, c * 128:(c + 1) * 128] for k in range(8)]
            tok_x1ev[c] = emit_transposes(x1src, pD, x1dst,
                                          pre_toks=(tok_tln[c],
                                                    tok_aoev[c + 1][-1]),
                                          evac="scalar")

        def l0a_wo_ln(c):
            def _(t, c=c):
                wt(t, wcast_tok["o"], tok_aoev[c][-1])
                if c >= 1 and tok_x1ev[c - 1] is not None:
                    wt(t, tok_x1ev[c - 1][-1])
                for k in range(8):
                    stt, sp = (k == 0), (k == 7)
                    t.matmul(pD[:, 0:512], PT[:, k, 0, :], l0w["o"][:, k, 0:512], start=stt, stop=sp)
                    i = t.matmul(pD[:, 512:1024], PT[:, k, 0, :], l0w["o"][:, k, 512:1024], start=stt, stop=sp)
                return sT.sig(i)
            twoT = emit(block.tensor, _)

            def _(v, c=c, twoT=twoT):
                wt(v, twoT, xin_tok[c])
                return sV.sig(v.tensor_add(X[:, c, :], X[:, c, :], pD[:, :]))
            tok_tres[c] = tres = emit(block.vector, _)

            tvar_pred = (sV, sV.n + 3)

            def _(s, c=c, tres=tres):
                wt(s, tres, eps_tok)
                junk = XNT[:, :, c * 128:(c + 1) * 128]
                xin3 = X[:, c, :].rearrange("p (a t) -> p a t", a=8)
                i = s.activation(junk, xin3, ACT.Copy,
                                 accum_out=lnm[:, 0:1])
                wt(s, sS.sig(i))
                tstt = sS.sig(s.activation(junk, xin3, ACT.Square,
                                           accum_out=lnm[:, 1:2]))
                wt(s, tvar_pred)
                i = s.activation(lnm[:, 4:5], lnm[:, 3:4], ACT.Ln,
                                 bias=cst_sm[:, 0:1])
                wt(s, sS.sig(i))
                return tstt, sS.sig(s.activation(lnm[:, 5:6], lnm[:, 4:5],
                                                 ACT.Exp, scale=-0.5))
            tst, tsd = emit(block.scalar, _)

            def _(v, tst=tst):
                wt(v, tst)
                v.tensor_scalar(lnm[:, 2:3], lnm[:, 0:1], 1.0 / D, None, op0=OP.mult)
                i = v.tensor_scalar(lnm[:, 3:4], lnm[:, 1:2], 1.0 / D, None, op0=OP.mult)
                wt(v, sV.sig(i))
                i = v.tensor_tensor(lnm[:, 4:5], lnm[:, 2:3], lnm[:, 2:3], op=OP.mult)
                wt(v, sV.sig(i))
                return sV.sig(v.tensor_sub(lnm[:, 3:4], lnm[:, 3:4], lnm[:, 4:5]))
            tvar = emit(block.vector, _)
            assert tvar == tvar_pred, (tvar, tvar_pred)

            def _(v, c=c, tsd=tsd):
                wt(v, tsd)
                i = v.tensor_scalar(lnm[:, 6:7], lnm[:, 2:3], -1.0, None, op0=OP.mult)
                wt(v, sV.sig(i))
                return sV.sig(v.tensor_scalar(X[:, c, :], X[:, c, :],
                                              lnm[:, 6:7], lnm[:, 5:6],
                                              op0=OP.add, op1=OP.mult))
            tok_tln[c] = emit(block.vector, _)

        def buf(hh):
            b = kn if hh < 4 else qn
            return b[:, (hh % 4) * 256:(hh % 4) * 256 + 256]

        def l0a_scores(qc, g, part, last):
            qcur, qnxt = qc % 2, (qc + 1) % 2

            def ps_of(g, hh):
                if g == 0:
                    return (pB if hh < 4 else pC)
                return (pA if hh < 4 else pB)

            def _(t, qc=qc, g=g, part=part, last=last):
                kc = qc if last else qc + 1
                for tk in tok_qkev[kc]:
                    wt(t, tk)
                wt(t, tok_tnrm[kc], tok_tvev[kc])
                if g == 1 and part == "b":
                    wt(t, tok_tsm[qc][0])
                rng = range(4) if g == 0 or part == "a" else range(4, 8)
                if g == 0:
                    rng = range(8)
                i = None
                for hh in rng:
                    h = g * 8 + hh
                    tl = h // 2
                    ps = ps_of(g, hh)
                    base = (hh % 4) * 256
                    i = t.matmul(ps[:, base + 128:base + 256],
                                 kT[qcur][:, tl, :],
                                 qT[qcur][:, h, :], start=True, stop=True)
                    if not last:
                        i = t.matmul(ps[:, base:base + 128],
                                     kT[qnxt][:, tl, :],
                                     qT[qcur][:, h, :], start=True, stop=True)
                return sT.sig(i)
            return emit(block.tensor, _)

        def l0a_bias(qc, g, sc_toks, last):
            def _(v, qc=qc, g=g, last=last):
                wt(v, bias_tok, *sc_toks)
                if g == 1:
                    wt(v, tok_exp[qc][0])
                elif qc >= 1:
                    wt(v, tok_exp[qc - 1][1])
                i = None
                if last:
                    for hh in range(8):
                        v.memset(buf(hh)[:, 0:128], NEG)
                    for hh in range(8):
                        h = g * 8 + hh
                        ps = (pB if hh < 4 else pC) if g == 0 else (pA if hh < 4 else pB)
                        base = (hh % 4) * 256
                        i = v.tensor_tensor(
                            buf(hh)[:, 128:256],
                            ps[:, base + 128:base + 256],
                            biasT[:, h, 1, :],
                            op=OP.add)
                else:
                    for hh in range(8):
                        h = g * 8 + hh
                        ps = (pB if hh < 4 else pC) if g == 0 else (pA if hh < 4 else pB)
                        base = (hh % 4) * 256
                        i = v.tensor_tensor(
                            buf(hh),
                            ps[:, base:base + 256],
                            biasT[:, h, :, :].rearrange("p r t -> p (r t)"),
                            op=OP.add)
                return sV.sig(i)
            tok_tsm[qc][g] = emit(block.vector, _)
            return tok_tsm[qc][g]

        def l0a_exp_merged(qc):
            tsm1_pred = (sV, sV.n + 1)
            pv0_pred = (sT, sT.n + 2)

            def _(s, qc=qc):
                wt(s, tok_tsm[qc][0])
                s.activation(
                    PT[:, 0:4, :, :].rearrange("p h r t -> p (h r t)"),
                    kn[:, :], ACT.Exp)
                e0 = sS.sig(s.activation(
                    PT[:, 4:8, :, :].rearrange("p h r t -> p (h r t)"),
                    qn[:, :], ACT.Exp))
                wt(s, tsm1_pred, pv0_pred)
                s.activation(
                    PT[:, 0:4, :, :].rearrange("p h r t -> p (h r t)"),
                    kn[:, :], ACT.Exp)
                e1 = sS.sig(s.activation(
                    PT[:, 4:8, :, :].rearrange("p h r t -> p (h r t)"),
                    qn[:, :], ACT.Exp))
                return e0, e1
            tok_exp[qc][0], tok_exp[qc][1] = emit(block.scalar, _)
            return tsm1_pred, pv0_pred

        def l0a_pv(qc, g, last):
            qcur, qnxt = qc % 2, (qc + 1) % 2

            def _(t, qc=qc, g=g, last=last):
                wt(t, tok_exp[qc][g])
                if g == 0:
                    if qc >= 2 and tok_x1ev[qc - 2] is not None:
                        wt(t, tok_x1ev[qc - 2][-1])
                    wt(t, tok_tres[qc - 1] if qc >= 1 else None)
                else:
                    wt(t, tok_tsm[qc][0])
                i = None
                for hh in range(8):
                    h = g * 8 + hh
                    ps = pD if g == 0 else pC
                    col = (hh // 4) * 512 + (hh % 4) * 66
                    i = t.matmul(ps[:, col:col + 65], PT[:, hh, 1, :],
                                 vbf[qcur][:, h, :], start=True, stop=last)
                    if not last:
                        i = t.matmul(ps[:, col:col + 65], PT[:, hh, 0, :],
                                     vbf[qnxt][:, h, :], start=False, stop=True)
                return sT.sig(i)
            tok_pvT[qc][g] = emit(block.tensor, _)
            return tok_pvT[qc][g]

        def l0a_pvevac(qc):
            def _(v, qc=qc):
                wt(v, tok_pvT[qc][0], tok_pvT[qc][1])
                i = None
                for g4 in range(4):
                    ps = pD if g4 < 2 else pC
                    col = (g4 % 2) * 512
                    i = v.tensor_copy(nrm[:, 6, g4 * 4:(g4 + 1) * 4],
                                      ps[:, col:col + 264].rearrange(
                                          "p (h e) -> p h e", h=4)[:, :, 64])
                wt(v, sV.sig(i))
                i = v.reciprocal(nrm[:, 7, :], nrm[:, 6, :])
                wt(v, sV.sig(i))
                i = None
                for tile in range(8):
                    ps = pD if tile < 4 else pC
                    col = (tile // 4 % 2) * 0 + ((tile % 4) // 2) * 512 + (tile % 2) * 132
                    i = v.tensor_tensor(
                        PT[:, tile, 1, :].rearrange("p (h d) -> p h d", h=2),
                        ps[:, col:col + 132].rearrange("p (h e) -> p h e", h=2)[:, :, 0:64],
                        nrm[:, 7, tile * 2:tile * 2 + 2].unsqueeze(2).broadcast_to([128, 2, HD]),
                        op=OP.mult)
                return sV.sig(i)
            tok_pve[qc] = emit(block.vector, _)

        def l0a_att_head(qc, last):
            t0 = l0a_scores(qc, 0, None, last)
            l0a_bias(qc, 0, (t0,), last)
            tsm1_pred, pv0_pred = l0a_exp_merged(qc)
            t1a = l0a_scores(qc, 1, "a", last)
            pv0 = l0a_pv(qc, 0, last)
            assert pv0 == pv0_pred, (pv0, pv0_pred)
            t1b = l0a_scores(qc, 1, "b", last)
            tsm1 = l0a_bias(qc, 1, (t1a, t1b), last)
            assert tsm1 == tsm1_pred, (tsm1, tsm1_pred)
            l0a_pv(qc, 1, last)
            l0a_pvevac(qc)

        if "l0a" in phases:
            for i in range(NCH):
                xld = l0a_xload(i)
                l0a_qkv(i, xld)
                if i >= 2:
                    l0a_aoT(i - 2)
                l0a_norm(i)
                if i >= 3:
                    l0a_x1ev(i - 3)
                if i >= 2:
                    l0a_wo_ln(i - 2)
                l0a_qktr(i)
                if i >= 1:
                    l0a_att_head(i - 1, last=False)
            l0a_aoT(NCH - 2)
            if NCH >= 3:
                l0a_x1ev(NCH - 3)
            l0a_wo_ln(NCH - 2)
            l0a_att_head(NCH - 1, last=True)
            l0a_aoT(NCH - 1)
            l0a_x1ev(NCH - 2)
            l0a_wo_ln(NCH - 1)
            x1src = [X[:, NCH - 1, k * 128:(k + 1) * 128] for k in range(8)]
            x1dst = [XNT[:, k, (NCH - 1) * 128:NCH * 128] for k in range(8)]
            tok_x1ev[NCH - 1] = emit_transposes(
                x1src, pD, x1dst,
                pre_toks=(tok_tln[NCH - 1], tok_tres[NCH - 1]))
        l0actx.close()
        bar_l0a = [(sT, sT.n), (sV, sV.n), (sS, sS.n)]

        def emit_ln_chunk(c, pre_toks, ptile, skip_t=False):
            tvar_pred = (sV, sV.n + 3)

            def _(s):
                wt(s, *pre_toks)
                s.activation(kn[:, :], X[:, c, :], ACT.Copy,
                             accum_out=lnm[:, 0:1])
                i = s.activation(qn[:, :], X[:, c, :], ACT.Square,
                                 accum_out=lnm[:, 1:2])
                wt(s, sS.sig(i))
                wt(s, tvar_pred)
                i = s.activation(lnm[:, 4:5], lnm[:, 3:4], ACT.Ln,
                                 bias=cst_sm[:, 0:1])
                wt(s, sS.sig(i))
                return sS.sig(s.activation(lnm[:, 5:6], lnm[:, 4:5], ACT.Exp,
                                           scale=-0.5))
            tsd = emit(block.scalar, _)
            tst = (sS, tsd[1] - 2)

            def _(v):
                wt(v, tst)
                v.tensor_scalar(lnm[:, 2:3], lnm[:, 0:1], 1.0 / D, None, op0=OP.mult)
                i = v.tensor_scalar(lnm[:, 3:4], lnm[:, 1:2], 1.0 / D, None, op0=OP.mult)
                wt(v, sV.sig(i))
                i = v.tensor_tensor(lnm[:, 4:5], lnm[:, 2:3], lnm[:, 2:3], op=OP.mult)
                wt(v, sV.sig(i))
                i = sV.sig(v.tensor_sub(lnm[:, 3:4], lnm[:, 3:4], lnm[:, 4:5]))
                assert i == tvar_pred, (i, tvar_pred)
                return i
            tvar = emit(block.vector, _)

            def _(v):
                wt(v, tsd)
                i = v.tensor_scalar(lnm[:, 6:7], lnm[:, 2:3], -1.0, None, op0=OP.mult)
                wt(v, sV.sig(i))
                return sV.sig(v.tensor_scalar(X[:, c, :], X[:, c, :],
                                              lnm[:, 6:7], lnm[:, 5:6],
                                              op0=OP.add, op1=OP.mult))
            tln = emit(block.vector, _)

            if skip_t:
                return tln, [tln]
            xsrc = [X[:, c, k * 128:(k + 1) * 128] for k in range(8)]
            xdst = [XNT[:, k, c * 128:(c + 1) * 128] for k in range(8)]
            xev = emit_transposes(xsrc, ptile, xdst, pre_toks=(tln,))
            return tln, xev

        if "l0f" in phases:
            l0fctx = contextlib.ExitStack()
            EF = l0fctx.enter_context
            gateT = EF(nc.sbuf_tensor("gateT", [128, 8, NEXP], BF16))
            gstg = EF(nc.sbuf_tensor("gstg", [128, 8, NEXP], F32))
            scores = EF(nc.sbuf_tensor("scores", [128, NCH, NEXP], F32))
            Lall = EF(nc.sbuf_tensor("Lall", [128, NCH, NEXP], F32))
            Lwk = EF(nc.sbuf_tensor("Lwk", [128, NCH, NEXP], F32))
            rmax = EF(nc.sbuf_tensor("rmax", [128, 4, NCH], F32))
            mw1 = [EF(nc.sbuf_tensor(f"mw1_{i}", [128, 8, 512], BF16)) for i in range(2)]
            mw3 = [EF(nc.sbuf_tensor(f"mw3_{i}", [128, 8, 512], BF16)) for i in range(2)]
            mw2 = [EF(nc.sbuf_tensor(f"mw2_{i}", [128, 4, D], BF16)) for i in range(2)]
            act_bf = EF(nc.sbuf_tensor("act_bf", [128, 4, 512], BF16))
            sg_sb2 = EF(nc.sbuf_tensor("sg_sb2", [128, 4, 512], F32))

            def _(sy):
                wt(sy, *bar_l0a)
                return dW.sig(sy.dma_start(out=gstg[:, :, :],
                                           in_=gate_hbm.rearrange("(k p) e -> p k e", p=128)))
            tgl = emit(block.sync, _)

            def _(v):
                wt(v, tgl, *bar_l0a)
                return sV.sig(v.tensor_copy(gateT[:, :, :], gstg[:, :, :]))
            tgc = emit(block.vector, _)

            rt_toks = []

            def _(t):
                wt(t, tgc)
                for c in range(NCH):
                    for k in range(8):
                        i = t.matmul(pA[:, c * NEXP:(c + 1) * NEXP],
                                     XNT[:, k, c * 128:(c + 1) * 128],
                                     gateT[:, k, :], start=(k == 0), stop=(k == 7))
                    rt_toks.append(sT.sig(i))
                return rt_toks[-1]
            emit(block.tensor, _)

            def _(v):
                wt(v, rt_toks[-1])
                return sV.sig(v.tensor_copy(Lall[:, :, :],
                                            pA[:, 0:NCH * NEXP].rearrange(
                                                "p (c e) -> p c e", e=NEXP)))
            trl = emit(block.vector, _)

            def _(v):
                wt(v, trl)
                i = v.tensor_copy(Lwk[:, :, :], Lall[:, :, :])
                wt(v, sV.sig(i))
                for j in range(4):
                    i = v.tensor_reduce(rmax[:, j, :], Lwk[:, :, :], axis=AX.X,
                                        op=OP.max)
                    wt(v, sV.sig(i))
                    if j < 3:
                        i = v.tensor_tensor(qn[:, 0:NCH * NEXP].rearrange(
                                                "p (c e) -> p c e", e=NEXP),
                                            Lwk[:, :, :],
                                            rmax[:, j, :].unsqueeze(2).broadcast_to(
                                                [128, NCH, NEXP]),
                                            op=OP.is_ge)
                        wt(v, sV.sig(i))
                        i = v.scalar_tensor_tensor(Lwk[:, :, :],
                                                   qn[:, 0:NCH * NEXP].rearrange(
                                                       "p (c e) -> p c e", e=NEXP),
                                                   -1e30, Lwk[:, :, :],
                                                   op0=OP.mult, op1=OP.add)
                        wt(v, sV.sig(i))
                i = v.tensor_tensor(kn[:, 0:NCH * NEXP].rearrange(
                                        "p (c e) -> p c e", e=NEXP),
                                    Lall[:, :, :],
                                    rmax[:, 3, :].unsqueeze(2).broadcast_to(
                                        [128, NCH, NEXP]),
                                    op=OP.is_ge)
                i = v.tensor_scalar(rmax[:, 1, :], rmax[:, 0, :], -1.0, None,
                                    op0=OP.mult)
                return sV.sig(i)
            ttk = emit(block.vector, _)

            def _(v):
                wt(v, ttk)
                i = v.tensor_tensor(Lwk[:, :, :], Lall[:, :, :],
                                    rmax[:, 0, :].unsqueeze(2).broadcast_to(
                                        [128, NCH, NEXP]),
                                    op=OP.subtract)
                return sV.sig(i)
            tsub = emit(block.vector, _)

            def _(s):
                wt(s, tsub)
                return sS.sig(s.activation(Lwk[:, :, :], Lwk[:, :, :], ACT.Exp))
            texp = emit(block.scalar, _)

            def _(v):
                wt(v, texp)
                i = v.tensor_mul(Lwk[:, :, :], Lwk[:, :, :],
                                 kn[:, 0:NCH * NEXP].rearrange(
                                     "p (c e) -> p c e", e=NEXP))
                wt(v, sV.sig(i))
                i = v.tensor_reduce(rmax[:, 2, :], Lwk[:, :, :], axis=AX.X,
                                    op=OP.add)
                wt(v, sV.sig(i))
                i = v.reciprocal(rmax[:, 3, :], rmax[:, 2, :])
                wt(v, sV.sig(i))
                i = v.tensor_tensor(scores[:, :, :], Lwk[:, :, :],
                                    rmax[:, 3, :].unsqueeze(2).broadcast_to(
                                        [128, NCH, NEXP]),
                                    op=OP.mult)
                return sV.sig(i)
            tsc_done = emit(block.vector, _)

            wbuf_free = [None, None]
            cast_tok = [None, None]
            acc_tok = [None] * NCH
            prev_dma = tgl
            act_ev = [None, None]
            dn_ev = [None, None]

            for e in range(NEXP):
                b = e % 2

                def _(g, e=e, b=b):
                    wt(g, wbuf_free[b])
                    if e < 2:
                        wt(g, *bar_l0a)
                    for k in range(8):
                        dG.sig(g.dma_start(out=mw1[b][:, k, :],
                                           in_=moe_w1_hbm[e, k * 128:(k + 1) * 128, :]))
                        dG.throttle(g)
                    for k in range(8):
                        dG.sig(g.dma_start(out=mw3[b][:, k, :],
                                           in_=moe_w3_hbm[e, k * 128:(k + 1) * 128, :]))
                        dG.throttle(g)
                    for k in range(4):
                        dG.sig(g.dma_start(out=mw2[b][:, k, :],
                                           in_=moe_w2_hbm[e, k * 128:(k + 1) * 128, :]))
                        dG.throttle(g)
                    return dG.tok()
                cast_tok[b] = emit(block.gpsimd, _)

                last_dn_tok = None
                TG = 512 if T >= 512 else T
                SUBS = TG // 128
                GPS = (pA, pA, pC, pC)
                UPS = (pB, pB, pD, pD)
                for g in range(T // TG):
                    def _(t, g=g, e=e):
                        wt(t, cast_tok[b], tsc_done)
                        if e == 0 and g == 0:
                            wt(t, trl)
                        wt(t, act_ev[0], act_ev[1])
                        wt(t, dn_ev[0], dn_ev[1])
                        i = None
                        for it in range(4):
                            gp, up = GPS[it], UPS[it]
                            col = (it % 2) * 512
                            for k in range(8):
                                st, sp = (k == 0), (k == 7)
                                t.matmul(gp[:, col:col + TG],
                                         mw1[b][:, k, it * 128:(it + 1) * 128],
                                         XNT[:, k, g * TG:(g + 1) * TG],
                                         start=st, stop=sp)
                                i = t.matmul(up[:, col:col + TG],
                                             mw3[b][:, k, it * 128:(it + 1) * 128],
                                             XNT[:, k, g * TG:(g + 1) * TG],
                                             start=st, stop=sp)
                        return sT.sig(i)
                    tup = emit(block.tensor, _)

                    sg_toks = []

                    def _(s, tup=tup):
                        wt(s, tup)
                        for it in range(4):
                            gp = GPS[it]
                            col = (it % 2) * 512
                            sg_toks.append(sS.sig(s.activation(
                                sg_sb[:, it % 2, 0:TG] if False else sg_sb2[:, it, 0:TG],
                                gp[:, col:col + TG], ACT.Silu)))
                        return None
                    emit(block.scalar, _)

                    def _(v):
                        i = None
                        for it in range(4):
                            up = UPS[it]
                            col = (it % 2) * 512
                            wt(v, sg_toks[it])
                            i = v.tensor_mul(act_bf[:, it, 0:TG],
                                             sg_sb2[:, it, 0:TG],
                                             up[:, col:col + TG])
                        return sV.sig(i)
                    aev = emit(block.vector, _)
                    act_ev[0] = act_ev[1] = aev

                    for sub in range(SUBS):
                        ch = g * SUBS + sub
                        ps = pC if sub % 2 == 0 else pD
                        def _(t, sub=sub, g=g, ps=ps):
                            wt(t, act_ev[0])
                            wt(t, dn_ev[sub % 2])
                            i = None
                            for it in range(4):
                                st, sp = (it == 0), (it == 3)
                                t.matmul(ps[:, 0:512],
                                         act_bf[:, it, sub * 128:(sub + 1) * 128],
                                         mw2[b][:, it, 0:512], start=st, stop=sp)
                                i = t.matmul(ps[:, 512:1024],
                                             act_bf[:, it, sub * 128:(sub + 1) * 128],
                                             mw2[b][:, it, 512:1024], start=st, stop=sp)
                            return sT.sig(i)
                        tdn = emit(block.tensor, _)
                        last_dn_tok = tdn

                        def _(v, ch=ch, e=e, tdn=tdn, ps=ps):
                            wt(v, tdn, acc_tok[ch], tsc_done)
                            return sV.sig(v.scalar_tensor_tensor(
                                X[:, ch, :], ps[:, :], scores[:, ch, e:e + 1],
                                X[:, ch, :], op0=OP.mult, op1=OP.add))
                        acc_tok[ch] = emit(block.vector, _)
                        dn_ev[sub % 2] = acc_tok[ch]
                wbuf_free[b] = last_dn_tok

            for c in range(NCH):
                pre = [acc_tok[c]]
                if c == 0:
                    pre += [last_dn_tok]
                emit_ln_chunk(c, pre, pB if c % 2 else pA)
            l0fctx.close()

        bar_l0f = [(sT, sT.n), (sV, sV.n), (sS, sS.n)]

        if "l1a" in phases:
            S1 = float(np.float32(64.0 ** (-0.25)))
            S2 = float(np.float32(np.float64(64.0 ** (-0.25)) / np.float64(S1)))
            LN1E4 = np.float64(np.log(10000.0)) / 32.0
            E1 = float(np.float32(-LN1E4))
            E2 = float(np.float32(-LN1E4 - np.float64(np.float32(-LN1E4))))
            TWO_PI = 2.0 * np.pi
            C1 = float(np.float32(TWO_PI))
            C2 = float(np.float32(TWO_PI - np.float64(np.float32(TWO_PI))))
            INV2PI = float(np.float32(1.0 / TWO_PI))
            PI_LO = float(np.float32(3.14159250))
            PI2_HI = float(np.float32(np.pi / 2))
            PI2_LO = float(np.float32(np.pi / 2 - np.float64(np.float32(np.pi / 2))))

            l1ctx = contextlib.ExitStack()
            EL = l1ctx.enter_context
            wqkv = EL(nc.sbuf_tensor("wqkv", [128, 8, 3 * D], BF16))
            wout = EL(nc.sbuf_tensor("wout", [128, 8, D], BF16))

            cost = EL(nc.sbuf_tensor("cost", [128, NCH, 32], F32))
            sint = EL(nc.sbuf_tensor("sint", [128, NCH, 32], F32))
            invf = EL(nc.sbuf_tensor("invf", [128, 32], F32))
            pi2t = EL(nc.sbuf_tensor("pi2t", [128, 1], F32))
            UT = EL(nc.sbuf_tensor("UT", [128, 128], F32))
            xTl = EL(nc.sbuf_tensor("xTl", [128, 8, 128], BF16))
            vaug = EL(nc.sbuf_tensor("vaug", [128, H, 65], BF16))
            kTz = EL(nc.sbuf_tensor("kTz", [128, H, 128], BF16))
            qTp = EL(nc.sbuf_tensor("qTp", [128, 8, 128], BF16))
            ktk = EL(nc.sbuf_tensor("ktk", [128, H, 64], BF16))
            PTl = EL(nc.sbuf_tensor("PTl", [128, 8, 128], BF16))
            KVf = EL(nc.sbuf_tensor("KVf", [128, H, 65], F32))
            KVb = EL(nc.sbuf_tensor("KVb", [128, H, 65], BF16))
            rtmp = EL(nc.sbuf_tensor("rtmp", [128, 3, 512], F32))
            rrep = EL(nc.sbuf_tensor("rrep", [128, D], F32))
            trctx = contextlib.ExitStack()
            trig2 = trctx.enter_context(nc.sbuf_tensor("trig2", [128, NCH, 32], F32))
            rni = trctx.enter_context(nc.sbuf_tensor("rni", [128, NCH, 16], mybir.dt.int32))

            l1w_cast = [None]

            def _(g):
                wt(g, *bar_l0f)
                for q in range(8):
                    for j in range(3):
                        dG.sig(g.dma_start(
                            out=wqkv[:, q, j * 1024:(j + 1) * 1024],
                            in_=l1wqkv_hbm[q][:, j * 1024:(j + 1) * 1024]))
                        dG.throttle(g)
                for q in range(8):
                    dG.sig(g.dma_start(out=wout[:, q, :], in_=l1wout_hbm[q]))
                    dG.throttle(g)
                return dG.tok()
            l1w_cast[0] = emit(block.gpsimd, _)

            def _(sy):
                wt(sy, *bar_l0f)
                dC.sig(sy.dma_start(out=rrep[:, :], in_=rrep_hbm))
                return dC.sig(sy.dma_start(out=UT[:, :], in_=ut_hbm))
            ut_tok = emit(block.sync, _)

            def _(g):
                wt(g, *bar_l0f)
                g.iota(cost[:, :, 0:1], pattern=[[128, NCH]], base=0,
                       channel_multiplier=1,
                       allow_small_or_imprecise_dtypes=True)
                return sG.sig(g.iota(invf[0:1, :], pattern=[[1, 32]], base=0,
                                     channel_multiplier=0,
                                     allow_small_or_imprecise_dtypes=True))
            iota_tok = emit(block.gpsimd, _)

            def _(s):
                wt(s, iota_tok)
                i = s.activation(trig2[0:1, 0, 0:32], invf[0:1, :], ACT.Exp, scale=E1)
                wt(s, sS.sig(i))
                return sS.sig(s.activation(invf[0:1, :], invf[0:1, :], ACT.Exp,
                                           scale=E2))
            tinv1 = emit(block.scalar, _)

            def _(v):
                wt(v, tinv1)
                i = v.tensor_mul(invf[0:1, :], invf[0:1, :], trig2[0:1, 0, 0:32])
                wt(v, sV.sig(i))
                i = v.memset(pi2t[:, :], PI2_HI)
                wt(v, sV.sig(i))
                return sV.sig(v.tensor_scalar(pi2t[:, :], pi2t[:, :], PI2_LO,
                                              None, op0=OP.add))
            tinv2 = emit(block.vector, _)

            def _(t):
                wt(t, tinv2, ut_tok)
                return sT.sig(t.matmul(pA[:, 0:32], UT[0:1, :],
                                       invf[0:1, :], start=True, stop=True))
            tbmm = emit(block.tensor, _)

            def _(v):
                wt(v, tbmm)
                return sV.sig(v.tensor_copy(invf[:, :], pA[:, 0:32]))
            tbc = emit(block.vector, _)

            def _(v):
                wt(v, tbc, iota_tok)
                i = v.tensor_tensor(trig2[:, :, :],
                                    cost[:, :, 0:1].broadcast_to([128, NCH, 32]),
                                    invf[:, :].unsqueeze(1).broadcast_to([128, NCH, 32]),
                                    op=OP.mult)
                wt(v, sV.sig(i))
                i = v.tensor_scalar(cost[:, :, :], trig2[:, :, :], INV2PI, 0.5,
                                    op0=OP.mult, op1=OP.add)
                return sV.sig(i)
            tph = emit(block.vector, _)

            def _(v):
                wt(v, tph)
                for h in range(2):
                    i = v.tensor_copy(rni[:, :, :], cost[:, :, h * 16:h * 16 + 16])
                    wt(v, sV.sig(i))
                    i = v.tensor_copy(cost[:, :, h * 16:h * 16 + 16], rni[:, :, :])
                    wt(v, sV.sig(i))
                i = v.scalar_tensor_tensor(trig2[:, :, :], cost[:, :, :], -C1,
                                           trig2[:, :, :], op0=OP.mult, op1=OP.add)
                wt(v, sV.sig(i))
                i = v.scalar_tensor_tensor(trig2[:, :, :], cost[:, :, :], -C2,
                                           trig2[:, :, :], op0=OP.mult, op1=OP.add)
                wt(v, sV.sig(i))
                i = v.tensor_scalar(cost[:, :, :], trig2[:, :, :], PI_LO, None,
                                    op0=OP.is_gt)
                wt(v, sV.sig(i))
                i = v.scalar_tensor_tensor(trig2[:, :, :], cost[:, :, :], -C1,
                                           trig2[:, :, :], op0=OP.mult, op1=OP.add)
                wt(v, sV.sig(i))
                i = v.scalar_tensor_tensor(trig2[:, :, :], cost[:, :, :], -C2,
                                           trig2[:, :, :], op0=OP.mult, op1=OP.add)
                wt(v, sV.sig(i))
                i = v.tensor_scalar(cost[:, :, :], trig2[:, :, :], -PI_LO, None,
                                    op0=OP.is_lt)
                wt(v, sV.sig(i))
                i = v.scalar_tensor_tensor(trig2[:, :, :], cost[:, :, :], C1,
                                           trig2[:, :, :], op0=OP.mult, op1=OP.add)
                wt(v, sV.sig(i))
                i = v.scalar_tensor_tensor(trig2[:, :, :], cost[:, :, :], C2,
                                           trig2[:, :, :], op0=OP.mult, op1=OP.add)
                wt(v, sV.sig(i))
                i = v.tensor_scalar(trig2[:, :, :], trig2[:, :, :], PI_LO, None,
                                    op0=OP.min)
                wt(v, sV.sig(i))
                i = v.tensor_scalar(trig2[:, :, :], trig2[:, :, :], -PI_LO, None,
                                    op0=OP.max)
                return sV.sig(i)
            tred = emit(block.vector, _)

            def _(s):
                wt(s, tred)
                s.activation(sint[:, :, :], trig2[:, :, :], ACT.Sin)
                i = s.activation(cost[:, :, :], trig2[:, :, :], ACT.Abs)
                return sS.sig(i)
            tsin = emit(block.scalar, _)

            def _(s):
                wt(s, tsin)
                return sS.sig(s.activation(cost[:, :, :], cost[:, :, :],
                                           ACT.Sin, bias=pi2t[:, :], scale=-1.0))
            tcos = emit(block.scalar, _)

            def _(v):
                wt(v, tcos)
                i = v.memset(KVf[:, :, :], 0.0)
                wt(v, sV.sig(i))
                i = v.memset(KVb[:, :, :], 0.0)
                wt(v, sV.sig(i))
                i = v.memset(vaug[:, :, 64:65], 1.0)
                wt(v, sV.sig(i))
                return sV.sig(v.memset(kTz[:, :, :], 0.0))
            init_tok = emit(block.vector, _)

            st_l1 = {"pv_evac": None, "kv_cast": init_tok, "wo": None,
                     "att_V": None, "att_S": None, "ptev": None,
                     "kvup": None}

            for c in range(NCH):
                xsrc = [X[:, c, k * 128:(k + 1) * 128] for k in range(8)]
                xdst = [xTl[:, k, :] for k in range(8)]
                xev = emit_transposes(xsrc, pD, xdst,
                                      pre_toks=(st_l1["pv_evac"], init_tok))

                def _(t, c=c):
                    wt(t, l1w_cast[0], st_l1["att_V"], st_l1["att_S"])
                    for k in range(8):
                        wt(t, xev[k])
                        st, sp = (k == 0), (k == 7)
                        for half in range(2):
                            t.matmul(pA[:, half * 512:(half + 1) * 512],
                                     xTl[:, k, :], wqkv[:, k, half * 512:(half + 1) * 512],
                                     start=st, stop=sp)
                            t.matmul(pB[:, half * 512:(half + 1) * 512],
                                     xTl[:, k, :], wqkv[:, k, D + half * 512:D + (half + 1) * 512],
                                     start=st, stop=sp)
                            i = t.matmul(pC[:, half * 512:(half + 1) * 512],
                                         xTl[:, k, :], wqkv[:, k, 2 * D + half * 512:2 * D + (half + 1) * 512],
                                         start=st, stop=sp)
                    return sT.sig(i)
                tqkv = emit(block.tensor, _)

                def _(s, c=c):
                    wt(s, tqkv)
                    return sS.sig(s.copy(vaug[:, :, 0:64],
                                         pC[:, :].rearrange("p (h d) -> p h d", h=H)))
                tvev = emit(block.scalar, _)

                rp_toks = []
                for (srcp, dst) in ((pA, qn), (pB, kn)):
                    def _(v, c=c, srcp=srcp, dst=dst):
                        wt(v, tqkv, rp_toks[-1] if rp_toks else None)
                        s3 = srcp[:, :].rearrange("p (h d) -> p h d", h=H)
                        d3 = dst[:, :].rearrange("p (h d) -> p h d", h=H)
                        cosb = cost[:, c, :].unsqueeze(1).broadcast_to([128, H, 32])
                        sinb = sint[:, c, :].unsqueeze(1).broadcast_to([128, H, 32])
                        c1t = rtmp[:, 0, :].rearrange("p (h x) -> p h x", h=H)
                        s1t = rtmp[:, 1, :].rearrange("p (h x) -> p h x", h=H)
                        t2t = rtmp[:, 2, :].rearrange("p (h x) -> p h x", h=H)
                        v.tensor_tensor(c1t, s3[:, :, 0:32], cosb, op=OP.mult)
                        v.tensor_tensor(s1t, s3[:, :, 0:32], sinb, op=OP.mult)
                        v.tensor_tensor(d3[:, :, 32:64], s3[:, :, 32:64], cosb, op=OP.mult)
                        i = v.tensor_tensor(t2t, s3[:, :, 32:64], sinb, op=OP.mult)
                        wt(v, sV.sig(i))
                        v.tensor_add(d3[:, :, 32:64], d3[:, :, 32:64], s1t)
                        i = v.tensor_sub(d3[:, :, 0:32], c1t, t2t)
                        wt(v, sV.sig(i))
                        i = v.tensor_scalar(dst[:, :], dst[:, :], S1, S2,
                                            op0=OP.mult, op1=OP.mult)
                        wt(v, sV.sig(i))
                        i = v.tensor_scalar(rtmp[:, 0:2, :].rearrange("p a x -> p (a x)"),
                                            dst[:, :], 0.0, None, op0=OP.min)
                        wt(v, sV.sig(i))
                        return sV.sig(v.tensor_scalar(dst[:, :], dst[:, :], 0.0,
                                                      None, op0=OP.max))
                    trelu = emit(block.vector, _)

                    def _(s, trelu=trelu):
                        wt(s, trelu)
                        return sS.sig(s.activation(
                            rtmp[:, 0:2, :].rearrange("p a x -> p (a x)"),
                            rtmp[:, 0:2, :].rearrange("p a x -> p (a x)"), ACT.Exp))
                    texp = emit(block.scalar, _)

                    def _(v, dst=dst, texp=texp):
                        wt(v, texp)
                        i = v.tensor_add(dst[:, :], dst[:, :],
                                         rtmp[:, 0:2, :].rearrange("p a x -> p (a x)"))
                        return sV.sig(i)
                    rp_toks.append(emit(block.vector, _))
                trp = rp_toks[-1]

                def _(v):
                    wt(v, trp, st_l1["kvup"])
                    return sV.sig(v.tensor_copy(
                        ktk[:, :, :], kn[:, :].rearrange("p (h d) -> p h d", h=H)))
                tktk = emit(block.vector, _)

                qsrc = [qn[:, k * 128:(k + 1) * 128] for k in range(8)]
                qdst = [qTp[:, k, :] for k in range(8)]
                qev = emit_transposes(qsrc, pA, qdst, pre_toks=(trp,))
                ksrc = [kn[:, k * 128:(k + 1) * 128] for k in range(8)]
                kdst = [kTz[:, 2 * k, :] for k in range(8)]
                kev = emit_transposes(ksrc, pB, kdst, pre_toks=(trp,),
                                      qtz_split=kTz)

                pv_toks = []
                for g in range(2):
                    def _(t, g=g, c=c):
                        for tk in qev + kev:
                            wt(t, tk)
                        wt(t, st_l1["ptev"], st_l1["kv_cast"], tvev)
                        if g == 1:
                            wt(t, pv_toks[0])
                        i = None
                        for hh in range(8):
                            h = g * 8 + hh
                            i = t.matmul(pA[:, hh * 128:(hh + 1) * 128],
                                         kTz[:, h, :], qTp[:, h // 2, :],
                                         start=True, stop=True)
                        return sT.sig(i)
                    tat = emit(block.tensor, _)

                    def _(v, g=g, tat=tat):
                        wt(v, tat, ut_tok)
                        i = v.tensor_tensor(
                            PTl[:, :, :],
                            pA[:, 0:1024].rearrange("p (a b) -> p a b", a=8),
                            UT[:, :].unsqueeze(1).broadcast_to([128, 8, 128]),
                            op=OP.mult)
                        return sV.sig(i)
                    tmask = emit(block.vector, _)
                    st_l1["ptev"] = tmask

                    def _(t, g=g, c=c, tmask=tmask):
                        wt(t, tmask)
                        ps = pD if g == 0 else pC
                        i = None
                        for hh in range(8):
                            h = g * 8 + hh
                            col = (hh // 4) * 512 + (hh % 4) * 66
                            t.matmul(ps[:, col:col + 65], PTl[:, hh, :],
                                     vaug[:, h, :], start=True, stop=False)
                            i = t.matmul(ps[:, col:col + 65], qTp[:, h // 2, :],
                                         KVb[:, h, :], start=False, stop=True)
                        return sT.sig(i)
                    pv_toks.append(emit(block.tensor, _))

                def _(t, c=c):
                    wt(t, tktk, pv_toks[0], pv_toks[1])
                    for tk in kev:
                        wt(t, tk)
                    i = None
                    for h in range(H):
                        sl = h // 2
                        po = (h % 2) * 64
                        col = (sl // 4) * 512 + (sl % 4) * 66
                        i = t.matmul(pB[po:po + 64, col:col + 65],
                                     ktk[:, h, :], vaug[:, h, :],
                                     start=True, stop=True,
                                     tile_position=(0, po))
                    return sT.sig(i)
                tkv = emit(block.tensor, _)
                st_l1["kvup"] = tkv

                def _(v, tkv=tkv):
                    wt(v, tkv, st_l1["kv_cast"])
                    i = None
                    for half in range(2):
                        for po in range(2):
                            hs = half * 8 + po
                            he = min(hs + 8, H)
                            i = v.tensor_add(
                                KVf[po * 64:po * 64 + 64, hs:he:2, :],
                                KVf[po * 64:po * 64 + 64, hs:he:2, :],
                                pB[po * 64:po * 64 + 64,
                                   half * 512:half * 512 + 264].rearrange(
                                    "p (s e) -> p s e", e=66)[:, :, 0:65])
                    wt(v, sV.sig(i))
                    return sV.sig(v.tensor_copy(KVb[:, :, :], KVf[:, :, :]))
                st_l1["kv_cast"] = emit(block.vector, _)

                def _(v):
                    wt(v, pv_toks[0], pv_toks[1])
                    i = None
                    for g4 in range(4):
                        ps = pD if g4 < 2 else pC
                        col = (g4 % 2) * 512
                        i = v.tensor_copy(nrm[:, 6, g4 * 4:(g4 + 1) * 4],
                                          ps[:, col:col + 264].rearrange(
                                              "p (h e) -> p h e", h=4)[:, :, 64])
                    wt(v, sV.sig(i))
                    i = v.tensor_scalar(nrm[:, 6, :], nrm[:, 6, :], 1e-6, None,
                                        op0=OP.max)
                    wt(v, sV.sig(i))
                    i = v.reciprocal(nrm[:, 7, :], nrm[:, 6, :])
                    wt(v, sV.sig(i))
                    i = None
                    for g4 in range(4):
                        ps = pD if g4 < 2 else pC
                        col = (g4 % 2) * 512
                        i = v.tensor_tensor(
                            qn[:, :].rearrange("p (h d) -> p h d", h=H)[:, g4 * 4:(g4 + 1) * 4, :],
                            ps[:, col:col + 264].rearrange("p (h e) -> p h e", h=4)[:, :, 0:64],
                            nrm[:, 7, g4 * 4:(g4 + 1) * 4].unsqueeze(2).broadcast_to([128, 4, HD]),
                            op=OP.mult)
                    return sV.sig(i)
                tpvev = emit(block.vector, _)
                st_l1["pv_evac"] = tpvev

                aosrc = [qn[:, k * 128:(k + 1) * 128] for k in range(8)]
                aodst = [xTl[:, k, :] for k in range(8)]
                aoev = emit_transposes(aosrc, pB, aodst,
                                       pre_toks=(tpvev, st_l1["kv_cast"]),
                                       evac="scalar")

                def _(t, c=c):
                    wt(t, st_l1["wo"])
                    for k in range(8):
                        wt(t, aoev[k])
                        st, sp = (k == 0), (k == 7)
                        t.matmul(pA[:, 0:512], xTl[:, k, :], wout[:, k, 0:512],
                                 start=st, stop=sp)
                        i = t.matmul(pA[:, 512:1024], xTl[:, k, :],
                                     wout[:, k, 512:1024], start=st, stop=sp)
                    return sT.sig(i)
                twoT = emit(block.tensor, _)

                def _(v, c=c):
                    wt(v, twoT)
                    return sV.sig(v.tensor_add(X[:, c, :], X[:, c, :], pA[:, :]))
                tres = emit(block.vector, _)
                st_l1["wo"] = tres

                tln, _ = emit_ln_chunk(c, (tres,), pC, skip_t=True)

                tsv2_pred = (sV, sV.n + 2)

                def _(s, tln=tln):
                    wt(s, tln)
                    i = s.activation(kn[:, :], X[:, c, :], ACT.Square,
                                     accum_out=lnm[:, 0:1])
                    wt(s, sS.sig(i))
                    wt(s, tsv2_pred)
                    i = s.activation(lnm[:, 1:2], lnm[:, 0:1], ACT.Ln)
                    wt(s, sS.sig(i))
                    return sS.sig(s.activation(lnm[:, 2:3], lnm[:, 1:2], ACT.Exp,
                                               scale=-0.5, bias=cst_sm[:, 1:2]))
                tn2 = emit(block.scalar, _)
                tss2 = (sS, tn2[1] - 2)

                def _(v, c=c, tss2=tss2):
                    wt(v, tss2, ut_tok)
                    i = v.tensor_mul(kn[:, :], X[:, c, :], rrep[:, :])
                    wt(v, sV.sig(i))
                    i = sV.sig(v.tensor_reduce(svals[:, c:c + 1], kn[:, :],
                                               axis=AX.X, op=OP.add))
                    assert i == tsv2_pred, (i, tsv2_pred)
                    return i
                tsv2 = emit(block.vector, _)

                def _(v, c=c, tn2=tn2):
                    wt(v, tn2)
                    i = v.tensor_scalar(lnm[:, 2:3], lnm[:, 2:3], 3.2e13, None,
                                        op0=OP.min)
                    wt(v, sV.sig(i))
                    return sV.sig(v.tensor_scalar(qn[:, :], X[:, c, :],
                                                  lnm[:, 2:3], None, op0=OP.mult))
                txn2 = emit(block.vector, _)

                xsrc2 = [qn[:, k * 128:(k + 1) * 128] for k in range(8)]
                xdst2 = [XNT[:, k, c * 128:(c + 1) * 128] for k in range(8)]
                x3ev = emit_transposes(xsrc2, pC, xdst2, pre_toks=(txn2,))
                rms_sv_tok[c] = x3ev[-1]
                st_l1["att_V"] = x3ev[-1]
                st_l1["att_S"] = tn2
                if c == 0:
                    trctx.close()
            l1ctx.close()

        bar_l1a = [(sT, sT.n), (sV, sV.n), (sS, sS.n)]

        if "l1f" in phases:
            N_ITERS = 50
            EFF_K = min(1024.0 * 9.0 / 8.0, float(T))
            LOGK = float(np.float32(np.log(np.float64(EFF_K))))
            A1 = float(np.float32(np.log(np.float64(EFF_K)) - np.log(np.float64(T))))
            NHEAVY = min(1024, T)

            fctx = contextlib.ExitStack()
            EF = fctx.enter_context
            ONES = EF(nc.sbuf_tensor("ONES", [128, 128], F32))
            UT2 = EF(nc.sbuf_tensor("UT2", [128, 128], F32))
            lgam = EF(nc.sbuf_tensor("lgam", [128, 8], F32))
            hgam = EF(nc.sbuf_tensor("hgam", [128, 8], F32))
            lw1 = EF(nc.sbuf_tensor("lw1", [128, 8, 512], BF16))
            lw2 = EF(nc.sbuf_tensor("lw2", [128, 4, D], BF16))
            lb1 = EF(nc.sbuf_tensor("lb1", [1, 512], BF16))
            lb2 = EF(nc.sbuf_tensor("lb2", [1, D], BF16))
            hb2 = EF(nc.sbuf_tensor("hb2", [1, D], BF16))
            hw1q = [EF(nc.sbuf_tensor(f"hw1q{i}", [128, 8, 1024], BF16)) for i in range(2)]
            hw2q = [EF(nc.sbuf_tensor(f"hw2q{i}", [128, 8, 1024], BF16)) for i in range(2)]
            hb1q = EF(nc.sbuf_tensor("hb1q", [1, 1024], BF16))
            onesr = EF(nc.sbuf_tensor("onesr", [1, 512], BF16))
            hh_bf = EF(nc.sbuf_tensor("hh_bf", [128, 8, 512], BF16))
            hl_bf = hh_bf
            fmeta = EF(nc.sbuf_tensor("fmeta", [128, 8], F32))
            sat = EF(nc.sbuf_tensor("sat", [128, NCH], F32))
            hmask = EF(nc.sbuf_tensor("hmask", [128, NCH], F32))
            pcol = EF(nc.sbuf_tensor("pcol", [128, NCH], F32))

            def _(sy):
                wt(sy, *bar_l1a)
                dC.sig(sy.dma_start(out=ONES[:, :], in_=ones_hbm))
                dC.sig(sy.dma_start(out=UT2[:, :], in_=ut_hbm))
                dC.sig(sy.dma_start(out=lgam[:, :], in_=lgam_hbm))
                return dC.sig(sy.dma_start(out=hgam[:, :], in_=hgam_hbm))
            cst2 = emit(block.sync, _)

            def _(g):
                wt(g, *bar_l1a)
                for k in range(8):
                    dG.sig(g.dma_start(out=lw1[:, k, :], in_=lw1_hbm[k]))
                    dG.throttle(g)
                for k in range(4):
                    dG.sig(g.dma_start(out=lw2[:, k, :], in_=lw2_hbm[k]))
                    dG.throttle(g)
                return dG.tok()
            lwd = emit(block.gpsimd, _)

            def _(v):
                wt(v, lwd, cst2, *bar_l1a)
                i = None
                for k in range(8):
                    i = v.tensor_scalar(lw1[:, k, :], lw1[:, k, :],
                                        lgam[:, k:k + 1], None, op0=OP.mult)
                return sV.sig(i)
            ltoks = [emit(block.vector, _)]

            def _(g):
                wt(g, *bar_l1a)
                dG.sig(g.dma_start(out=lb1[0:1, :], in_=lb1_hbm))
                dG.throttle(g)
                dG.sig(g.dma_start(out=lb2[0:1, :], in_=lb2_hbm))
                dG.throttle(g)
                dG.sig(g.dma_start(out=hb2[0:1, :], in_=hb2_hbm))
                dG.throttle(g)
                return dG.tok()
            btok = emit(block.gpsimd, _)

            def _(v):
                wt(v, btok)
                return sV.sig(v.memset(onesr[0:1, :], 1.0))
            bcast_tok = emit(block.vector, _)

            rms_toks = [rms_sv_tok[NCH - 1]]
            sv_tok = rms_sv_tok[NCH - 1]

            def _(v):
                wt(v, sv_tok)
                return sV.sig(v.memset(fmeta[:, 0:1], A1))
            atok = emit(block.vector, _)

            V0 = sV.n

            def _(s):
                out = []
                for it in range(N_ITERS - 1):
                    wt(s, (sV, V0 + 4 * it + 1))
                    e = sS.sig(s.activation(kn[:, 0:NCH], kn[:, 0:NCH], ACT.Exp))
                    wt(s, (sV, V0 + 4 * it + 2))
                    l = sS.sig(s.activation(fmeta[:, 2:3], fmeta[:, 1:2], ACT.Ln))
                    out.append((e, l))
                return out
            desc_sl = emit(block.scalar, _)

            for it in range(N_ITERS - 1):
                t2, t5 = desc_sl[it]

                def _(v, atok=atok):
                    wt(v, atok)
                    return sV.sig(v.tensor_scalar(kn[:, 0:NCH], svals[:, 0:NCH],
                                                  fmeta[:, 0:1], 0.0,
                                                  op0=OP.add, op1=OP.min))
                t1 = emit(block.vector, _)
                assert t1 == (sV, V0 + 4 * it + 1), (t1, it)

                def _(t, t2=t2, it=it):
                    wt(t, t2, cst2)
                    if it == 0:
                        wt(t, rms_toks[-1])
                    return sT.sig(t.matmul(pD[:, 0:NCH], ONES[:, :], kn[:, 0:NCH],
                                           start=True, stop=True))
                t3 = emit(block.tensor, _)

                def _(v, t3=t3):
                    wt(v, t3)
                    return sV.sig(v.tensor_reduce(fmeta[:, 1:2], pD[:, 0:NCH],
                                                  axis=AX.X, op=OP.add))
                t4 = emit(block.vector, _)
                assert t4 == (sV, V0 + 4 * it + 2), (t4, it)

                def _(v, t5=t5):
                    wt(v, t5)
                    i = v.scalar_tensor_tensor(fmeta[:, 3:4], fmeta[:, 2:3], -1.0,
                                               fmeta[:, 0:1], op0=OP.mult, op1=OP.add)
                    wt(v, sV.sig(i))
                    return sV.sig(v.tensor_scalar(fmeta[:, 0:1], fmeta[:, 3:4],
                                                  LOGK, None, op0=OP.add))
                atok = emit(block.vector, _)

            def _(v):
                wt(v, atok)
                i = v.tensor_scalar(fmeta[:, 4:5], fmeta[:, 0:1], -1.0, None,
                                    op0=OP.mult)
                wt(v, sV.sig(i))
                return sV.sig(v.tensor_scalar(sat[:, :], svals[:, 0:NCH],
                                              fmeta[:, 4:5], None, op0=OP.is_ge))
            sat_tok = emit(block.vector, _)

            def _(t):
                wt(t, sat_tok)
                t.matmul(pD[:, 0:NCH], UT2[:, :], sat[:, :], start=True, stop=True)
                return sT.sig(t.matmul(pD[:, 512:512 + NCH], ONES[:, :], sat[:, :],
                                       start=True, stop=True))
            pre_tok = emit(block.tensor, _)

            def _(v):
                wt(v, pre_tok)
                i = v.tensor_copy(pcol[:, :], pD[:, 0:NCH])
                wt(v, sV.sig(i))
                i = v.tensor_copy(fmeta[:, 5:6], pD[:, 512:513])
                wt(v, sV.sig(i))
                for c in range(1, NCH):
                    i = v.tensor_add(pcol[:, c:c + 1], pcol[:, c:c + 1],
                                     fmeta[:, 5:6])
                    wt(v, sV.sig(i))
                    if c < NCH - 1:
                        i = v.tensor_tensor(fmeta[:, 5:6], fmeta[:, 5:6],
                                            pD[:, 512 + c:512 + c + 1], op=OP.add)
                        wt(v, sV.sig(i))
                return sV.tok()
            px_tok = emit(block.vector, _)

            def _(v):
                wt(v, px_tok)
                if NHEAVY >= T:
                    return sV.sig(v.memset(hmask[:, :], 1.0))
                i = v.tensor_scalar(kn[:, 0:NCH], pcol[:, :], float(NHEAVY), None,
                                    op0=OP.is_le)
                wt(v, sV.sig(i))
                return sV.sig(v.tensor_mul(hmask[:, :], kn[:, 0:NCH], sat[:, :]))
            hm_tok = emit(block.vector, _)

            acc_tok2 = [None] * NCH
            dn_free = [None, None]
            up_free = None
            TG = 512 if T >= 512 else T
            SUBS = TG // 128
            for g in range(T // TG):
                def _(t, g=g):
                    wt(t, ltoks[-1], bcast_tok, rms_toks[-1], up_free)
                    if g == 0:
                        wt(t, hm_tok)
                    i = None
                    for ii in range(4):
                        ps = (pA, pB)[ii // 2]
                        col = (ii % 2) * 512
                        for k in range(8):
                            i = t.matmul(ps[:, col:col + TG],
                                         lw1[:, k, ii * 128:(ii + 1) * 128],
                                         XNT[:, k, g * TG:(g + 1) * TG],
                                         start=(k == 0), stop=False)
                        i = t.matmul(ps[:, col:col + TG],
                                     lb1[0:1, ii * 128:(ii + 1) * 128],
                                     onesr[0:1, 0:TG], start=False, stop=True)
                    return sT.sig(i)
                tup = emit(block.tensor, _)

                def _(s, g=g, tup=tup):
                    wt(s, tup)
                    i = None
                    for ii in range(4):
                        ps = (pA, pB)[ii // 2]
                        col = (ii % 2) * 512
                        i = s.activation(hl_bf[:, ii, 0:TG], ps[:, col:col + TG],
                                         ACT.Gelu)
                    return sS.sig(i)
                tgl = emit(block.scalar, _)
                up_free = tgl

                for sub in range(SUBS):
                    ch = g * SUBS + sub
                    ps = pC if sub % 2 == 0 else pD
                    def _(t, sub=sub, ps=ps, tgl=tgl):
                        wt(t, tgl, dn_free[sub % 2])
                        i = None
                        for ii in range(4):
                            t.matmul(ps[:, 0:512],
                                     hl_bf[:, ii, sub * 128:(sub + 1) * 128],
                                     lw2[:, ii, 0:512], start=(ii == 0), stop=False)
                            i = t.matmul(ps[:, 512:1024],
                                         hl_bf[:, ii, sub * 128:(sub + 1) * 128],
                                         lw2[:, ii, 512:1024], start=(ii == 0),
                                         stop=False)
                        t.matmul(ps[:, 0:512], onesr[0:1, sub * 128:sub * 128 + 128],
                                 lb2[0:1, 0:512], start=False, stop=True)
                        i = t.matmul(ps[:, 512:1024],
                                     onesr[0:1, sub * 128:sub * 128 + 128],
                                     lb2[0:1, 512:1024], start=False, stop=True)
                        return sT.sig(i)
                    tdn = emit(block.tensor, _)

                    def _(v, ch=ch, tdn=tdn, ps=ps):
                        wt(v, tdn, acc_tok2[ch])
                        return sV.sig(v.tensor_add(X[:, ch, :], X[:, ch, :],
                                                   ps[:, :]))
                    acc_tok2[ch] = emit(block.vector, _)
                    dn_free[sub % 2] = acc_tok2[ch]

            hacc = list(acc_tok2)
            hq_prev_mm = None
            hbuf_free = [None, None]
            hup_free = [up_free] * 4
            hdn_free = [None, None]
            for q in range(4):
                qb = q % 2
                def _(g, q=q, qb=qb):
                    wt(g, hbuf_free[qb], cst2)
                    for k in range(8):
                        dG.sig(g.dma_start(
                            out=hw1q[qb][:, k, :],
                            in_=hw1_hbm[k * 128:(k + 1) * 128,
                                        q * 1024:(q + 1) * 1024]))
                        dG.throttle(g)
                    for k in range(8):
                        dG.sig(g.dma_start(
                            out=hw2q[qb][:, k, :],
                            in_=hw2_hbm[q * 1024 + k * 128:q * 1024 + (k + 1) * 128, :]))
                        dG.throttle(g)
                    return dG.tok()
                hwd = emit(block.gpsimd, _)

                def _(v, q=q, qb=qb):
                    wt(v, hwd, hbuf_free[qb], cst2)
                    i = None
                    for k in range(8):
                        i = v.tensor_scalar(hw1q[qb][:, k, :], hw1q[qb][:, k, :],
                                            hgam[:, k:k + 1], None, op0=OP.mult)
                    return sV.sig(i)
                htoks = [emit(block.vector, _)]

                def _(g, q=q):
                    wt(g, htoks[-1], hq_prev_mm)
                    i = dG.sig(g.dma_start(
                        out=hb1q[0:1, :],
                        in_=hb1_hbm[0:1, q * 1024:(q + 1) * 1024]))
                    dG.throttle(g)
                    return i
                hbtok = emit(block.gpsimd, _)

                last_dn = None
                for g in range(T // TG):
                    PSL = (pA, pA, pB, pB, pC, pC, pD, pD)

                    def _(t, q=q, g=g, hbtok=hbtok):
                        wt(t, htoks[-1], hbtok, *hup_free)
                        wt(t, hdn_free[0], hdn_free[1])
                        i = None
                        for ii in range(8):
                            ps = PSL[ii]
                            col = (ii % 2) * 512
                            for k in range(8):
                                t.matmul(ps[:, col:col + TG],
                                         hw1q[qb][:, k, ii * 128:(ii + 1) * 128],
                                         XNT[:, k, g * TG:(g + 1) * TG],
                                         start=(k == 0), stop=False)
                            i = t.matmul(ps[:, col:col + TG],
                                         hb1q[0:1, ii * 128:(ii + 1) * 128],
                                         onesr[0:1, 0:TG], start=False, stop=True)
                        return sT.sig(i)
                    tuph = emit(block.tensor, _)

                    def _(s, tuph=tuph):
                        wt(s, tuph)
                        i = None
                        for ii in range(8):
                            ps = PSL[ii]
                            col = (ii % 2) * 512
                            i = s.activation(hh_bf[:, ii, 0:TG], ps[:, col:col + TG],
                                             ACT.Gelu)
                        return sS.sig(i)
                    tglh = emit(block.scalar, _)
                    hup_free = [tglh] * 4

                    for sub in range(SUBS):
                        ch = g * SUBS + sub
                        ps = pC if sub % 2 == 0 else pD

                        def _(t, q=q, sub=sub, ps=ps, tglh=tglh):
                            wt(t, tglh, hdn_free[sub % 2])
                            i = None
                            for ii in range(8):
                                t.matmul(ps[:, 0:512],
                                         hh_bf[:, ii, sub * 128:(sub + 1) * 128],
                                         hw2q[qb][:, ii, 0:512], start=(ii == 0),
                                         stop=(ii == 7 and q != 0))
                                i = t.matmul(ps[:, 512:1024],
                                             hh_bf[:, ii, sub * 128:(sub + 1) * 128],
                                             hw2q[qb][:, ii, 512:1024], start=(ii == 0),
                                             stop=(ii == 7 and q != 0))
                            if q == 0:
                                t.matmul(ps[:, 0:512],
                                         onesr[0:1, sub * 128:sub * 128 + 128],
                                         hb2[0:1, 0:512], start=False, stop=True)
                                i = t.matmul(ps[:, 512:1024],
                                             onesr[0:1, sub * 128:sub * 128 + 128],
                                             hb2[0:1, 512:1024], start=False, stop=True)
                            return sT.sig(i)
                        tdnh = emit(block.tensor, _)
                        last_dn = tdnh

                        def _(v, ch=ch, tdnh=tdnh, ps=ps):
                            wt(v, tdnh, hacc[ch], hm_tok)
                            return sV.sig(v.scalar_tensor_tensor(
                                X[:, ch, :], ps[:, :], hmask[:, ch:ch + 1],
                                X[:, ch, :], op0=OP.mult, op1=OP.add))
                        hacc[ch] = emit(block.vector, _)
                        hdn_free[sub % 2] = hacc[ch]
                hq_prev_mm = last_dn
                hbuf_free[qb] = last_dn

            ln4_tok = [None] * NCH
            for c in range(NCH):
                pre = [hacc[c]]
                if c == 0:
                    pre.append(hq_prev_mm)
                ln4_tok[c], _ = emit_ln_chunk(c, pre, pA if c % 2 else pB,
                                              skip_t=True)
            fctx.close()



        def _(g):
            for c in range(NCH):
                if "l1f" in phases:
                    wt(g, ln4_tok[c])
                else:
                    wt(g, (sV, sV.n), (sS, sS.n), (sT, sT.n))
                dX.sig(g.dma_start(out=y_hbm[c * 128:(c + 1) * 128, :],
                                   in_=X[:, c, :]))
            if DBG:
                wt(g, (sV, sV.n), (sS, sS.n), (sT, sT.n))
                dX.sig(g.dma_start(out=dbg_hbm["qn"], in_=qn[:, :]))
                dX.sig(g.dma_start(out=dbg_hbm["PT"],
                                   in_=PT[:, :, :, :].rearrange("p a b c -> p (a b c)")))
                dX.sig(g.dma_start(out=dbg_hbm["kT"],
                                   in_=kT[(NCH - 1) % 2][:, :, :].rearrange("p a b -> p (a b)")))
                dX.sig(g.dma_start(out=dbg_hbm["qT"],
                                   in_=qT[(NCH - 1) % 2][:, :, :].rearrange("p a b -> p (a b)")))
                dX.sig(g.dma_start(out=dbg_hbm["nrm"],
                                   in_=nrm[:, :, :].rearrange("p a b -> p (a b)")))
                dX.sig(g.dma_start(out=dbg_hbm["vbf"],
                                   in_=vbf[(NCH - 1) % 2][:, :, :].rearrange("p a b -> p (a b)")))
            g.wait_ge(dX.h, dX.n)
            return None
        emit(block.gpsimd, _)

    return nc


def kernel(**inputs):
    base = host_consts()
    x_full = np.ascontiguousarray(inputs["L0_attn_x"], dtype=np.float32)
    for w, nm in zip("qkvo", ["Wq", "Wk", "Wv", "Wo"]):
        base[f"l0_w{w}T"] = np.ascontiguousarray(
            inputs[f"L0_attn_{nm}"].T.reshape(8, 128, D).astype(np.float32))
    base["l0_gateT"] = np.ascontiguousarray(inputs["L0_ffn_gate_w"].T.astype(np.float32))
    base["l0_moe_w1"] = np.ascontiguousarray(inputs["L0_ffn_w1"].astype(np.float32))
    base["l0_moe_w3"] = np.ascontiguousarray(inputs["L0_ffn_w3"].astype(np.float32))
    base["l0_moe_w2"] = np.ascontiguousarray(inputs["L0_ffn_w2"].astype(np.float32))
    base["l1_wqkv"] = np.ascontiguousarray(
        inputs["L1_attn_w_qkv"].reshape(8, 128, 3 * D).astype(np.float32))
    base["l1_wout"] = np.ascontiguousarray(
        inputs["L1_attn_w_out"].reshape(8, 128, D).astype(np.float32))
    base["l1_rrep"] = np.ascontiguousarray(
        np.tile(np.asarray(inputs["L1_ffn_routing_token"], np.float32)[None, :], (128, 1)))
    base["l1_lgam"] = np.ascontiguousarray(
        np.asarray(inputs["L1_ffn_light_gamma"], np.float32).reshape(8, 128).T)
    base["l1_hgam"] = np.ascontiguousarray(
        np.asarray(inputs["L1_ffn_heavy_gamma"], np.float32).reshape(8, 128).T)
    base["l1_lw1"] = np.ascontiguousarray(
        np.asarray(inputs["L1_ffn_light_w1"], np.float32).reshape(8, 128, 512))
    base["l1_lw2"] = np.ascontiguousarray(
        np.asarray(inputs["L1_ffn_light_w2"], np.float32).reshape(4, 128, D))
    base["l1_lb1"] = np.ascontiguousarray(np.asarray(inputs["L1_ffn_light_b1"], np.float32)[None, :])
    base["l1_lb2"] = np.ascontiguousarray(np.asarray(inputs["L1_ffn_light_b2"], np.float32)[None, :])
    base["l1_hb1"] = np.ascontiguousarray(np.asarray(inputs["L1_ffn_heavy_b1"], np.float32)[None, :])
    base["l1_hb2"] = np.ascontiguousarray(np.asarray(inputs["L1_ffn_heavy_b2"], np.float32)[None, :])
    base["l1_hw1"] = np.ascontiguousarray(np.asarray(inputs["L1_ffn_heavy_w1"], np.float32))
    base["l1_hw2"] = np.ascontiguousarray(np.asarray(inputs["L1_ffn_heavy_w2"], np.float32))
    nc = build_program()
    in_maps = []
    for core in range(8):
        m = dict(base)
        m["x"] = np.ascontiguousarray(x_full[core])
        m["xTin"] = np.ascontiguousarray(x_full[core].T)
        in_maps.append(m)
    res = run_bass_kernel_spmd(nc, in_maps, core_ids=list(range(8)))
    return np.stack([res.results[i]["y"] for i in range(8)], axis=0)
